# version 04
# baseline (speedup 1.0000x reference)
import numpy as np
import concourse.bass as bass
import concourse.mybir as mybir
from concourse.bass_utils import run_bass_kernel_spmd

F32 = mybir.dt.float32
A = mybir.AluOpType
ACT = mybir.ActivationFunctionType
AX = mybir.AxisListType


F32 = mybir.dt.float32
A = mybir.AluOpType
ACT = mybir.ActivationFunctionType
AX = mybir.AxisListType


NLANES = 8

class Op:
    __slots__ = ("eng", "fn", "reads", "writes", "seq", "need_inc", "waits", "soft", "tag", "lane")
    def __init__(self, eng, fn, reads, writes, tag=""):
        self.eng = eng
        self.fn = fn
        self.reads = reads
        self.writes = writes
        self.seq = None
        self.need_inc = False
        self.waits = []
        self.soft = []
        self.tag = tag
        self.lane = 0

_INC = {"T": 1, "V": 1, "S": 1, "G": 1, "Q0": 16, "Q1": 16, "QG": 16}
_QENG = {"Q0": "sync", "Q1": "scalar", "QG": "gpsimd", "T": "tensor", "V": "vector",
         "S": "scalar", "G": "gpsimd"}

class Sched:
    def __init__(self):
        self.ops = []
        self.last_writer = {}
        self.readers = {}

    def add(self, eng, fn, reads=(), writes=(), tag=""):
        op = Op(eng, fn, list(reads), list(writes), tag)
        deps = []
        for r in op.reads:
            w = self.last_writer.get(r)
            if w is not None:
                deps.append(w)
        for r in op.writes:
            w = self.last_writer.get(r)
            if w is not None:
                deps.append(w)
            deps.extend(self.readers.get(r, ()))
        seen = set()
        for d in deps:
            if d is op or id(d) in seen:
                continue
            seen.add(id(d))
            if d.eng == op.eng and d.eng == "T":
                continue
            d.need_inc = True
            op.waits.append(d)
        for r in op.reads:
            self.readers.setdefault(r, []).append(op)
        for r in op.writes:
            self.last_writer[r] = op
            self.readers[r] = []
        self.ops.append(op)
        return op

    def finalize(self):
        counters = {k: 0 for k in _INC}
        for op in self.ops:
            if op.eng in ("Q0", "Q1", "QG"):
                op.need_inc = True
                n = counters[op.eng]
                op.lane = n % NLANES
                op.seq = n // NLANES + 1
                counters[op.eng] = n + 1
            elif op.need_inc:
                counters[op.eng] += 1
                op.seq = counters[op.eng]
        self.total = dict(counters)

    def emit(self, nc, sems):
        per_eng = {}
        for op in self.ops:
            per_eng.setdefault(_QENG[op.eng], []).append(op)
        def semkey(op):
            return (op.eng, op.lane) if op.eng in ("Q0", "Q1", "QG") else op.eng
        with nc.Block() as block:
            def run(engine_obj, key):
                done = {}
                def wait(sk, tgt):
                    if done.get(sk, 0) < tgt:
                        engine_obj.wait_ge(sems[sk], tgt)
                        done[sk] = tgt
                for op in per_eng.get(key, ()):
                    for d in op.waits:
                        wait(semkey(d), d.seq * _INC[d.eng])
                    for d in op.soft:
                        if d.need_inc:
                            wait(semkey(d), d.seq * _INC[d.eng])
                    if op.eng in ("Q0", "Q1", "QG") and op.seq > 1:
                        wait(semkey(op), (op.seq - 1) * 16)
                    ins = op.fn(engine_obj)
                    if op.need_inc:
                        assert ins is not None, f"op {op.tag} needs inc but fn returned None"
                        ins.then_inc(sems[semkey(op)], _INC[op.eng])
            @block.sync
            def _(e):
                run(e, "sync")
            @block.tensor
            def _(e):
                run(e, "tensor")
            @block.vector
            def _(e):
                run(e, "vector")
            @block.scalar
            def _(e):
                run(e, "scalar")
            @block.gpsimd
            def _(e):
                run(e, "gpsimd")

D = 1024
HN = 8
NH = 32
M = 256
E = 8
ISH = 2048
NDT = 8
CT = 128
TS = 16
TB = 8
RW = 4
SLAB = 256
KG = 4

def perm256():
    p = np.zeros(256, dtype=np.int64)
    for h in range(HN):
        for i in range(NH):
            p[(h // 4) * 128 + i * 4 + (h % 4)] = h * NH + i
    return p

def host_prep(inputs, Tcore):
    P = perm256()
    f = np.ascontiguousarray
    d = {}
    d["WinT"] = f(inputs["L0_attn_W_in"].T)
    d["WkT"] = f(inputs["L0_attn_W_k"].T)
    d["WqT"] = f(inputs["L0_attn_W_q"].T)
    d["WvT"] = f(inputs["L0_attn_W_v"].T[:, P])
    d["WbT"] = f(inputs["L0_attn_W_beta"].T[:, P])
    d["bbeta"] = f(inputs["L0_attn_b_beta"].reshape(1, 256)[:, P])
    d["WoT"] = f(inputs["L0_attn_W_out"].T[P, :])
    d["wrout"] = f(inputs["L0_ffn_w_router"])
    d["fc1"] = f(inputs["L0_ffn_fc1_w"])
    d["fc2"] = f(inputs["L0_ffn_fc2_w"])
    d["gate"] = f(inputs["L0_ffn_gate_w"])
    d["up"] = f(inputs["L0_ffn_up_w"])
    d["down"] = f(inputs["L0_ffn_down_w"])
    d["eye"] = np.eye(128, dtype=np.float32)
    sel = np.zeros((2, 8, 128), dtype=np.float32)
    for hf in range(2):
        for pp in range(128):
            sel[hf, 4 * hf + (pp % 4), pp] = 1.0
    d["sel"] = f(sel.reshape(16, 128))
    selk = np.zeros((2, 8, 128), dtype=np.float32)
    for ft in range(2):
        for pp in range(128):
            selk[ft, ft * 4 + pp // 32, pp] = 1.0
    d["selk"] = f(selk.reshape(16, 128))
    bo = np.zeros((2, 128, 8), dtype=np.float32)
    for ft in range(2):
        for pp in range(128):
            bo[ft, pp, ft * 4 + pp // 32] = 1.0
    d["bones"] = f(bo.reshape(256, 8))
    d["ones128"] = np.ones((128, 1), dtype=np.float32)
    d["epsc"] = np.full((128, 1), 1e-5, dtype=np.float32)
    es = np.zeros((8, 8, 128), dtype=np.float32)
    for e in range(8):
        es[e, e, :] = 1.0
    d["esel"] = np.ascontiguousarray(es.transpose(1, 0, 2).reshape(8, 8 * 128))
    d["onesrow"] = np.ones((1, 128), dtype=np.float32)
    d["zeros"] = np.zeros((128, 32), dtype=np.float32)
    return d


class MK:
    def __init__(self, T, stop_after=None):
        self.T = T
        self.NC = T // CT
        self.stop_after = stop_after
        nc = bass.Bass("TRN2", target_bir_lowering=False)
        self.nc = nc
        self.sch = Sched()
        self.dbg = {}
        T_ = T

        def dram(name, shape):
            return nc.dram_tensor(name, list(shape), F32, kind="ExternalInput").ap()
        self.i_x = dram("x", (T_, D))
        self.i_WinT = dram("WinT", (D, D))
        self.i_WkT = dram("WkT", (D, M)); self.i_WqT = dram("WqT", (D, M))
        self.i_WvT = dram("WvT", (D, M)); self.i_WbT = dram("WbT", (D, M))
        self.i_bbeta = dram("bbeta", (1, M))
        self.i_WoT = dram("WoT", (M, D))
        self.i_wrout = dram("wrout", (D, E))
        self.i_fc1 = dram("fc1", (E, D, ISH))
        self.i_fc2 = dram("fc2", (E, D, D))
        self.i_gate = dram("gate", (D, ISH)); self.i_up = dram("up", (D, ISH))
        self.i_down = dram("down", (ISH, D))
        self.i_eye = dram("eye", (128, 128))
        self.i_sel = dram("sel", (16, 128)); self.i_selk = dram("selk", (16, 128))
        self.i_bones = dram("bones", (256, 8))
        self.i_ones128 = dram("ones128", (128, 1))
        self.i_onesrow = dram("onesrow", (1, 128))
        self.i_zeros = dram("zeros", (128, 32))
        self.i_esel = dram("esel", (8, 8 * 128))
        self.i_epsc = dram("epsc", (128, 1))
        self.o_out = nc.dram_tensor("out", [T_, D], F32, kind="ExternalOutput").ap()

        a = nc.alloc_sbuf_tensor
        self.s_fm = a("s_fm", [128, NDT, T_], F32)
        self.WinT = a("WinT_s", [128, NDT, D], F32)
        self.WkT = a("WkT_s", [128, NDT, M], F32)
        self.WqT = a("WqT_s", [128, NDT, M], F32)
        self.WvT = a("WvT_s", [128, NDT, M], F32)
        self.WbT = a("WbT_s", [128, NDT, M], F32)
        self.WoT = a("WoT_s", [128, 2, D], F32)
        self.bbeta = a("bbeta_s", [1, M], F32)
        self.wrout = a("wrout_s", [128, NDT, E], F32)
        self.eye = a("eye_s", [128, 128], F32)
        self.sel = [a("sel0_s", [8, 128], F32), a("sel1_s", [8, 128], F32)]
        self.selk = [a("selk0_s", [8, 128], F32), a("selk1_s", [8, 128], F32)]
        self.bones = [a("bones0_s", [128, 8], F32), a("bones1_s", [128, 8], F32)]
        self.ones128 = a("ones128_s", [128, 1], F32)
        self.onesrow = a("onesrow_s", [1, 128], F32)
        self.zeroS = a("zeroS_s", [128, 32], F32)
        self.esel = a("esel_s", [8, 8 * 128], F32)
        self.epsc = a("epsc_s", [128, 1], F32)
        self.xin = a("xin", [128, 1, D], F32)
        NW = 3
        self.NW = NW
        self.xw = a("xw", [128, NW, NDT, CT], F32)
        self.xp = a("xp", [128, 1, NDT, CT], F32)
        self.k_sb = a("k_sb", [128, 2, CT], F32)
        self.ksq = a("ksq", [128, 2, CT], F32)
        self.q_sb = a("q_sb", [128, 2, 2, CT], F32)
        self.kn_fm = a("kn_fm", [128, 2, 2, CT], F32)
        self.v_fm = a("v_fm", [128, 2, 2, CT], F32)
        self.b_fm = a("b_fm", [128, 2, 2, CT], F32)
        self.rb_fm = a("rb_fm", [128, 2, 2, CT], F32)
        self.nrm = a("nrm", [8, 3, CT], F32)
        self.knq8 = a("knq8", [8, 2, 2, 32, TS], F32)
        self.bcw = a("bcw", [128, 2, 2, 2, 32, TB], F32)
        self.hist = a("hist", [128, 2, RW, 32, TB], F32)
        self.m_scr = a("m_scr", [128, 2, 32], F32)
        self.inner = a("inner", [128, 2, 32], F32)
        self.dcol = a("dcol", [128, 2, 1], F32)
        self.gcol = a("gcol", [128, 2, 1], F32)
        self.Sq = a("Sq", [128, 1, 2, CT], F32)
        self.gq_scr = a("gq_scr", [128, 1, 32, TB], F32)
        self.cell = a("cell", [128, 2, 2, CT], F32)
        p = nc.alloc_psum_tensor
        self.psT0 = p("psT0", [128, 512], F32)
        self.psT1 = p("psT1", [128, 512], F32)
        self.psA = p("psA", [128, 512], F32)
        self.psB = p("psB", [128, 512], F32)
        self.psW = [[p(f"psW{hf}{j}", [128, 512], F32) for j in range(2)] for hf in range(2)]

        self.sems = {}

    def op(self, eng, fn, reads=(), writes=(), tag=""):
        return self.sch.add(eng, fn, reads, writes, tag)

    def load_weights(self):
        o = self.op
        def ld(dst_ap, src_ap, key):
            o("Q0", lambda e, d=dst_ap, s=src_ap: e.dma_start(out=d, in_=s),
              writes=[key], tag=f"ld:{key}")
        ld(self.WinT.ap(), self.i_WinT.rearrange("(kt p) e -> p kt e", p=128), "WinT")
        ld(self.WkT.ap(), self.i_WkT.rearrange("(kt p) m -> p kt m", p=128), "WkT")
        ld(self.WqT.ap(), self.i_WqT.rearrange("(kt p) m -> p kt m", p=128), "WqT")
        ld(self.WvT.ap(), self.i_WvT.rearrange("(kt p) m -> p kt m", p=128), "WvT")
        ld(self.WbT.ap(), self.i_WbT.rearrange("(kt p) m -> p kt m", p=128), "WbT")
        ld(self.WoT.ap(), self.i_WoT.rearrange("(mt p) d -> p mt d", p=128), "WoT")
        ld(self.bbeta.ap(), self.i_bbeta, "bbeta")
        ld(self.wrout.ap(), self.i_wrout.rearrange("(kt p) e -> p kt e", p=128), "wrout")
        ld(self.eye.ap(), self.i_eye, "eye")
        ld(self.sel[0].ap(), self.i_sel[0:8, :], "sel:0")
        ld(self.sel[1].ap(), self.i_sel[8:16, :], "sel:1")
        ld(self.selk[0].ap(), self.i_selk[0:8, :], "selk:0")
        ld(self.selk[1].ap(), self.i_selk[8:16, :], "selk:1")
        ld(self.bones[0].ap(), self.i_bones[0:128, :], "bones:0")
        ld(self.bones[1].ap(), self.i_bones[128:256, :], "bones:1")
        ld(self.ones128.ap(), self.i_ones128, "ones128")
        ld(self.onesrow.ap(), self.i_onesrow, "onesrow")
        ld(self.zeroS.ap(), self.i_zeros, "zeroS")
        ld(self.esel.ap(), self.i_esel, "esel")
        ld(self.epsc.ap(), self.i_epsc, "epsc")

    def proj_chunk(self, c):
        o = self.op
        sl = c % 2
        t0 = c * CT
        o("Q0", lambda e: e.dma_start(out=self.xin[:, 0, :], in_=self.i_x[t0:t0 + CT, :]),
          reads=[], writes=["xin"], tag=f"ldx:{c}")
        wsl = c % self.NW
        for dt in range(NDT):
            ps = [self.psT0, self.psT1][dt % 2]
            pskey = f"psT{dt % 2}"
            o("T", lambda e, dt=dt, ps=ps: e.transpose(
                ps[:, 0:128], self.xin[:, 0, dt * 128:(dt + 1) * 128], self.eye[:]),
              reads=["xin", "eye", pskey + ":f"], writes=[pskey], tag=f"tr:{c}:{dt}")
            o("S", lambda e, dt=dt, ps=ps: e.copy(
                out=self.xw[:, wsl, dt, :], in_=ps[:, 0:128]),
              reads=[pskey], writes=[f"xw:{wsl}:{dt}", pskey + ":f"], tag=f"trev:{c}:{dt}")
        for et in range(NDT):
            ps = [self.psA, self.psB][et % 2]
            pk = ["psA", "psB"][et % 2]
            for kd in range(NDT):
                o("T", lambda e, et=et, kd=kd, ps=ps: e.matmul(
                    ps[:, 0:CT], self.WinT[:, kd, et * 128:(et + 1) * 128],
                    self.xw[:, wsl, kd, :], start=(kd == 0), stop=(kd == NDT - 1)),
                  reads=[f"xw:{wsl}:{kd}", "WinT"] + ([pk + ":f"] if kd == 0 else [pk]),
                  writes=[pk], tag=f"xp:{c}:{et}:{kd}")
            o("S", lambda e, et=et, ps=ps: e.activation(
                out=self.xp[:, 0, et, :], in_=ps[:, 0:CT], func=ACT.Silu),
              reads=[pk], writes=[f"xp:{et}", pk + ":f"], tag=f"xpev:{c}:{et}")
        projs = [("k", self.WkT, None), ("q", self.WqT, None),
                 ("v", self.WvT, None), ("b", self.WbT, "bias")]
        for nm, W, bias in projs:
            for ft in range(2):
                ps = [self.psA, self.psB][ft]
                pk = ["psA", "psB"][ft]
                for kd in range(NDT):
                    stp = (kd == NDT - 1 and bias is None)
                    o("T", lambda e, W=W, ft=ft, kd=kd, ps=ps, stp=stp: e.matmul(
                        ps[:, 0:CT], W[:, kd, ft * 128:(ft + 1) * 128],
                        self.xp[:, 0, kd, :], start=(kd == 0), stop=stp),
                      reads=[f"xp:{kd}", "W" + nm + "T"] + ([pk + ":f"] if kd == 0 else [pk]),
                      writes=[pk], tag=f"pj{nm}:{c}:{ft}:{kd}")
                if bias is not None:
                    o("T", lambda e, ft=ft, ps=ps: e.matmul(
                        ps[:, 0:CT], self.bbeta.ap()[0:1, ft * 128:(ft + 1) * 128],
                        self.onesrow.ap()[0:1, 0:CT], start=False, stop=True),
                      reads=["bbeta", "onesrow", pk], writes=[pk], tag=f"pjbias:{c}:{ft}")
                if nm == "k":
                    o("V", lambda e, ft=ft, ps=ps: e.tensor_copy(
                        self.k_sb[:, ft, :], ps[:, 0:CT]),
                      reads=[pk], writes=[f"ksb:{ft}", pk + ":f"], tag=f"kev:{c}:{ft}")
                elif nm == "q":
                    o("V", lambda e, ft=ft, ps=ps: e.tensor_copy(
                        self.q_sb[:, sl, ft, :], ps[:, 0:CT]),
                      reads=[pk], writes=[f"qsb:{sl}:{ft}", pk + ":f"], tag=f"qev:{c}:{ft}")
                elif nm == "v":
                    o("S", lambda e, ft=ft, ps=ps: e.copy(
                        out=self.v_fm[:, sl, ft, :], in_=ps[:, 0:CT]),
                      reads=[pk], writes=[f"vfm:{sl}:{ft}", pk + ":f"], tag=f"vev:{c}:{ft}")
                else:
                    o("S", lambda e, ft=ft, ps=ps: e.activation(
                        out=self.b_fm[:, sl, ft, :], in_=ps[:, 0:CT], func=ACT.Sigmoid),
                      reads=[pk], writes=[f"bfm:{sl}:{ft}", pk + ":f"], tag=f"bev:{c}:{ft}")
                    o("V", lambda e, ft=ft: e.reciprocal(
                        self.rb_fm[:, sl, ft, :], self.b_fm[:, sl, ft, :]),
                      reads=[f"bfm:{sl}:{ft}"], writes=[f"rbfm:{sl}:{ft}"], tag=f"rbev:{c}:{ft}")
        for ft in range(2):
            o("S", lambda e, ft=ft: e.activation(
                out=self.ksq[:, ft, :], in_=self.k_sb[:, ft, :], func=ACT.Square),
              reads=[f"ksb:{ft}"], writes=[f"ksq:{ft}"], tag=f"ksq:{c}:{ft}")
        for ft in range(2):
            o("T", lambda e, ft=ft: e.matmul(
                self.psT1[0:8, 0:CT], self.bones[ft][:],
                self.ksq[:, ft, :], start=(ft == 0), stop=(ft == 1)),
              reads=[f"ksq:{ft}", f"bones:{ft}"] + (["psT1:f"] if ft == 0 else ["psT1"]),
              writes=["psT1"], tag=f"knrm:{c}:{ft}")
        nr = self.nrm
        o("S", lambda e: e.activation(out=nr[:, 1, :], in_=self.psT1[0:8, 0:CT], func=ACT.Sqrt),
          reads=["psT1"], writes=["nrm1"], tag=f"knsqrt:{c}")
        o("V", lambda e: e.tensor_copy(nr[:, 0, :], self.psT1[0:8, 0:CT]),
          reads=["psT1"], writes=["nrm0", "psT1:f"], tag=f"knss:{c}")
        o("V", lambda e: e.reciprocal(nr[:, 2, :], nr[:, 1, :]),
          reads=["nrm1"], writes=["nrm2"], tag=f"knr1:{c}")
        o("V", lambda e: e.tensor_tensor(nr[:, 2, :], nr[:, 0, :], nr[:, 2, :], A.mult),
          reads=["nrm0", "nrm2"], writes=["nrm2"], tag=f"knt:{c}")
        o("V", lambda e: e.tensor_tensor(nr[:, 0, :], nr[:, 1, :], nr[:, 2, :], A.add),
          reads=["nrm1", "nrm2"], writes=["nrm0"], tag=f"kns1:{c}")
        o("V", lambda e: e.tensor_scalar(
            out=nr[:, 0, :], in0=nr[:, 0, :], scalar1=0.5, scalar2=1e-6, op0=A.mult, op1=A.add),
          reads=["nrm0"], writes=["nrm0"], tag=f"kns2:{c}")
        o("V", lambda e: e.reciprocal(nr[:, 2, :], nr[:, 0, :]),
          reads=["nrm0"], writes=["nrm5"], tag=f"knr2:{c}")
        for ft in range(2):
            o("T", lambda e, ft=ft: e.matmul(
                self.psT1[:, 0:CT], self.selk[ft][:],
                nr[:, 2, :], start=True, stop=True),
              reads=["nrm5", f"selk:{ft}", "psT1:f"], writes=["psT1"], tag=f"knbc:{c}:{ft}")
            o("V", lambda e, ft=ft: e.tensor_tensor(
                self.kn_fm[:, sl, ft, :], self.k_sb[:, ft, :], self.psT1[:, 0:CT], A.mult),
              reads=["psT1", f"ksb:{ft}"], writes=[f"knfm:{sl}:{ft}", "psT1:f"],
              tag=f"knmul:{c}:{ft}")

    def scan_chunk(self, c):
        o = self.op
        sl = c % 2
        NWIN = CT // TB
        for sg in range(CT // TS):
            ss = sg % 2
            for kq, (src_t, skey) in enumerate(
                    [(self.kn_fm, "knfm"), (self.q_sb, "qsb")]):
                for ft in range(2):
                    o("QG", lambda e, kq=kq, ft=ft, src_t=src_t, ss=ss, sg=sg: e.dma_start(
                        out=self.knq8[ft * 4:(ft + 1) * 4, ss, kq, :, :],
                        in_=src_t[:, sl, ft, sg * TS:(sg + 1) * TS]),
                      reads=[f"{skey}:{sl}:{ft}"], writes=[f"knq8:{ss}"],
                      tag=f"rsh:{c}:{sg}:{kq}:{ft}")
            for wl in range(TS // TB):
                W = c * NWIN + sg * (TS // TB) + wl
                wp = W % 2
                wr = W % RW
                u0 = sg * TS + wl * TB
                for hf in range(2):
                    ps = self.psW[hf][wp]
                    pk = f"psW{hf}{wp}"
                    o("T", lambda e, hf=hf, ps=ps, wl=wl, ss=ss: e.matmul(
                        ps[:, 0:512], self.sel[hf][:],
                        self.knq8[0:8, ss, :, :, wl * TB:(wl + 1) * TB],
                        start=True, stop=True),
                      reads=[f"knq8:{ss}", f"sel:{hf}", pk + ":f"], writes=[pk],
                      tag=f"wbc:{W}:{hf}")
                    o("S", lambda e, hf=hf, ps=ps, wp=wp: e.copy(
                        out=self.bcw[:, hf, wp, :, :, :], in_=ps[:, 0:512]),
                      reads=[pk], writes=[f"bcw:{hf}:{wp}", pk + ":f"],
                      tag=f"wbcev:{W}:{hf}")
                for t in range(TB):
                    u = u0 + t
                    for hf in range(2):
                        if W == 0 and t == 0 and c == 0:
                            prev = self.zeroS[:, :]
                            prevkey = "zeroS"
                        elif t == 0:
                            prev = self.hist[:, hf, (W - 1) % RW, :, TB - 1]
                            prevkey = f"hist:{hf}:{(W - 1) % RW}"
                        else:
                            prev = self.hist[:, hf, wr, :, t - 1]
                            prevkey = f"hist:{hf}:{wr}"
                        knsl = self.bcw[:, hf, wp, 0, :, t]
                        vcol = self.v_fm[:, sl, hf, u:u + 1]
                        rbcol = self.rb_fm[:, sl, hf, u:u + 1]
                        bcol = self.b_fm[:, sl, hf, u:u + 1]
                        o("V", lambda e, hf=hf, prev=prev, knsl=knsl: e.scalar_tensor_tensor(
                            out=self.m_scr[:, hf, :], in0=prev, scalar=-1.0, in1=knsl,
                            op0=A.mult, op1=A.mult, accum_out=self.dcol[:, hf, :]),
                          reads=[prevkey, f"bcw:{hf}:{wp}", f"mscr:{hf}:f"],
                          writes=[f"mscr:{hf}", f"dcol:{hf}"], tag=f"ttr:{u}:{hf}")
                        o("V", lambda e, hf=hf, vcol=vcol, rbcol=rbcol: e.tensor_scalar(
                            out=self.gcol[:, hf, :], in0=self.dcol[:, hf, :],
                            scalar1=vcol, scalar2=rbcol, op0=A.add, op1=A.mult),
                          reads=[f"dcol:{hf}", f"vfm:{sl}:{hf}", f"rbfm:{sl}:{hf}"],
                          writes=[f"gcol:{hf}"], tag=f"gc:{u}:{hf}")
                        o("V", lambda e, hf=hf, prev=prev, knsl=knsl: e.scalar_tensor_tensor(
                            out=self.inner[:, hf, :], in0=knsl, scalar=self.gcol[:, hf, :],
                            in1=prev, op0=A.mult, op1=A.add),
                          reads=[prevkey, f"bcw:{hf}:{wp}", f"gcol:{hf}"],
                          writes=[f"inner:{hf}"], tag=f"stt:{u}:{hf}")
                        o("S", lambda e, hf=hf, wr=wr, t=t, bcol=bcol: e.activation(
                            out=self.hist[:, hf, wr, :, t], in_=self.inner[:, hf, :],
                            func=ACT.Tanh, scale=bcol),
                          reads=[f"inner:{hf}", f"bfm:{sl}:{hf}"],
                          writes=[f"hist:{hf}:{wr}", f"mscr:{hf}:f"], tag=f"tanh:{u}:{hf}")
                for hf in range(2):
                    o("G", lambda e, hf=hf, wr=wr, wp=wp: e.tensor_tensor(
                        self.gq_scr[:, 0, :, :], self.hist[:, hf, wr, :, :],
                        self.bcw[:, hf, wp, 1, :, :], A.mult),
                      reads=[f"hist:{hf}:{wr}", f"bcw:{hf}:{wp}"],
                      writes=["gq"], tag=f"mq:{W}:{hf}")
                    o("V", lambda e, hf=hf, u0=u0: e.tensor_reduce(
                        out=self.Sq[:, 0, hf, u0:u0 + TB],
                        in_=self.gq_scr[:, 0, :, :].transpose([0, 2, 1]),
                        axis=AX.X, op=A.add),
                      reads=["gq"], writes=[f"Sq:{hf}"], tag=f"sqr:{W}:{hf}")

    def wout_chunk(self, c):
        o = self.op
        sl = c % 2
        wsl = c % self.NW
        for hf in range(2):
            o("S", lambda e, hf=hf: e.activation(
                out=self.cell[:, sl, hf, :], in_=self.Sq[:, 0, hf, :], func=ACT.Sigmoid),
              reads=[f"Sq:{hf}"], writes=[f"cell:{sl}:{hf}"], tag=f"sgm:{c}:{hf}")
            o("V", lambda e, hf=hf: e.tensor_tensor(
                self.cell[:, sl, hf, :], self.cell[:, sl, hf, :], self.Sq[:, 0, hf, :], A.mult),
              reads=[f"Sq:{hf}", f"cell:{sl}:{hf}"], writes=[f"cell:{sl}:{hf}"],
              tag=f"cl1:{c}:{hf}")
            o("V", lambda e, hf=hf: e.tensor_tensor(
                self.cell[:, sl, hf, :], self.cell[:, sl, hf, :], self.Sq[:, 0, hf, :], A.mult),
              reads=[f"Sq:{hf}", f"cell:{sl}:{hf}"], writes=[f"cell:{sl}:{hf}"],
              tag=f"cl2:{c}:{hf}")
        for dt in range(NDT):
            ps = [self.psT0, self.psT1][dt % 2]
            pk = ["psT0", "psT1"][dt % 2]
            for mt in range(2):
                o("T", lambda e, dt=dt, mt=mt, ps=ps: e.matmul(
                    ps[:, 0:CT], self.WoT[:, mt, dt * 128:(dt + 1) * 128],
                    self.cell[:, sl, mt, :], start=(mt == 0), stop=(mt == 1)),
                  reads=[f"cell:{sl}:{mt}", "WoT"] + ([pk + ":f"] if mt == 0 else [pk]),
                  writes=[pk], tag=f"wo:{c}:{dt}:{mt}")
            o("V", lambda e, dt=dt, ps=ps: e.tensor_tensor(
                self.s_fm[:, dt, c * CT:(c + 1) * CT], self.xw[:, wsl, dt, :],
                ps[:, 0:CT], A.add),
              reads=[pk, f"xw:{wsl}:{dt}"], writes=[f"s:{c}:{dt}", pk + ":f"],
              tag=f"woev:{c}:{dt}")

    def dump_sfm(self):
        o = self.op
        for dt in range(NDT):
            o("Q0", lambda e, dt=dt: e.dma_start(
                out=self.o_out[0:self.T, dt * 128:(dt + 1) * 128].transpose([1, 0]),
                in_=self.s_fm[:, dt, :], allow_slow_non_contiguous=True),
              reads=[f"s:{c}:{dt}" for c in range(self.NC)], writes=[f"outdma:{dt % (self.T // 128)}"],
              tag=f"dump:{dt}")

    def terminal(self):
        o = self.op
        keys = [f"outdma:{c}" for c in range(self.T // 128)]
        o("G", lambda e: e.nop(), reads=keys, tag="fin")

    def build_attn(self):
        self.load_weights()
        self.proj_chunk(0)
        for c in range(1, self.NC):
            self.proj_chunk(c)
            self.scan_chunk(c - 1)
            self.wout_chunk(c - 1)
        self.scan_chunk(self.NC - 1)
        self.wout_chunk(self.NC - 1)

    def finish(self):
        self.sch.finalize()
        sems = {}
        for k in _INC:
            if k in ("Q0", "Q1", "QG"):
                for ln in range(NLANES):
                    sems[(k, ln)] = self.nc.alloc_semaphore(f"sem{k}_{ln}")
            else:
                sems[k] = self.nc.alloc_semaphore(f"sem{k}")
        self.sch.emit(self.nc, sems)

    def alloc_ffn(self):
        nc = self.nc
        def addr(t):
            return nc.lookup_mloc(t).addr
        def at(name, shape, base, off):
            return nc.alloc_sbuf_tensor_at(name, shape, F32, offset=base + off)
        baseA = addr(self.WinT)
        endA = addr(self.WoT) + 2 * D * 4
        baseB = addr(self.xin)
        endB = addr(self.cell) + 2 * 2 * CT * 4
        T_ = self.T
        assert endA - baseA >= NDT * T_ * 4 + 8192, (baseA, endA)
        self.acc = at("acc_fm", [128, NDT, T_], baseA, 0)
        offA = NDT * T_ * 4
        self.s1tm = at("s1tm", [128, D], baseA, offA); offA += D * 4
        self.o1tm = at("o1tm", [128, D], baseA, offA); offA += D * 4
        self.lncA = at("lncA", [128, 12], baseA, offA); offA += 64
        offB = 0
        def atB(name, shape, nbytes):
            nonlocal offB
            t = at(name, shape, baseB, offB)
            offB += nbytes
            return t
        self.wf1p = atB("wf1p", [128, 2, 2, NDT, 128], 2 * 2 * NDT * 128 * 4)
        self.wf1g = atB("wf1g", [128, 2, 2, NDT, 128], 2 * 2 * NDT * 128 * 4)
        self.wf2 = atB("wf2", [128, 2, 2, NDT, 128], 2 * 2 * NDT * 128 * 4)
        self.gfm = atB("gfm", [8, T_], T_ * 4)
        self.gbc = atB("gbc", [128, 512], 512 * 4)
        self.mgrp = atB("mgrp", [128, 4, SLAB], 4 * SLAB * 4)
        self.hgt = atB("hgt", [128, SLAB], SLAB * 4)
        self.lgt = atB("lgt", [128, 8], 8 * 4)
        self.gtm = atB("gtm", [128, 8], 8 * 4)
        self.rsc = atB("rsc", [128, 16], 16 * 4)
        offC = 0
        baseC = addr(self.wf1p)
        def atC(name, shape, nbytes):
            nonlocal offC
            t = at(name, shape, baseC, offC)
            offC += nbytes
            return t
        self.s2sb = atC("s2sb", [128, NDT, 128], NDT * 128 * 4)
        self.s2tm = atC("s2tm", [128, 2, D], 2 * D * 4)
        self.otm = atC("otm", [128, 2, D], 2 * D * 4)
        self.scr2 = atC("scr2", [128, D], D * 4)
        self.lnc = atC("lnc", [128, 12], 12 * 4)

    def barrier(self, name, keys):
        self.op("V", lambda e: e.engine_nop(), reads=keys, writes=[name], tag=f"bar:{name}")

    def ln1(self):
        o = self.op
        T_ = self.T
        for c in range(T_ // 128):
            for dt in range(NDT):
                ps = [self.psT0, self.psT1][dt % 2]
                pk = ["psT0", "psT1"][dt % 2]
                o("T", lambda e, c=c, dt=dt, ps=ps: e.transpose(
                    ps[:, 0:128], self.s_fm[:, dt, c * 128:(c + 1) * 128], self.eye[:]),
                  reads=[f"s:{c}:{dt}", "eye", pk + ":f"], writes=[pk], tag=f"l1tr:{c}:{dt}")
                o("S", lambda e, dt=dt, ps=ps: e.copy(
                    out=self.s1tm[:, dt * 128:(dt + 1) * 128], in_=ps[:, 0:128]),
                  reads=[pk, "PHASE2"], writes=["s1tm"], tag=f"l1trev:{c}:{dt}")
            L = self.lncA
            o("S", lambda e: e.activation(
                out=self.o1tm[:, :], in_=self.s1tm[:, :], func=ACT.Copy,
                accum_out=L[:, 0:1]),
              reads=["s1tm", "PHASE2"], writes=["o1tm", "la0"], tag=f"l1sum:{c}")
            o("S", lambda e: e.activation(
                out=self.o1tm[:, :], in_=self.s1tm[:, :], func=ACT.Square,
                accum_out=L[:, 1:2]),
              reads=["s1tm"], writes=["o1tm", "la1"], tag=f"l1sq:{c}")
            sc = 1.0 / D
            o("V", lambda e: e.tensor_scalar(out=L[:, 2:3], in0=L[:, 0:1], scalar1=sc,
                scalar2=None, op0=A.mult),
              reads=["la0"], writes=["la2"], tag=f"l1mu:{c}")
            o("V", lambda e: e.tensor_scalar(out=L[:, 3:4], in0=L[:, 1:2], scalar1=sc,
                scalar2=None, op0=A.mult),
              reads=["la1"], writes=["la3"], tag=f"l1ex2:{c}")
            o("V", lambda e: e.tensor_tensor(L[:, 4:5], L[:, 2:3], L[:, 2:3], A.mult),
              reads=["la2"], writes=["la4"], tag=f"l1m2:{c}")
            o("V", lambda e: e.tensor_tensor(L[:, 4:5], L[:, 3:4], L[:, 4:5], A.subtract),
              reads=["la3", "la4"], writes=["la4"], tag=f"l1var:{c}")
            self.rstd_cols(L, f"l1rs:{c}", 4, 5, eps=1e-5, pre="la")
            o("V", lambda e: e.tensor_scalar(
                out=self.o1tm[:, :], in0=self.s1tm[:, :],
                scalar1=L[:, 2:3], scalar2=L[:, 5:6], op0=A.subtract, op1=A.mult),
              reads=["s1tm", "la2", "la5"], writes=["o1tm"], tag=f"l1nrm:{c}")
            for dt in range(NDT):
                ps = [self.psT0, self.psT1][dt % 2]
                pk = ["psT0", "psT1"][dt % 2]
                o("T", lambda e, dt=dt, ps=ps: e.transpose(
                    ps[:, 0:128], self.o1tm[:, dt * 128:(dt + 1) * 128], self.eye[:]),
                  reads=["o1tm", "eye", pk + ":f"], writes=[pk], tag=f"l1trb:{c}:{dt}")
                o("V", lambda e, c=c, dt=dt, ps=ps: e.tensor_copy(
                    self.s_fm[:, dt, c * 128:(c + 1) * 128], ps[:, 0:128]),
                  reads=[pk], writes=[f"s:{c}:{dt}", pk + ":f"], tag=f"l1trbe:{c}:{dt}")

    def router(self):
        o = self.op
        T_ = self.T
        nt = T_ // 128
        for tt in range(nt):
            cc = tt
            for kd in range(NDT):
                o("T", lambda e, tt=tt, kd=kd: e.matmul(
                    self.psB[:, 0:8], self.s_fm[:, kd, tt * 128:(tt + 1) * 128],
                    self.wrout[:, kd, :], start=(kd == 0), stop=(kd == NDT - 1)),
                  reads=[f"s:{cc}:{kd}", "wrout"] + (["psB:f"] if kd == 0 else ["psB"]),
                  writes=["psB"], tag=f"rt:{tt}:{kd}")
            o("V", lambda e, tt=tt: e.tensor_copy(self.lgt[:, :], self.psB[:, 0:8]),
              reads=["psB"], writes=["lgt", "psB:f"], tag=f"rtev:{tt}")
            R = self.rsc
            o("V", lambda e, tt=tt: e.max(out=R[:, 0:8], in_=self.lgt[:, :]),
              reads=["lgt"], writes=["rsc8"], tag=f"rtmax:{tt}")
            o("V", lambda e: e.tensor_tensor(R[:, 8:9], R[:, 0:1], R[:, 1:2], A.subtract),
              reads=["rsc8"], writes=["rscd"], tag=f"rtd:{tt}")
            o("S", lambda e: e.activation(out=R[:, 9:10], in_=R[:, 8:9], func=ACT.Sigmoid),
              reads=["rscd"], writes=["rscs1"], tag=f"rts1:{tt}")
            o("S", lambda e: e.activation(out=R[:, 10:11], in_=R[:, 8:9], func=ACT.Sigmoid,
                                          scale=-1.0),
              reads=["rscd"], writes=["rscs2"], tag=f"rts2:{tt}")
            o("V", lambda e, tt=tt: e.tensor_tensor(
                self.gtm[:, :], self.lgt[:, :],
                R[:, 0:1].to_broadcast([128, 8]), A.is_equal),
              reads=["lgt", "rsc8"], writes=["gtm"], tag=f"rte1:{tt}")
            o("V", lambda e, tt=tt: e.tensor_scalar(
                out=self.gtm[:, :], in0=self.gtm[:, :], scalar1=R[:, 9:10],
                scalar2=None, op0=A.mult),
              reads=["gtm", "rscs1"], writes=["gtm"], tag=f"rtg1:{tt}")
            o("V", lambda e, tt=tt: e.tensor_tensor(
                R[:, 0:8], self.lgt[:, :],
                R[:, 1:2].to_broadcast([128, 8]), A.is_equal),
              reads=["lgt", "rsc8"], writes=["rsc8b"], tag=f"rte2:{tt}")
            o("V", lambda e: e.tensor_scalar(
                out=R[:, 0:8], in0=R[:, 0:8], scalar1=R[:, 10:11],
                scalar2=None, op0=A.mult),
              reads=["rsc8b", "rscs2"], writes=["rsc8b"], tag=f"rtg2:{tt}")
            o("V", lambda e, tt=tt: e.tensor_tensor(
                self.gtm[:, :], self.gtm[:, :], R[:, 0:8], A.add),
              reads=["gtm", "rsc8b"], writes=["gtm"], tag=f"rtg3:{tt}")
            o("T", lambda e, tt=tt: e.transpose(
                self.psB[0:8, 0:128], self.gtm[:, :], self.eye[:]),
              reads=["gtm", "eye", "psB:f"], writes=["psB"], tag=f"rttr:{tt}")
            o("V", lambda e, tt=tt: e.tensor_copy(
                self.gfm.ap()[0:8, tt * 128:(tt + 1) * 128], self.psB[0:8, 0:128]),
              reads=["psB"], writes=[f"gfm:{tt}", "psB:f"], tag=f"rttrev:{tt}")

    def moe(self):
        o = self.op
        T_ = self.T
        SL = min(SLAB, T_)
        nsl = T_ // SL
        nt = T_ // 128
        for dt in range(NDT):
            o("G", lambda e, dt=dt: e.memset(self.acc[:, dt, :], 0.0),
              reads=["PHASE2"], writes=[f"acc:{dt}"], tag=f"accz:{dt}")
        for e_i in range(E + 1):
            shared = (e_i == E)
            KI = ISH // 128 if shared else D // 128
            if not shared:
                pass
            for g4 in range(KI // 4):
                for half in range(2):
                    slot = half
                    for kk in range(2):
                        ki = g4 * 4 + half * 2 + kk
                        if shared:
                            p_src = self.i_gate[:, ki * 128:(ki + 1) * 128]
                            g_src = self.i_up[:, ki * 128:(ki + 1) * 128]
                            f2_src = self.i_down[ki * 128:(ki + 1) * 128, :]
                        else:
                            p_src = self.i_fc1[e_i, :, ki * 128:(ki + 1) * 128]
                            g_src = self.i_fc1[e_i, :, D + ki * 128:D + (ki + 1) * 128]
                            f2_src = self.i_fc2[e_i, ki * 128:(ki + 1) * 128, :]
                        o("Q0", lambda e, p_src=p_src, slot=slot, kk=kk: e.dma_start(
                            out=self.wf1p[:, slot, kk, :, :],
                            in_=p_src.rearrange("(kt p) i -> p kt i", p=128)),
                          reads=["PHASE2"], writes=[f"wf1p:{slot}:{kk}"], tag=f"w1p:{e_i}:{ki}")
                        o("Q0", lambda e, g_src=g_src, slot=slot, kk=kk: e.dma_start(
                            out=self.wf1g[:, slot, kk, :, :],
                            in_=g_src.rearrange("(kt p) i -> p kt i", p=128)),
                          reads=["PHASE2"], writes=[f"wf1g:{slot}:{kk}"], tag=f"w1g:{e_i}:{ki}")
                        o("Q1", lambda e, f2_src=f2_src, slot=slot, kk=kk: e.dma_start(
                            out=self.wf2[:, slot, kk, :, :],
                            in_=f2_src.rearrange("p (dt n) -> p dt n", dt=NDT)),
                          reads=["PHASE2"], writes=[f"wf2:{slot}:{kk}"], tag=f"w2:{e_i}:{ki}")
                for sl_i in range(nsl):
                    s0 = sl_i * SL
                    cks = [c for c in range(T_ // CT) if c * CT >= s0 and c * CT < s0 + SL]
                    xkeys = [f"s:{c}:{kd}" for c in cks for kd in range(NDT)]
                    LS = min(512, T_)
                    if (not shared) and s0 % LS == 0:
                        gb0 = s0
                        o("T", lambda e, e_i=e_i, gb0=gb0, LS=LS: e.matmul(
                            self.psT0[:, 0:LS], self.esel.ap()[:, e_i * 128:(e_i + 1) * 128],
                            self.gfm.ap()[:, gb0:gb0 + LS], start=True, stop=True),
                          reads=[f"gfm:{tt}" for tt in range(nt)] + ["esel", "psT0:f"],
                          writes=["psT0"], tag=f"gbc:{e_i}:{g4}:{gb0}")
                        o("S", lambda e, LS=LS: e.copy(
                            out=self.gbc[:, 0:LS], in_=self.psT0[:, 0:LS]),
                          reads=["psT0", "PHASE2"], writes=["gbc", "psT0:f"],
                          tag=f"gbcev:{e_i}:{g4}:{gb0}")
                    for kq in range(4):
                        slot, kk = kq // 2, kq % 2
                        for kd in range(NDT):
                            o("T", lambda e, slot=slot, kk=kk, kd=kd, s0=s0: e.matmul(
                                self.psA[:, 0:SL], self.wf1p[:, slot, kk, kd, :],
                                self.s_fm[:, kd, s0:s0 + SL],
                                start=(kd == 0), stop=(kd == NDT - 1)),
                              reads=xkeys + [f"wf1p:{slot}:{kk}"] + (["psA:f"] if kd == 0 else ["psA"]),
                              writes=["psA"], tag=f"f1p:{e_i}:{g4}:{sl_i}:{kq}:{kd}")
                            o("T", lambda e, slot=slot, kk=kk, kd=kd, s0=s0: e.matmul(
                                self.psB[:, 0:SL], self.wf1g[:, slot, kk, kd, :],
                                self.s_fm[:, kd, s0:s0 + SL],
                                start=(kd == 0), stop=(kd == NDT - 1)),
                              reads=xkeys + [f"wf1g:{slot}:{kk}"] + (["psB:f"] if kd == 0 else ["psB"]),
                              writes=["psB"], tag=f"f1g:{e_i}:{g4}:{sl_i}:{kq}:{kd}")
                        o("S", lambda e, kq=kq: e.activation(out=self.mgrp[:, kq, 0:SL],
                            in_=self.psA[:, 0:SL], func=ACT.Silu),
                          reads=["psA"], writes=[f"m:{kq}", "psA:f"], tag=f"hp:{e_i}:{g4}:{sl_i}:{kq}")
                        if shared:
                            o("V", lambda e: e.tensor_copy(self.hgt[:, 0:SL], self.psB[:, 0:SL]),
                              reads=["psB"], writes=["hgt", "psB:f"], tag=f"hg:{e_i}:{g4}:{sl_i}:{kq}")
                        else:
                            o("V", lambda e, s0=s0: e.tensor_tensor(
                                self.hgt[:, 0:SL], self.psB[:, 0:SL],
                                self.gbc[:, s0 % 512:s0 % 512 + SL], A.mult),
                              reads=["psB", "gbc"], writes=["hgt", "psB:f"],
                              tag=f"hg:{e_i}:{g4}:{sl_i}:{kq}")
                        o("V", lambda e, kq=kq: e.tensor_tensor(
                            self.mgrp[:, kq, 0:SL], self.mgrp[:, kq, 0:SL], self.hgt[:, 0:SL], A.mult),
                          reads=[f"m:{kq}", "hgt"], writes=[f"m:{kq}"], tag=f"m:{e_i}:{g4}:{sl_i}:{kq}")
                    for dt in range(NDT):
                        ps = self.psW[dt % 2][(dt // 2) % 2]
                        pk = f"psW{dt % 2}{(dt // 2) % 2}"
                        for kq in range(4):
                            slot, kk = kq // 2, kq % 2
                            o("T", lambda e, slot=slot, kk=kk, dt=dt, kq=kq, ps=ps: e.matmul(
                                ps[:, 0:SL], self.wf2[:, slot, kk, dt, :],
                                self.mgrp[:, kq, 0:SL],
                                start=(kq == 0), stop=(kq == 3)),
                              reads=[f"m:{kq}", f"wf2:{slot}:{kk}"] + ([pk + ":f"] if kq == 0 else [pk]),
                              writes=[pk], tag=f"f2:{e_i}:{g4}:{sl_i}:{dt}:{kq}")
                        o("V", lambda e, dt=dt, s0=s0, ps=ps: e.tensor_tensor(
                            self.acc[:, dt, s0:s0 + SL], self.acc[:, dt, s0:s0 + SL],
                            ps[:, 0:SL], A.add),
                          reads=[pk, f"acc:{dt}"], writes=[f"acc:{dt}", pk + ":f"],
                          tag=f"f2ev:{e_i}:{g4}:{sl_i}:{dt}")

    def ln2_out(self):
        o = self.op
        T_ = self.T
        for c in range(T_ // 128):
            sl = c % 2
            for dt in range(NDT):
                o("V", lambda e, c=c, dt=dt: e.tensor_tensor(
                    self.s2sb[:, dt, :], self.s_fm[:, dt, c * 128:(c + 1) * 128],
                    self.acc[:, dt, c * 128:(c + 1) * 128], A.add),
                  reads=[f"s:{c}:{dt}", f"acc:{dt}", "PHASE3"],
                  writes=[f"s2:{dt}"], tag=f"o_add:{c}:{dt}")
                ps = [self.psT0, self.psT1][dt % 2]
                pk = ["psT0", "psT1"][dt % 2]
                o("T", lambda e, dt=dt, ps=ps: e.transpose(
                    ps[:, 0:128], self.s2sb[:, dt, :], self.eye[:]),
                  reads=[f"s2:{dt}", "eye", pk + ":f"], writes=[pk], tag=f"o_tr:{c}:{dt}")
                o("S", lambda e, dt=dt, sl=sl, ps=ps: e.copy(
                    out=self.s2tm[:, sl, dt * 128:(dt + 1) * 128], in_=ps[:, 0:128]),
                  reads=[pk, "PHASE3"], writes=[f"s2tm:{sl}", pk + ":f"], tag=f"o_trev:{c}:{dt}")
            L = self.lnc
            o("S", lambda e, sl=sl: e.activation(
                out=self.scr2[:, :], in_=self.s2tm[:, sl, :], func=ACT.Copy,
                accum_out=L[:, 0:1]),
              reads=[f"s2tm:{sl}", "PHASE3"], writes=["scr2", "lc0"], tag=f"o_sum:{c}")
            o("S", lambda e, sl=sl: e.activation(
                out=self.scr2[:, :], in_=self.s2tm[:, sl, :], func=ACT.Square,
                accum_out=L[:, 1:2]),
              reads=[f"s2tm:{sl}"], writes=["scr2", "lc1"], tag=f"o_sq:{c}")
            sc = 1.0 / D
            o("V", lambda e: e.tensor_scalar(out=L[:, 2:3], in0=L[:, 0:1], scalar1=sc,
                scalar2=None, op0=A.mult),
              reads=["lc0"], writes=["lc2"], tag=f"o_mu:{c}")
            o("V", lambda e: e.tensor_scalar(out=L[:, 3:4], in0=L[:, 1:2], scalar1=sc,
                scalar2=None, op0=A.mult),
              reads=["lc1"], writes=["lc3"], tag=f"o_ex2:{c}")
            o("V", lambda e: e.tensor_tensor(L[:, 4:5], L[:, 2:3], L[:, 2:3], A.mult),
              reads=["lc2"], writes=["lc4"], tag=f"o_m2:{c}")
            o("V", lambda e: e.tensor_tensor(L[:, 4:5], L[:, 3:4], L[:, 4:5], A.subtract),
              reads=["lc3", "lc4"], writes=["lc4"], tag=f"o_var:{c}")
            self.rstd_cols(L, f"o_rs:{c}", 4, 5, eps=1e-5)
            o("V", lambda e, sl=sl: e.tensor_scalar(
                out=self.otm[:, sl, :], in0=self.s2tm[:, sl, :],
                scalar1=L[:, 2:3], scalar2=L[:, 5:6], op0=A.subtract, op1=A.mult),
              reads=[f"s2tm:{sl}", "lc2", "lc5", "PHASE3"], writes=[f"otm:{sl}"],
              tag=f"o_nrm:{c}")
            o("Q0", lambda e, c=c, sl=sl: e.dma_start(
                out=self.o_out[c * 128:(c + 1) * 128, :], in_=self.otm[:, sl, :]),
              reads=[f"otm:{sl}"], writes=[f"outdma:{c}"], tag=f"o_dma:{c}")

    def rstd_cols(self, L, tag, vrow, orow, eps, pre="lc"):
        o = self.op
        o("S", lambda e: e.activation(out=L[:, 6:7], in_=L[:, vrow:vrow + 1],
                                      func=ACT.Sqrt, bias=self.epsc.ap()),
          reads=[f"{pre}{vrow}", "epsc"], writes=[pre + "6"], tag=f"{tag}:sqrt")
        o("V", lambda e: e.tensor_scalar(out=L[:, 7:8], in0=L[:, vrow:vrow + 1],
            scalar1=eps, scalar2=None, op0=A.add),
          reads=[f"{pre}{vrow}"], writes=[pre + "7"], tag=f"{tag}:va")
        o("V", lambda e: e.reciprocal(L[:, 8:9], L[:, 6:7]),
          reads=[pre + "6"], writes=[pre + "8"], tag=f"{tag}:r0")
        o("V", lambda e: e.tensor_tensor(L[:, 7:8], L[:, 7:8], L[:, 8:9], A.mult),
          reads=[pre + "7", pre + "8"], writes=[pre + "7"], tag=f"{tag}:t")
        o("V", lambda e: e.tensor_tensor(L[:, 7:8], L[:, 7:8], L[:, 6:7], A.add),
          reads=[pre + "7", pre + "6"], writes=[pre + "7"], tag=f"{tag}:s1")
        o("V", lambda e: e.tensor_scalar(out=L[:, 7:8], in0=L[:, 7:8], scalar1=0.5,
            scalar2=None, op0=A.mult),
          reads=[pre + "7"], writes=[pre + "7"], tag=f"{tag}:s1b")
        o("V", lambda e: e.reciprocal(L[:, orow:orow + 1], L[:, 7:8]),
          reads=[pre + "7"], writes=[f"{pre}{orow}"], tag=f"{tag}:rstd")

    def build_full(self):
        self.build_attn()
        self.alloc_ffn()
        skeys = [f"s:{c}:{dt}" for c in range(self.NC) for dt in range(NDT)]
        self.barrier("PHASE2", skeys)
        self.ln1()
        self.router()
        self.moe()
        akeys = [f"acc:{dt}" for dt in range(NDT)]
        self.barrier("PHASE3", akeys)
        self.ln2_out()
        self.terminal()
        self.finish()

def build_full_test(T):
    mk = MK(T)
    mk.build_full()
    return mk


def build_attn_test(T):
    mk = MK(T)
    mk.build_attn()
    mk.dump_sfm()
    mk.terminal()
    mk.finish()
    return mk


def kernel(**inputs):
    B, T = 8, 2048
    hp = host_prep(inputs, T)
    x = np.asarray(inputs["L0_attn_x"], dtype=np.float32)
    mk = MK(T)
    mk.build_full()
    in_maps = []
    for b in range(B):
        m = dict(hp)
        m["x"] = np.ascontiguousarray(x[b])
        in_maps.append(m)
    res = run_bass_kernel_spmd(mk.nc, in_maps, core_ids=list(range(B)))
    out = np.stack([np.asarray(r["out"]) for r in res.results], axis=0)
    return out.astype(np.float32)


# version 05
# speedup vs baseline: 1.1009x; 1.1009x over previous
import numpy as np
import concourse.bass as bass
import concourse.mybir as mybir
from concourse.bass_utils import run_bass_kernel_spmd

F32 = mybir.dt.float32
A = mybir.AluOpType
ACT = mybir.ActivationFunctionType
AX = mybir.AxisListType


F32 = mybir.dt.float32
A = mybir.AluOpType
ACT = mybir.ActivationFunctionType
AX = mybir.AxisListType


NLANES = 8

class Op:
    __slots__ = ("eng", "fn", "reads", "writes", "seq", "need_inc", "waits", "soft", "tag", "lane")
    def __init__(self, eng, fn, reads, writes, tag=""):
        self.eng = eng
        self.fn = fn
        self.reads = reads
        self.writes = writes
        self.seq = None
        self.need_inc = False
        self.waits = []
        self.soft = []
        self.tag = tag
        self.lane = 0

_INC = {"T": 1, "V": 1, "S": 1, "G": 1, "Q0": 16, "Q1": 16, "QG": 16}
_QENG = {"Q0": "sync", "Q1": "scalar", "QG": "gpsimd", "T": "tensor", "V": "vector",
         "S": "scalar", "G": "gpsimd"}

class Sched:
    def __init__(self):
        self.ops = []
        self.last_writer = {}
        self.readers = {}

    def add(self, eng, fn, reads=(), writes=(), tag=""):
        op = Op(eng, fn, list(reads), list(writes), tag)
        deps = []
        for r in op.reads:
            w = self.last_writer.get(r)
            if w is not None:
                deps.append(w)
        for r in op.writes:
            w = self.last_writer.get(r)
            if w is not None:
                deps.append(w)
            deps.extend(self.readers.get(r, ()))
        seen = set()
        for d in deps:
            if d is op or id(d) in seen:
                continue
            seen.add(id(d))
            if d.eng == op.eng and d.eng == "T":
                continue
            d.need_inc = True
            op.waits.append(d)
        for r in op.reads:
            self.readers.setdefault(r, []).append(op)
        for r in op.writes:
            self.last_writer[r] = op
            self.readers[r] = []
        self.ops.append(op)
        return op

    def finalize(self):
        counters = {k: 0 for k in _INC}
        for op in self.ops:
            if op.eng in ("Q0", "Q1", "QG"):
                op.need_inc = True
                n = counters[op.eng]
                op.lane = n % NLANES
                op.seq = n // NLANES + 1
                counters[op.eng] = n + 1
            elif op.need_inc:
                counters[op.eng] += 1
                op.seq = counters[op.eng]
        self.total = dict(counters)

    def emit(self, nc, sems):
        per_eng = {}
        for op in self.ops:
            per_eng.setdefault(_QENG[op.eng], []).append(op)
        def semkey(op):
            return (op.eng, op.lane) if op.eng in ("Q0", "Q1", "QG") else op.eng
        with nc.Block() as block:
            def run(engine_obj, key):
                done = {}
                def wait(sk, tgt):
                    if done.get(sk, 0) < tgt:
                        engine_obj.wait_ge(sems[sk], tgt)
                        done[sk] = tgt
                for op in per_eng.get(key, ()):
                    for d in op.waits:
                        wait(semkey(d), d.seq * _INC[d.eng])
                    for d in op.soft:
                        if d.need_inc:
                            wait(semkey(d), d.seq * _INC[d.eng])
                    if op.eng in ("Q0", "Q1", "QG") and op.seq > 1:
                        wait(semkey(op), (op.seq - 1) * 16)
                    ins = op.fn(engine_obj)
                    if op.need_inc:
                        assert ins is not None, f"op {op.tag} needs inc but fn returned None"
                        ins.then_inc(sems[semkey(op)], _INC[op.eng])
            @block.sync
            def _(e):
                run(e, "sync")
            @block.tensor
            def _(e):
                run(e, "tensor")
            @block.vector
            def _(e):
                run(e, "vector")
            @block.scalar
            def _(e):
                run(e, "scalar")
            @block.gpsimd
            def _(e):
                run(e, "gpsimd")

D = 1024
HN = 8
NH = 32
M = 256
E = 8
ISH = 2048
NDT = 8
CT = 128
TS = 16
TB = 8
RW = 4
SLAB = 256
KG = 4

def perm256():
    p = np.zeros(256, dtype=np.int64)
    for h in range(HN):
        for i in range(NH):
            p[(h // 4) * 128 + i * 4 + (h % 4)] = h * NH + i
    return p

def host_prep(inputs, Tcore):
    P = perm256()
    f = np.ascontiguousarray
    d = {}
    d["WinT"] = f(inputs["L0_attn_W_in"].T)
    d["WkT"] = f(inputs["L0_attn_W_k"].T)
    d["WqT"] = f(inputs["L0_attn_W_q"].T)
    d["WvT"] = f(inputs["L0_attn_W_v"].T[:, P])
    d["WbT"] = f(inputs["L0_attn_W_beta"].T[:, P])
    d["bbeta"] = f(inputs["L0_attn_b_beta"].reshape(1, 256)[:, P])
    d["WoT"] = f(inputs["L0_attn_W_out"].T[P, :])
    d["wrout"] = f(inputs["L0_ffn_w_router"])
    d["fc1"] = f(inputs["L0_ffn_fc1_w"])
    d["fc2"] = f(inputs["L0_ffn_fc2_w"])
    d["gate"] = f(inputs["L0_ffn_gate_w"])
    d["up"] = f(inputs["L0_ffn_up_w"])
    d["down"] = f(inputs["L0_ffn_down_w"])
    d["eye"] = np.eye(128, dtype=np.float32)
    sel = np.zeros((2, 8, 128), dtype=np.float32)
    for hf in range(2):
        for pp in range(128):
            sel[hf, 4 * hf + (pp % 4), pp] = 1.0
    d["sel"] = f(sel.reshape(16, 128))
    selk = np.zeros((2, 8, 128), dtype=np.float32)
    for ft in range(2):
        for pp in range(128):
            selk[ft, ft * 4 + pp // 32, pp] = 1.0
    d["selk"] = f(selk.reshape(16, 128))
    bo = np.zeros((2, 128, 8), dtype=np.float32)
    for ft in range(2):
        for pp in range(128):
            bo[ft, pp, ft * 4 + pp // 32] = 1.0
    d["bones"] = f(bo.reshape(256, 8))
    d["ones128"] = np.ones((128, 1), dtype=np.float32)
    d["epsc"] = np.full((128, 1), 1e-5, dtype=np.float32)
    es = np.zeros((8, 8, 128), dtype=np.float32)
    for e in range(8):
        es[e, e, :] = 1.0
    d["esel"] = np.ascontiguousarray(es.transpose(1, 0, 2).reshape(8, 8 * 128))
    d["onesrow"] = np.ones((1, 128), dtype=np.float32)
    d["zeros"] = np.zeros((128, 32), dtype=np.float32)
    return d


class MK:
    def __init__(self, T, stop_after=None):
        self.T = T
        self.NC = T // CT
        self.stop_after = stop_after
        nc = bass.Bass("TRN2", target_bir_lowering=False)
        self.nc = nc
        self.sch = Sched()
        self.dbg = {}
        T_ = T

        def dram(name, shape):
            return nc.dram_tensor(name, list(shape), F32, kind="ExternalInput").ap()
        self.i_x = dram("x", (T_, D))
        self.i_WinT = dram("WinT", (D, D))
        self.i_WkT = dram("WkT", (D, M)); self.i_WqT = dram("WqT", (D, M))
        self.i_WvT = dram("WvT", (D, M)); self.i_WbT = dram("WbT", (D, M))
        self.i_bbeta = dram("bbeta", (1, M))
        self.i_WoT = dram("WoT", (M, D))
        self.i_wrout = dram("wrout", (D, E))
        self.i_fc1 = dram("fc1", (E, D, ISH))
        self.i_fc2 = dram("fc2", (E, D, D))
        self.i_gate = dram("gate", (D, ISH)); self.i_up = dram("up", (D, ISH))
        self.i_down = dram("down", (ISH, D))
        self.i_eye = dram("eye", (128, 128))
        self.i_sel = dram("sel", (16, 128)); self.i_selk = dram("selk", (16, 128))
        self.i_bones = dram("bones", (256, 8))
        self.i_ones128 = dram("ones128", (128, 1))
        self.i_onesrow = dram("onesrow", (1, 128))
        self.i_zeros = dram("zeros", (128, 32))
        self.i_esel = dram("esel", (8, 8 * 128))
        self.i_epsc = dram("epsc", (128, 1))
        self.o_out = nc.dram_tensor("out", [T_, D], F32, kind="ExternalOutput").ap()

        a = nc.alloc_sbuf_tensor
        self.s_fm = a("s_fm", [128, NDT, T_], F32)
        self.WinT = a("WinT_s", [128, NDT, D], F32)
        self.WkT = a("WkT_s", [128, NDT, M], F32)
        self.WqT = a("WqT_s", [128, NDT, M], F32)
        self.WvT = a("WvT_s", [128, NDT, M], F32)
        self.WbT = a("WbT_s", [128, NDT, M], F32)
        self.WoT = a("WoT_s", [128, 2, D], F32)
        self.bbeta = a("bbeta_s", [1, M], F32)
        self.wrout = a("wrout_s", [128, NDT, E], F32)
        self.eye = a("eye_s", [128, 128], F32)
        self.sel = [a("sel0_s", [8, 128], F32), a("sel1_s", [8, 128], F32)]
        self.selk = [a("selk0_s", [8, 128], F32), a("selk1_s", [8, 128], F32)]
        self.bones = [a("bones0_s", [128, 8], F32), a("bones1_s", [128, 8], F32)]
        self.ones128 = a("ones128_s", [128, 1], F32)
        self.onesrow = a("onesrow_s", [1, 128], F32)
        self.zeroS = a("zeroS_s", [128, 32], F32)
        self.esel = a("esel_s", [8, 8 * 128], F32)
        self.epsc = a("epsc_s", [128, 1], F32)
        self.xin = a("xin", [128, 1, D], F32)
        NW = 3
        self.NW = NW
        self.xw = a("xw", [128, NW, NDT, CT], F32)
        self.xp = a("xp", [128, 1, NDT, CT], F32)
        self.k_sb = a("k_sb", [128, 2, CT], F32)
        self.ksq = a("ksq", [128, 2, CT], F32)
        self.q_sb = a("q_sb", [128, 2, 2, CT], F32)
        self.kn_fm = a("kn_fm", [128, 2, 2, CT], F32)
        self.v_fm = a("v_fm", [128, 2, 2, CT], F32)
        self.b_fm = a("b_fm", [128, 2, 2, CT], F32)
        self.rb_fm = a("rb_fm", [128, 2, 2, CT], F32)
        self.nrm = a("nrm", [8, 3, CT], F32)
        self.knq8 = a("knq8", [8, 2, 2, 32, TS], F32)
        self.bcw = a("bcw", [128, 2, 2, 2, TB, 32], F32)
        self.hist = a("hist", [128, 2, RW, TB, 32], F32)
        self.m_scr = a("m_scr", [128, 2, 32], F32)
        self.inner = a("inner", [128, 2, 32], F32)
        self.dcol = a("dcol", [128, 2, 1], F32)
        self.gcol = a("gcol", [128, 2, 1], F32)
        self.Sq = a("Sq", [128, 1, 2, CT], F32)
        self.gq_scr = a("gq_scr", [128, 1, TB, 32], F32)
        self.cell = a("cell", [128, 2, 2, CT], F32)
        p = nc.alloc_psum_tensor
        self.psT0 = p("psT0", [128, 512], F32)
        self.psT1 = p("psT1", [128, 512], F32)
        self.psA = p("psA", [128, 512], F32)
        self.psB = p("psB", [128, 512], F32)
        self.psW = [[p(f"psW{hf}{j}", [128, 512], F32) for j in range(2)] for hf in range(2)]

        self.sems = {}

    def op(self, eng, fn, reads=(), writes=(), tag=""):
        return self.sch.add(eng, fn, reads, writes, tag)

    def load_weights(self):
        o = self.op
        def ld(dst_ap, src_ap, key):
            o("Q0", lambda e, d=dst_ap, s=src_ap: e.dma_start(out=d, in_=s),
              writes=[key], tag=f"ld:{key}")
        ld(self.WinT.ap(), self.i_WinT.rearrange("(kt p) e -> p kt e", p=128), "WinT")
        ld(self.WkT.ap(), self.i_WkT.rearrange("(kt p) m -> p kt m", p=128), "WkT")
        ld(self.WqT.ap(), self.i_WqT.rearrange("(kt p) m -> p kt m", p=128), "WqT")
        ld(self.WvT.ap(), self.i_WvT.rearrange("(kt p) m -> p kt m", p=128), "WvT")
        ld(self.WbT.ap(), self.i_WbT.rearrange("(kt p) m -> p kt m", p=128), "WbT")
        ld(self.WoT.ap(), self.i_WoT.rearrange("(mt p) d -> p mt d", p=128), "WoT")
        ld(self.bbeta.ap(), self.i_bbeta, "bbeta")
        ld(self.wrout.ap(), self.i_wrout.rearrange("(kt p) e -> p kt e", p=128), "wrout")
        ld(self.eye.ap(), self.i_eye, "eye")
        ld(self.sel[0].ap(), self.i_sel[0:8, :], "sel:0")
        ld(self.sel[1].ap(), self.i_sel[8:16, :], "sel:1")
        ld(self.selk[0].ap(), self.i_selk[0:8, :], "selk:0")
        ld(self.selk[1].ap(), self.i_selk[8:16, :], "selk:1")
        ld(self.bones[0].ap(), self.i_bones[0:128, :], "bones:0")
        ld(self.bones[1].ap(), self.i_bones[128:256, :], "bones:1")
        ld(self.ones128.ap(), self.i_ones128, "ones128")
        ld(self.onesrow.ap(), self.i_onesrow, "onesrow")
        ld(self.zeroS.ap(), self.i_zeros, "zeroS")
        ld(self.esel.ap(), self.i_esel, "esel")
        ld(self.epsc.ap(), self.i_epsc, "epsc")

    def proj_chunk(self, c):
        o = self.op
        sl = c % 2
        t0 = c * CT
        o("Q0", lambda e: e.dma_start(out=self.xin[:, 0, :], in_=self.i_x[t0:t0 + CT, :]),
          reads=[], writes=["xin"], tag=f"ldx:{c}")
        wsl = c % self.NW
        for dt in range(NDT):
            ps = [self.psT0, self.psT1][dt % 2]
            pskey = f"psT{dt % 2}"
            o("T", lambda e, dt=dt, ps=ps: e.transpose(
                ps[:, 0:128], self.xin[:, 0, dt * 128:(dt + 1) * 128], self.eye[:]),
              reads=["xin", "eye", pskey + ":f"], writes=[pskey], tag=f"tr:{c}:{dt}")
            o("S", lambda e, dt=dt, ps=ps: e.copy(
                out=self.xw[:, wsl, dt, :], in_=ps[:, 0:128]),
              reads=[pskey], writes=[f"xw:{wsl}:{dt}", pskey + ":f"], tag=f"trev:{c}:{dt}")
        for et in range(NDT):
            ps = [self.psA, self.psB][et % 2]
            pk = ["psA", "psB"][et % 2]
            for kd in range(NDT):
                o("T", lambda e, et=et, kd=kd, ps=ps: e.matmul(
                    ps[:, 0:CT], self.WinT[:, kd, et * 128:(et + 1) * 128],
                    self.xw[:, wsl, kd, :], start=(kd == 0), stop=(kd == NDT - 1)),
                  reads=[f"xw:{wsl}:{kd}", "WinT"] + ([pk + ":f"] if kd == 0 else [pk]),
                  writes=[pk], tag=f"xp:{c}:{et}:{kd}")
            o("S", lambda e, et=et, ps=ps: e.activation(
                out=self.xp[:, 0, et, :], in_=ps[:, 0:CT], func=ACT.Silu),
              reads=[pk], writes=[f"xp:{et}", pk + ":f"], tag=f"xpev:{c}:{et}")
        projs = [("k", self.WkT, None), ("q", self.WqT, None),
                 ("v", self.WvT, None), ("b", self.WbT, "bias")]
        for nm, W, bias in projs:
            for ft in range(2):
                ps = [self.psA, self.psB][ft]
                pk = ["psA", "psB"][ft]
                for kd in range(NDT):
                    stp = (kd == NDT - 1 and bias is None)
                    o("T", lambda e, W=W, ft=ft, kd=kd, ps=ps, stp=stp: e.matmul(
                        ps[:, 0:CT], W[:, kd, ft * 128:(ft + 1) * 128],
                        self.xp[:, 0, kd, :], start=(kd == 0), stop=stp),
                      reads=[f"xp:{kd}", "W" + nm + "T"] + ([pk + ":f"] if kd == 0 else [pk]),
                      writes=[pk], tag=f"pj{nm}:{c}:{ft}:{kd}")
                if bias is not None:
                    o("T", lambda e, ft=ft, ps=ps: e.matmul(
                        ps[:, 0:CT], self.bbeta.ap()[0:1, ft * 128:(ft + 1) * 128],
                        self.onesrow.ap()[0:1, 0:CT], start=False, stop=True),
                      reads=["bbeta", "onesrow", pk], writes=[pk], tag=f"pjbias:{c}:{ft}")
                if nm == "k":
                    o("V", lambda e, ft=ft, ps=ps: e.tensor_copy(
                        self.k_sb[:, ft, :], ps[:, 0:CT]),
                      reads=[pk], writes=[f"ksb:{ft}", pk + ":f"], tag=f"kev:{c}:{ft}")
                elif nm == "q":
                    o("V", lambda e, ft=ft, ps=ps: e.tensor_copy(
                        self.q_sb[:, sl, ft, :], ps[:, 0:CT]),
                      reads=[pk], writes=[f"qsb:{sl}:{ft}", pk + ":f"], tag=f"qev:{c}:{ft}")
                elif nm == "v":
                    o("S", lambda e, ft=ft, ps=ps: e.copy(
                        out=self.v_fm[:, sl, ft, :], in_=ps[:, 0:CT]),
                      reads=[pk], writes=[f"vfm:{sl}:{ft}", pk + ":f"], tag=f"vev:{c}:{ft}")
                else:
                    o("S", lambda e, ft=ft, ps=ps: e.activation(
                        out=self.b_fm[:, sl, ft, :], in_=ps[:, 0:CT], func=ACT.Sigmoid),
                      reads=[pk], writes=[f"bfm:{sl}:{ft}", pk + ":f"], tag=f"bev:{c}:{ft}")
                    o("V", lambda e, ft=ft: e.reciprocal(
                        self.rb_fm[:, sl, ft, :], self.b_fm[:, sl, ft, :]),
                      reads=[f"bfm:{sl}:{ft}"], writes=[f"rbfm:{sl}:{ft}"], tag=f"rbev:{c}:{ft}")
        for ft in range(2):
            o("S", lambda e, ft=ft: e.activation(
                out=self.ksq[:, ft, :], in_=self.k_sb[:, ft, :], func=ACT.Square),
              reads=[f"ksb:{ft}"], writes=[f"ksq:{ft}"], tag=f"ksq:{c}:{ft}")
        for ft in range(2):
            o("T", lambda e, ft=ft: e.matmul(
                self.psT1[0:8, 0:CT], self.bones[ft][:],
                self.ksq[:, ft, :], start=(ft == 0), stop=(ft == 1)),
              reads=[f"ksq:{ft}", f"bones:{ft}"] + (["psT1:f"] if ft == 0 else ["psT1"]),
              writes=["psT1"], tag=f"knrm:{c}:{ft}")
        nr = self.nrm
        o("S", lambda e: e.activation(out=nr[:, 1, :], in_=self.psT1[0:8, 0:CT], func=ACT.Sqrt),
          reads=["psT1"], writes=["nrm1"], tag=f"knsqrt:{c}")
        o("V", lambda e: e.tensor_copy(nr[:, 0, :], self.psT1[0:8, 0:CT]),
          reads=["psT1"], writes=["nrm0", "psT1:f"], tag=f"knss:{c}")
        o("V", lambda e: e.reciprocal(nr[:, 2, :], nr[:, 1, :]),
          reads=["nrm1"], writes=["nrm2"], tag=f"knr1:{c}")
        o("V", lambda e: e.tensor_tensor(nr[:, 2, :], nr[:, 0, :], nr[:, 2, :], A.mult),
          reads=["nrm0", "nrm2"], writes=["nrm2"], tag=f"knt:{c}")
        o("V", lambda e: e.tensor_tensor(nr[:, 0, :], nr[:, 1, :], nr[:, 2, :], A.add),
          reads=["nrm1", "nrm2"], writes=["nrm0"], tag=f"kns1:{c}")
        o("V", lambda e: e.tensor_scalar(
            out=nr[:, 0, :], in0=nr[:, 0, :], scalar1=0.5, scalar2=1e-6, op0=A.mult, op1=A.add),
          reads=["nrm0"], writes=["nrm0"], tag=f"kns2:{c}")
        o("V", lambda e: e.reciprocal(nr[:, 2, :], nr[:, 0, :]),
          reads=["nrm0"], writes=["nrm5"], tag=f"knr2:{c}")
        for ft in range(2):
            o("T", lambda e, ft=ft: e.matmul(
                self.psT1[:, 0:CT], self.selk[ft][:],
                nr[:, 2, :], start=True, stop=True),
              reads=["nrm5", f"selk:{ft}", "psT1:f"], writes=["psT1"], tag=f"knbc:{c}:{ft}")
            o("V", lambda e, ft=ft: e.tensor_tensor(
                self.kn_fm[:, sl, ft, :], self.k_sb[:, ft, :], self.psT1[:, 0:CT], A.mult),
              reads=["psT1", f"ksb:{ft}"], writes=[f"knfm:{sl}:{ft}", "psT1:f"],
              tag=f"knmul:{c}:{ft}")

    def scan_chunk(self, c):
        o = self.op
        sl = c % 2
        NWIN = CT // TB
        NSG = CT // TS
        def rsh(sg):
            ss = sg % 2
            for kq, (src_t, skey) in enumerate(
                    [(self.kn_fm, "knfm"), (self.q_sb, "qsb")]):
                for ft in range(2):
                    o("Q0", lambda e, kq=kq, ft=ft, src_t=src_t, ss=ss, sg=sg: e.dma_start(
                        out=self.knq8[ft * 4:(ft + 1) * 4, ss, kq, :, :],
                        in_=src_t[:, sl, ft, sg * TS:(sg + 1) * TS]),
                      reads=[f"{skey}:{sl}:{ft}"], writes=[f"knq8:{ss}"],
                      tag=f"rsh:{c}:{sg}:{kq}:{ft}")
        def prep(w):
            sg = w // (TS // TB)
            wl = w % (TS // TB)
            ss = sg % 2
            wp = w % 2
            for hf in range(2):
                ps = self.psW[hf][wp]
                pk = f"psW{hf}{wp}"
                o("T", lambda e, hf=hf, ps=ps, wl=wl, ss=ss: e.matmul(
                    ps[:, 0:512], self.sel[hf][:],
                    self.knq8[0:8, ss, :, :, wl * TB:(wl + 1) * TB],
                    start=True, stop=True),
                  reads=[f"knq8:{ss}", f"sel:{hf}", pk + ":f"], writes=[pk],
                  tag=f"wbc:{c}:{w}:{hf}")
                o("S", lambda e, hf=hf, ps=ps, wp=wp: e.copy(
                    out=self.bcw[:, hf, wp, :, :, :],
                    in_=ps[:, 0:512].rearrange("p (kq j t) -> p kq j t", kq=2, j=32).transpose([0, 1, 3, 2])),
                  reads=[pk], writes=[f"bcw:{hf}:{wp}", pk + ":f"],
                  tag=f"wbcev:{c}:{w}:{hf}")
        rsh(0)
        prep(0)
        for sg in range(NSG):
            if sg + 1 < NSG:
                rsh(sg + 1)
            for wl in range(TS // TB):
                w = sg * (TS // TB) + wl
                if w + 1 < NWIN:
                    prep(w + 1)
                W = c * NWIN + w
                wp = W % 2
                wr = W % RW
                u0 = w * TB
                for t in range(TB):
                    u = u0 + t
                    for hf in range(2):
                        if W == 0 and t == 0 and c == 0:
                            prev = self.zeroS[:, :]
                            prevkey = "zeroS"
                        elif t == 0:
                            prev = self.hist[:, hf, (W - 1) % RW, TB - 1, :]
                            prevkey = f"hist:{hf}:{(W - 1) % RW}"
                        else:
                            prev = self.hist[:, hf, wr, t - 1, :]
                            prevkey = f"hist:{hf}:{wr}"
                        knsl = self.bcw[:, hf, wp, 0, t, :]
                        vcol = self.v_fm[:, sl, hf, u:u + 1]
                        rbcol = self.rb_fm[:, sl, hf, u:u + 1]
                        bcol = self.b_fm[:, sl, hf, u:u + 1]
                        o("V", lambda e, hf=hf, prev=prev, knsl=knsl: e.scalar_tensor_tensor(
                            out=self.m_scr[:, hf, :], in0=prev, scalar=-1.0, in1=knsl,
                            op0=A.mult, op1=A.mult, accum_out=self.dcol[:, hf, :]),
                          reads=[prevkey, f"bcw:{hf}:{wp}", f"mscr:{hf}:f"],
                          writes=[f"mscr:{hf}", f"dcol:{hf}"], tag=f"ttr:{u}:{hf}")
                        o("V", lambda e, hf=hf, vcol=vcol, rbcol=rbcol: e.tensor_scalar(
                            out=self.gcol[:, hf, :], in0=self.dcol[:, hf, :],
                            scalar1=vcol, scalar2=rbcol, op0=A.add, op1=A.mult),
                          reads=[f"dcol:{hf}", f"vfm:{sl}:{hf}", f"rbfm:{sl}:{hf}"],
                          writes=[f"gcol:{hf}"], tag=f"gc:{u}:{hf}")
                        o("V", lambda e, hf=hf, prev=prev, knsl=knsl: e.scalar_tensor_tensor(
                            out=self.inner[:, hf, :], in0=knsl, scalar=self.gcol[:, hf, :],
                            in1=prev, op0=A.mult, op1=A.add),
                          reads=[prevkey, f"bcw:{hf}:{wp}", f"gcol:{hf}"],
                          writes=[f"inner:{hf}"], tag=f"stt:{u}:{hf}")
                        o("S", lambda e, hf=hf, wr=wr, t=t, bcol=bcol: e.activation(
                            out=self.hist[:, hf, wr, t, :], in_=self.inner[:, hf, :],
                            func=ACT.Tanh, scale=bcol),
                          reads=[f"inner:{hf}", f"bfm:{sl}:{hf}"],
                          writes=[f"hist:{hf}:{wr}", f"mscr:{hf}:f"], tag=f"tanh:{u}:{hf}")
                for hf in range(2):
                    o("G", lambda e, hf=hf, wr=wr, wp=wp: e.tensor_tensor(
                        self.gq_scr[:, 0, :, :], self.hist[:, hf, wr, :, :],
                        self.bcw[:, hf, wp, 1, :, :], A.mult),
                      reads=[f"hist:{hf}:{wr}", f"bcw:{hf}:{wp}", "gqf"],
                      writes=["gq"], tag=f"mq:{c}:{w}:{hf}")
                    o("V", lambda e, hf=hf, u0=u0: e.tensor_reduce(
                        out=self.Sq[:, 0, hf, u0:u0 + TB],
                        in_=self.gq_scr[:, 0, :, :],
                        axis=AX.X, op=A.add),
                      reads=["gq"], writes=[f"Sq:{hf}", "gqf"], tag=f"sqr:{c}:{w}:{hf}")

    def wout_chunk(self, c):
        o = self.op
        sl = c % 2
        wsl = c % self.NW
        for hf in range(2):
            o("S", lambda e, hf=hf: e.activation(
                out=self.cell[:, sl, hf, :], in_=self.Sq[:, 0, hf, :], func=ACT.Sigmoid),
              reads=[f"Sq:{hf}"], writes=[f"cell:{sl}:{hf}"], tag=f"sgm:{c}:{hf}")
            o("V", lambda e, hf=hf: e.tensor_tensor(
                self.cell[:, sl, hf, :], self.cell[:, sl, hf, :], self.Sq[:, 0, hf, :], A.mult),
              reads=[f"Sq:{hf}", f"cell:{sl}:{hf}"], writes=[f"cell:{sl}:{hf}"],
              tag=f"cl1:{c}:{hf}")
            o("V", lambda e, hf=hf: e.tensor_tensor(
                self.cell[:, sl, hf, :], self.cell[:, sl, hf, :], self.Sq[:, 0, hf, :], A.mult),
              reads=[f"Sq:{hf}", f"cell:{sl}:{hf}"], writes=[f"cell:{sl}:{hf}"],
              tag=f"cl2:{c}:{hf}")
        for dt in range(NDT):
            ps = [self.psT0, self.psT1][dt % 2]
            pk = ["psT0", "psT1"][dt % 2]
            for mt in range(2):
                o("T", lambda e, dt=dt, mt=mt, ps=ps: e.matmul(
                    ps[:, 0:CT], self.WoT[:, mt, dt * 128:(dt + 1) * 128],
                    self.cell[:, sl, mt, :], start=(mt == 0), stop=(mt == 1)),
                  reads=[f"cell:{sl}:{mt}", "WoT"] + ([pk + ":f"] if mt == 0 else [pk]),
                  writes=[pk], tag=f"wo:{c}:{dt}:{mt}")
            o("V", lambda e, dt=dt, ps=ps: e.tensor_tensor(
                self.s_fm[:, dt, c * CT:(c + 1) * CT], self.xw[:, wsl, dt, :],
                ps[:, 0:CT], A.add),
              reads=[pk, f"xw:{wsl}:{dt}"], writes=[f"s:{c}:{dt}", pk + ":f"],
              tag=f"woev:{c}:{dt}")

    def dump_sfm(self):
        o = self.op
        for dt in range(NDT):
            o("Q0", lambda e, dt=dt: e.dma_start(
                out=self.o_out[0:self.T, dt * 128:(dt + 1) * 128].transpose([1, 0]),
                in_=self.s_fm[:, dt, :], allow_slow_non_contiguous=True),
              reads=[f"s:{c}:{dt}" for c in range(self.NC)], writes=[f"outdma:{dt % (self.T // 128)}"],
              tag=f"dump:{dt}")

    def terminal(self):
        o = self.op
        keys = [f"outdma:{c}" for c in range(self.T // 128)]
        o("G", lambda e: e.nop(), reads=keys, tag="fin")

    def build_attn(self):
        self.load_weights()
        self.proj_chunk(0)
        for c in range(1, self.NC):
            self.proj_chunk(c)
            self.scan_chunk(c - 1)
            self.wout_chunk(c - 1)
        self.scan_chunk(self.NC - 1)
        self.wout_chunk(self.NC - 1)

    def finish(self):
        self.sch.finalize()
        sems = {}
        for k in _INC:
            if k in ("Q0", "Q1", "QG"):
                for ln in range(NLANES):
                    sems[(k, ln)] = self.nc.alloc_semaphore(f"sem{k}_{ln}")
            else:
                sems[k] = self.nc.alloc_semaphore(f"sem{k}")
        self.sch.emit(self.nc, sems)

    def alloc_ffn(self):
        nc = self.nc
        def addr(t):
            return nc.lookup_mloc(t).addr
        def at(name, shape, base, off):
            return nc.alloc_sbuf_tensor_at(name, shape, F32, offset=base + off)
        baseA = addr(self.WinT)
        endA = addr(self.WoT) + 2 * D * 4
        baseB = addr(self.xin)
        endB = addr(self.cell) + 2 * 2 * CT * 4
        T_ = self.T
        assert endA - baseA >= NDT * T_ * 4 + 8192, (baseA, endA)
        self.acc = at("acc_fm", [128, NDT, T_], baseA, 0)
        offA = NDT * T_ * 4
        self.s1tm = at("s1tm", [128, D], baseA, offA); offA += D * 4
        self.o1tm = at("o1tm", [128, D], baseA, offA); offA += D * 4
        self.lncA = at("lncA", [128, 12], baseA, offA); offA += 64
        offB = 0
        def atB(name, shape, nbytes):
            nonlocal offB
            t = at(name, shape, baseB, offB)
            offB += nbytes
            return t
        self.wf1p = atB("wf1p", [128, 2, 2, NDT, 128], 2 * 2 * NDT * 128 * 4)
        self.wf1g = atB("wf1g", [128, 2, 2, NDT, 128], 2 * 2 * NDT * 128 * 4)
        self.wf2 = atB("wf2", [128, 2, 2, NDT, 128], 2 * 2 * NDT * 128 * 4)
        self.gfm = atB("gfm", [8, T_], T_ * 4)
        self.gbc = atB("gbc", [128, 512], 512 * 4)
        self.mgrp = atB("mgrp", [128, 4, SLAB], 4 * SLAB * 4)
        self.hgt = atB("hgt", [128, SLAB], SLAB * 4)
        self.lgt = atB("lgt", [128, 8], 8 * 4)
        self.gtm = atB("gtm", [128, 8], 8 * 4)
        self.rsc = atB("rsc", [128, 16], 16 * 4)
        offC = 0
        baseC = addr(self.wf1p)
        def atC(name, shape, nbytes):
            nonlocal offC
            t = at(name, shape, baseC, offC)
            offC += nbytes
            return t
        self.s2sb = atC("s2sb", [128, NDT, 128], NDT * 128 * 4)
        self.s2tm = atC("s2tm", [128, 2, D], 2 * D * 4)
        self.otm = atC("otm", [128, 2, D], 2 * D * 4)
        self.scr2 = atC("scr2", [128, D], D * 4)
        self.lnc = atC("lnc", [128, 12], 12 * 4)

    def barrier(self, name, keys):
        self.op("V", lambda e: e.engine_nop(), reads=keys, writes=[name], tag=f"bar:{name}")

    def ln1(self):
        o = self.op
        T_ = self.T
        for c in range(T_ // 128):
            for dt in range(NDT):
                ps = [self.psT0, self.psT1][dt % 2]
                pk = ["psT0", "psT1"][dt % 2]
                o("T", lambda e, c=c, dt=dt, ps=ps: e.transpose(
                    ps[:, 0:128], self.s_fm[:, dt, c * 128:(c + 1) * 128], self.eye[:]),
                  reads=[f"s:{c}:{dt}", "eye", pk + ":f"], writes=[pk], tag=f"l1tr:{c}:{dt}")
                o("S", lambda e, dt=dt, ps=ps: e.copy(
                    out=self.s1tm[:, dt * 128:(dt + 1) * 128], in_=ps[:, 0:128]),
                  reads=[pk, "PHASE2"], writes=["s1tm"], tag=f"l1trev:{c}:{dt}")
            L = self.lncA
            o("S", lambda e: e.activation(
                out=self.o1tm[:, :], in_=self.s1tm[:, :], func=ACT.Copy,
                accum_out=L[:, 0:1]),
              reads=["s1tm", "PHASE2"], writes=["o1tm", "la0"], tag=f"l1sum:{c}")
            o("S", lambda e: e.activation(
                out=self.o1tm[:, :], in_=self.s1tm[:, :], func=ACT.Square,
                accum_out=L[:, 1:2]),
              reads=["s1tm"], writes=["o1tm", "la1"], tag=f"l1sq:{c}")
            sc = 1.0 / D
            o("V", lambda e: e.tensor_scalar(out=L[:, 2:3], in0=L[:, 0:1], scalar1=sc,
                scalar2=None, op0=A.mult),
              reads=["la0"], writes=["la2"], tag=f"l1mu:{c}")
            o("V", lambda e: e.tensor_scalar(out=L[:, 3:4], in0=L[:, 1:2], scalar1=sc,
                scalar2=None, op0=A.mult),
              reads=["la1"], writes=["la3"], tag=f"l1ex2:{c}")
            o("V", lambda e: e.tensor_tensor(L[:, 4:5], L[:, 2:3], L[:, 2:3], A.mult),
              reads=["la2"], writes=["la4"], tag=f"l1m2:{c}")
            o("V", lambda e: e.tensor_tensor(L[:, 4:5], L[:, 3:4], L[:, 4:5], A.subtract),
              reads=["la3", "la4"], writes=["la4"], tag=f"l1var:{c}")
            self.rstd_cols(L, f"l1rs:{c}", 4, 5, eps=1e-5, pre="la")
            o("V", lambda e: e.tensor_scalar(
                out=self.o1tm[:, :], in0=self.s1tm[:, :],
                scalar1=L[:, 2:3], scalar2=L[:, 5:6], op0=A.subtract, op1=A.mult),
              reads=["s1tm", "la2", "la5"], writes=["o1tm"], tag=f"l1nrm:{c}")
            for dt in range(NDT):
                ps = [self.psT0, self.psT1][dt % 2]
                pk = ["psT0", "psT1"][dt % 2]
                o("T", lambda e, dt=dt, ps=ps: e.transpose(
                    ps[:, 0:128], self.o1tm[:, dt * 128:(dt + 1) * 128], self.eye[:]),
                  reads=["o1tm", "eye", pk + ":f"], writes=[pk], tag=f"l1trb:{c}:{dt}")
                o("V", lambda e, c=c, dt=dt, ps=ps: e.tensor_copy(
                    self.s_fm[:, dt, c * 128:(c + 1) * 128], ps[:, 0:128]),
                  reads=[pk], writes=[f"s:{c}:{dt}", pk + ":f"], tag=f"l1trbe:{c}:{dt}")

    def router(self):
        o = self.op
        T_ = self.T
        nt = T_ // 128
        for tt in range(nt):
            cc = tt
            for kd in range(NDT):
                o("T", lambda e, tt=tt, kd=kd: e.matmul(
                    self.psB[:, 0:8], self.s_fm[:, kd, tt * 128:(tt + 1) * 128],
                    self.wrout[:, kd, :], start=(kd == 0), stop=(kd == NDT - 1)),
                  reads=[f"s:{cc}:{kd}", "wrout"] + (["psB:f"] if kd == 0 else ["psB"]),
                  writes=["psB"], tag=f"rt:{tt}:{kd}")
            o("V", lambda e, tt=tt: e.tensor_copy(self.lgt[:, :], self.psB[:, 0:8]),
              reads=["psB"], writes=["lgt", "psB:f"], tag=f"rtev:{tt}")
            R = self.rsc
            o("V", lambda e, tt=tt: e.max(out=R[:, 0:8], in_=self.lgt[:, :]),
              reads=["lgt"], writes=["rsc8"], tag=f"rtmax:{tt}")
            o("V", lambda e: e.tensor_tensor(R[:, 8:9], R[:, 0:1], R[:, 1:2], A.subtract),
              reads=["rsc8"], writes=["rscd"], tag=f"rtd:{tt}")
            o("S", lambda e: e.activation(out=R[:, 9:10], in_=R[:, 8:9], func=ACT.Sigmoid),
              reads=["rscd"], writes=["rscs1"], tag=f"rts1:{tt}")
            o("S", lambda e: e.activation(out=R[:, 10:11], in_=R[:, 8:9], func=ACT.Sigmoid,
                                          scale=-1.0),
              reads=["rscd"], writes=["rscs2"], tag=f"rts2:{tt}")
            o("V", lambda e, tt=tt: e.tensor_tensor(
                self.gtm[:, :], self.lgt[:, :],
                R[:, 0:1].to_broadcast([128, 8]), A.is_equal),
              reads=["lgt", "rsc8"], writes=["gtm"], tag=f"rte1:{tt}")
            o("V", lambda e, tt=tt: e.tensor_scalar(
                out=self.gtm[:, :], in0=self.gtm[:, :], scalar1=R[:, 9:10],
                scalar2=None, op0=A.mult),
              reads=["gtm", "rscs1"], writes=["gtm"], tag=f"rtg1:{tt}")
            o("V", lambda e, tt=tt: e.tensor_tensor(
                R[:, 0:8], self.lgt[:, :],
                R[:, 1:2].to_broadcast([128, 8]), A.is_equal),
              reads=["lgt", "rsc8"], writes=["rsc8b"], tag=f"rte2:{tt}")
            o("V", lambda e: e.tensor_scalar(
                out=R[:, 0:8], in0=R[:, 0:8], scalar1=R[:, 10:11],
                scalar2=None, op0=A.mult),
              reads=["rsc8b", "rscs2"], writes=["rsc8b"], tag=f"rtg2:{tt}")
            o("V", lambda e, tt=tt: e.tensor_tensor(
                self.gtm[:, :], self.gtm[:, :], R[:, 0:8], A.add),
              reads=["gtm", "rsc8b"], writes=["gtm"], tag=f"rtg3:{tt}")
            o("T", lambda e, tt=tt: e.transpose(
                self.psB[0:8, 0:128], self.gtm[:, :], self.eye[:]),
              reads=["gtm", "eye", "psB:f"], writes=["psB"], tag=f"rttr:{tt}")
            o("V", lambda e, tt=tt: e.tensor_copy(
                self.gfm.ap()[0:8, tt * 128:(tt + 1) * 128], self.psB[0:8, 0:128]),
              reads=["psB"], writes=[f"gfm:{tt}", "psB:f"], tag=f"rttrev:{tt}")

    def moe(self):
        o = self.op
        T_ = self.T
        SL = min(SLAB, T_)
        nsl = T_ // SL
        nt = T_ // 128
        for dt in range(NDT):
            o("G", lambda e, dt=dt: e.memset(self.acc[:, dt, :], 0.0),
              reads=["PHASE2"], writes=[f"acc:{dt}"], tag=f"accz:{dt}")
        for e_i in range(E + 1):
            shared = (e_i == E)
            KI = ISH // 128 if shared else D // 128
            if not shared:
                pass
            for g4 in range(KI // 4):
                for half in range(2):
                    slot = half
                    for kk in range(2):
                        ki = g4 * 4 + half * 2 + kk
                        if shared:
                            p_src = self.i_gate[:, ki * 128:(ki + 1) * 128]
                            g_src = self.i_up[:, ki * 128:(ki + 1) * 128]
                            f2_src = self.i_down[ki * 128:(ki + 1) * 128, :]
                        else:
                            p_src = self.i_fc1[e_i, :, ki * 128:(ki + 1) * 128]
                            g_src = self.i_fc1[e_i, :, D + ki * 128:D + (ki + 1) * 128]
                            f2_src = self.i_fc2[e_i, ki * 128:(ki + 1) * 128, :]
                        o("Q0", lambda e, p_src=p_src, slot=slot, kk=kk: e.dma_start(
                            out=self.wf1p[:, slot, kk, :, :],
                            in_=p_src.rearrange("(kt p) i -> p kt i", p=128)),
                          reads=["PHASE2"], writes=[f"wf1p:{slot}:{kk}"], tag=f"w1p:{e_i}:{ki}")
                        o("Q0", lambda e, g_src=g_src, slot=slot, kk=kk: e.dma_start(
                            out=self.wf1g[:, slot, kk, :, :],
                            in_=g_src.rearrange("(kt p) i -> p kt i", p=128)),
                          reads=["PHASE2"], writes=[f"wf1g:{slot}:{kk}"], tag=f"w1g:{e_i}:{ki}")
                        o("Q1", lambda e, f2_src=f2_src, slot=slot, kk=kk: e.dma_start(
                            out=self.wf2[:, slot, kk, :, :],
                            in_=f2_src.rearrange("p (dt n) -> p dt n", dt=NDT)),
                          reads=["PHASE2"], writes=[f"wf2:{slot}:{kk}"], tag=f"w2:{e_i}:{ki}")
                for sl_i in range(nsl):
                    s0 = sl_i * SL
                    cks = [c for c in range(T_ // CT) if c * CT >= s0 and c * CT < s0 + SL]
                    xkeys = [f"s:{c}:{kd}" for c in cks for kd in range(NDT)]
                    LS = min(512, T_)
                    if (not shared) and s0 % LS == 0:
                        gb0 = s0
                        o("T", lambda e, e_i=e_i, gb0=gb0, LS=LS: e.matmul(
                            self.psT0[:, 0:LS], self.esel.ap()[:, e_i * 128:(e_i + 1) * 128],
                            self.gfm.ap()[:, gb0:gb0 + LS], start=True, stop=True),
                          reads=[f"gfm:{tt}" for tt in range(nt)] + ["esel", "psT0:f"],
                          writes=["psT0"], tag=f"gbc:{e_i}:{g4}:{gb0}")
                        o("S", lambda e, LS=LS: e.copy(
                            out=self.gbc[:, 0:LS], in_=self.psT0[:, 0:LS]),
                          reads=["psT0", "PHASE2"], writes=["gbc", "psT0:f"],
                          tag=f"gbcev:{e_i}:{g4}:{gb0}")
                    for kq in range(4):
                        slot, kk = kq // 2, kq % 2
                        for kd in range(NDT):
                            o("T", lambda e, slot=slot, kk=kk, kd=kd, s0=s0: e.matmul(
                                self.psA[:, 0:SL], self.wf1p[:, slot, kk, kd, :],
                                self.s_fm[:, kd, s0:s0 + SL],
                                start=(kd == 0), stop=(kd == NDT - 1)),
                              reads=xkeys + [f"wf1p:{slot}:{kk}"] + (["psA:f"] if kd == 0 else ["psA"]),
                              writes=["psA"], tag=f"f1p:{e_i}:{g4}:{sl_i}:{kq}:{kd}")
                            o("T", lambda e, slot=slot, kk=kk, kd=kd, s0=s0: e.matmul(
                                self.psB[:, 0:SL], self.wf1g[:, slot, kk, kd, :],
                                self.s_fm[:, kd, s0:s0 + SL],
                                start=(kd == 0), stop=(kd == NDT - 1)),
                              reads=xkeys + [f"wf1g:{slot}:{kk}"] + (["psB:f"] if kd == 0 else ["psB"]),
                              writes=["psB"], tag=f"f1g:{e_i}:{g4}:{sl_i}:{kq}:{kd}")
                        o("S", lambda e, kq=kq: e.activation(out=self.mgrp[:, kq, 0:SL],
                            in_=self.psA[:, 0:SL], func=ACT.Silu),
                          reads=["psA"], writes=[f"m:{kq}", "psA:f"], tag=f"hp:{e_i}:{g4}:{sl_i}:{kq}")
                        if shared:
                            o("V", lambda e: e.tensor_copy(self.hgt[:, 0:SL], self.psB[:, 0:SL]),
                              reads=["psB"], writes=["hgt", "psB:f"], tag=f"hg:{e_i}:{g4}:{sl_i}:{kq}")
                        else:
                            o("V", lambda e, s0=s0: e.tensor_tensor(
                                self.hgt[:, 0:SL], self.psB[:, 0:SL],
                                self.gbc[:, s0 % 512:s0 % 512 + SL], A.mult),
                              reads=["psB", "gbc"], writes=["hgt", "psB:f"],
                              tag=f"hg:{e_i}:{g4}:{sl_i}:{kq}")
                        o("V", lambda e, kq=kq: e.tensor_tensor(
                            self.mgrp[:, kq, 0:SL], self.mgrp[:, kq, 0:SL], self.hgt[:, 0:SL], A.mult),
                          reads=[f"m:{kq}", "hgt"], writes=[f"m:{kq}"], tag=f"m:{e_i}:{g4}:{sl_i}:{kq}")
                    for dt in range(NDT):
                        ps = self.psW[dt % 2][(dt // 2) % 2]
                        pk = f"psW{dt % 2}{(dt // 2) % 2}"
                        for kq in range(4):
                            slot, kk = kq // 2, kq % 2
                            o("T", lambda e, slot=slot, kk=kk, dt=dt, kq=kq, ps=ps: e.matmul(
                                ps[:, 0:SL], self.wf2[:, slot, kk, dt, :],
                                self.mgrp[:, kq, 0:SL],
                                start=(kq == 0), stop=(kq == 3)),
                              reads=[f"m:{kq}", f"wf2:{slot}:{kk}"] + ([pk + ":f"] if kq == 0 else [pk]),
                              writes=[pk], tag=f"f2:{e_i}:{g4}:{sl_i}:{dt}:{kq}")
                        o("V", lambda e, dt=dt, s0=s0, ps=ps: e.tensor_tensor(
                            self.acc[:, dt, s0:s0 + SL], self.acc[:, dt, s0:s0 + SL],
                            ps[:, 0:SL], A.add),
                          reads=[pk, f"acc:{dt}"], writes=[f"acc:{dt}", pk + ":f"],
                          tag=f"f2ev:{e_i}:{g4}:{sl_i}:{dt}")

    def ln2_out(self):
        o = self.op
        T_ = self.T
        for c in range(T_ // 128):
            sl = c % 2
            for dt in range(NDT):
                o("V", lambda e, c=c, dt=dt: e.tensor_tensor(
                    self.s2sb[:, dt, :], self.s_fm[:, dt, c * 128:(c + 1) * 128],
                    self.acc[:, dt, c * 128:(c + 1) * 128], A.add),
                  reads=[f"s:{c}:{dt}", f"acc:{dt}", "PHASE3"],
                  writes=[f"s2:{dt}"], tag=f"o_add:{c}:{dt}")
                ps = [self.psT0, self.psT1][dt % 2]
                pk = ["psT0", "psT1"][dt % 2]
                o("T", lambda e, dt=dt, ps=ps: e.transpose(
                    ps[:, 0:128], self.s2sb[:, dt, :], self.eye[:]),
                  reads=[f"s2:{dt}", "eye", pk + ":f"], writes=[pk], tag=f"o_tr:{c}:{dt}")
                o("S", lambda e, dt=dt, sl=sl, ps=ps: e.copy(
                    out=self.s2tm[:, sl, dt * 128:(dt + 1) * 128], in_=ps[:, 0:128]),
                  reads=[pk, "PHASE3"], writes=[f"s2tm:{sl}", pk + ":f"], tag=f"o_trev:{c}:{dt}")
            L = self.lnc
            o("S", lambda e, sl=sl: e.activation(
                out=self.scr2[:, :], in_=self.s2tm[:, sl, :], func=ACT.Copy,
                accum_out=L[:, 0:1]),
              reads=[f"s2tm:{sl}", "PHASE3"], writes=["scr2", "lc0"], tag=f"o_sum:{c}")
            o("S", lambda e, sl=sl: e.activation(
                out=self.scr2[:, :], in_=self.s2tm[:, sl, :], func=ACT.Square,
                accum_out=L[:, 1:2]),
              reads=[f"s2tm:{sl}"], writes=["scr2", "lc1"], tag=f"o_sq:{c}")
            sc = 1.0 / D
            o("V", lambda e: e.tensor_scalar(out=L[:, 2:3], in0=L[:, 0:1], scalar1=sc,
                scalar2=None, op0=A.mult),
              reads=["lc0"], writes=["lc2"], tag=f"o_mu:{c}")
            o("V", lambda e: e.tensor_scalar(out=L[:, 3:4], in0=L[:, 1:2], scalar1=sc,
                scalar2=None, op0=A.mult),
              reads=["lc1"], writes=["lc3"], tag=f"o_ex2:{c}")
            o("V", lambda e: e.tensor_tensor(L[:, 4:5], L[:, 2:3], L[:, 2:3], A.mult),
              reads=["lc2"], writes=["lc4"], tag=f"o_m2:{c}")
            o("V", lambda e: e.tensor_tensor(L[:, 4:5], L[:, 3:4], L[:, 4:5], A.subtract),
              reads=["lc3", "lc4"], writes=["lc4"], tag=f"o_var:{c}")
            self.rstd_cols(L, f"o_rs:{c}", 4, 5, eps=1e-5)
            o("V", lambda e, sl=sl: e.tensor_scalar(
                out=self.otm[:, sl, :], in0=self.s2tm[:, sl, :],
                scalar1=L[:, 2:3], scalar2=L[:, 5:6], op0=A.subtract, op1=A.mult),
              reads=[f"s2tm:{sl}", "lc2", "lc5", "PHASE3"], writes=[f"otm:{sl}"],
              tag=f"o_nrm:{c}")
            o("Q0", lambda e, c=c, sl=sl: e.dma_start(
                out=self.o_out[c * 128:(c + 1) * 128, :], in_=self.otm[:, sl, :]),
              reads=[f"otm:{sl}"], writes=[f"outdma:{c}"], tag=f"o_dma:{c}")

    def rstd_cols(self, L, tag, vrow, orow, eps, pre="lc"):
        o = self.op
        o("S", lambda e: e.activation(out=L[:, 6:7], in_=L[:, vrow:vrow + 1],
                                      func=ACT.Sqrt, bias=self.epsc.ap()),
          reads=[f"{pre}{vrow}", "epsc"], writes=[pre + "6"], tag=f"{tag}:sqrt")
        o("V", lambda e: e.tensor_scalar(out=L[:, 7:8], in0=L[:, vrow:vrow + 1],
            scalar1=eps, scalar2=None, op0=A.add),
          reads=[f"{pre}{vrow}"], writes=[pre + "7"], tag=f"{tag}:va")
        o("V", lambda e: e.reciprocal(L[:, 8:9], L[:, 6:7]),
          reads=[pre + "6"], writes=[pre + "8"], tag=f"{tag}:r0")
        o("V", lambda e: e.tensor_tensor(L[:, 7:8], L[:, 7:8], L[:, 8:9], A.mult),
          reads=[pre + "7", pre + "8"], writes=[pre + "7"], tag=f"{tag}:t")
        o("V", lambda e: e.tensor_tensor(L[:, 7:8], L[:, 7:8], L[:, 6:7], A.add),
          reads=[pre + "7", pre + "6"], writes=[pre + "7"], tag=f"{tag}:s1")
        o("V", lambda e: e.tensor_scalar(out=L[:, 7:8], in0=L[:, 7:8], scalar1=0.5,
            scalar2=None, op0=A.mult),
          reads=[pre + "7"], writes=[pre + "7"], tag=f"{tag}:s1b")
        o("V", lambda e: e.reciprocal(L[:, orow:orow + 1], L[:, 7:8]),
          reads=[pre + "7"], writes=[f"{pre}{orow}"], tag=f"{tag}:rstd")

    def build_full(self):
        self.build_attn()
        self.alloc_ffn()
        skeys = [f"s:{c}:{dt}" for c in range(self.NC) for dt in range(NDT)]
        self.barrier("PHASE2", skeys)
        self.ln1()
        self.router()
        self.moe()
        akeys = [f"acc:{dt}" for dt in range(NDT)]
        self.barrier("PHASE3", akeys)
        self.ln2_out()
        self.terminal()
        self.finish()

def build_full_test(T):
    mk = MK(T)
    mk.build_full()
    return mk


def build_attn_test(T):
    mk = MK(T)
    mk.build_attn()
    mk.dump_sfm()
    mk.terminal()
    mk.finish()
    return mk

def kernel(**inputs):
    B, T = 8, 2048
    hp = host_prep(inputs, T)
    x = np.asarray(inputs["L0_attn_x"], dtype=np.float32)
    mk = MK(T)
    mk.build_full()
    in_maps = []
    for b in range(B):
        m = dict(hp)
        m["x"] = np.ascontiguousarray(x[b])
        in_maps.append(m)
    res = run_bass_kernel_spmd(mk.nc, in_maps, core_ids=list(range(B)))
    out = np.stack([np.asarray(r["out"]) for r in res.results], axis=0)
    return out.astype(np.float32)


# version 08
# speedup vs baseline: 1.3397x; 1.2170x over previous
import numpy as np
import concourse.bass as bass
import concourse.mybir as mybir
from concourse.bass_core_rust import IndirectOffsetOnAxis as IOA
from concourse.bass_utils import run_bass_kernel_spmd

F32 = mybir.dt.float32
A = mybir.AluOpType
ACT = mybir.ActivationFunctionType
AX = mybir.AxisListType


NLANES = 8

class Op:
    __slots__ = ("eng", "fn", "reads", "writes", "seq", "need_inc", "waits", "soft", "tag", "lane")
    def __init__(self, eng, fn, reads, writes, tag=""):
        self.eng = eng
        self.fn = fn
        self.reads = reads
        self.writes = writes
        self.seq = None
        self.need_inc = False
        self.waits = []
        self.soft = []
        self.tag = tag
        self.lane = 0

_INC = {"T": 1, "V": 1, "S": 1, "G": 1, "Q0": 16, "Q1": 16, "QG": 16}
_QENG = {"Q0": "sync", "Q1": "scalar", "QG": "gpsimd", "T": "tensor", "V": "vector",
         "S": "scalar", "G": "gpsimd"}

class Sched:
    def __init__(self):
        self.ops = []
        self.last_writer = {}
        self.readers = {}

    def add(self, eng, fn, reads=(), writes=(), tag=""):
        op = Op(eng, fn, list(reads), list(writes), tag)
        deps = []
        for r in op.reads:
            w = self.last_writer.get(r)
            if w is not None:
                deps.append(w)
        for r in op.writes:
            w = self.last_writer.get(r)
            if w is not None:
                deps.append(w)
            deps.extend(self.readers.get(r, ()))
        seen = set()
        for d in deps:
            if d is op or id(d) in seen:
                continue
            seen.add(id(d))
            if d.eng == op.eng and d.eng == "T":
                continue
            d.need_inc = True
            op.waits.append(d)
        for r in op.reads:
            self.readers.setdefault(r, []).append(op)
        for r in op.writes:
            self.last_writer[r] = op
            self.readers[r] = []
        self.ops.append(op)
        return op

    def finalize(self):
        counters = {k: 0 for k in _INC}
        for op in self.ops:
            if op.eng in ("Q0", "Q1", "QG"):
                op.need_inc = True
                n = counters[op.eng]
                op.lane = n % NLANES
                op.seq = n // NLANES + 1
                counters[op.eng] = n + 1
            elif op.need_inc:
                counters[op.eng] += 1
                op.seq = counters[op.eng]
        self.total = dict(counters)

    def emit(self, nc, sems):
        per_eng = {}
        for op in self.ops:
            per_eng.setdefault(_QENG[op.eng], []).append(op)
        def semkey(op):
            return (op.eng, op.lane) if op.eng in ("Q0", "Q1", "QG") else op.eng
        with nc.Block() as block:
            def run(engine_obj, key):
                done = {}
                def wait(sk, tgt):
                    if done.get(sk, 0) < tgt:
                        engine_obj.wait_ge(sems[sk], tgt)
                        done[sk] = tgt
                for op in per_eng.get(key, ()):
                    for d in op.waits:
                        wait(semkey(d), d.seq * _INC[d.eng])
                    for d in op.soft:
                        if d.need_inc:
                            wait(semkey(d), d.seq * _INC[d.eng])
                    if op.eng in ("Q0", "Q1", "QG") and op.seq > 1:
                        wait(semkey(op), (op.seq - 1) * 16)
                    ins = op.fn(engine_obj)
                    if op.need_inc:
                        assert ins is not None, f"op {op.tag} needs inc but fn returned None"
                        ins.then_inc(sems[semkey(op)], _INC[op.eng])
            @block.sync
            def _(e):
                run(e, "sync")
            @block.tensor
            def _(e):
                run(e, "tensor")
            @block.vector
            def _(e):
                run(e, "vector")
            @block.scalar
            def _(e):
                run(e, "scalar")
            @block.gpsimd
            def _(e):
                run(e, "gpsimd")

D = 1024
HN = 8
NH = 32
M = 256
E = 8
ISH = 2048
NDT = 8
CT = 128
TS = 16
TB = 8
RW = 4
SLAB = 256
KG = 4

def perm256():
    p = np.zeros(256, dtype=np.int64)
    for h in range(HN):
        for i in range(NH):
            p[(h // 4) * 128 + i * 4 + (h % 4)] = h * NH + i
    return p

def host_prep(inputs, Tcore):
    P = perm256()
    f = np.ascontiguousarray
    d = {}
    d["WinT"] = f(inputs["L0_attn_W_in"].T)
    d["WkT"] = f(inputs["L0_attn_W_k"].T)
    d["WqT"] = f(inputs["L0_attn_W_q"].T)
    d["WvT"] = f(inputs["L0_attn_W_v"].T[:, P])
    d["WbT"] = f(inputs["L0_attn_W_beta"].T[:, P])
    d["bbeta"] = f(inputs["L0_attn_b_beta"].reshape(1, 256)[:, P])
    d["WoT"] = f(inputs["L0_attn_W_out"].T[P, :])
    d["wrout"] = f(inputs["L0_ffn_w_router"])
    d["fc1"] = f(inputs["L0_ffn_fc1_w"])
    d["fc2"] = f(inputs["L0_ffn_fc2_w"])
    d["gate"] = f(inputs["L0_ffn_gate_w"])
    d["up"] = f(inputs["L0_ffn_up_w"])
    d["down"] = f(inputs["L0_ffn_down_w"])
    d["eye"] = np.eye(128, dtype=np.float32)
    sel = np.zeros((2, 8, 128), dtype=np.float32)
    for hf in range(2):
        for pp in range(128):
            sel[hf, 4 * hf + (pp % 4), pp] = 1.0
    d["sel"] = f(sel.reshape(16, 128))
    selk = np.zeros((2, 8, 128), dtype=np.float32)
    for ft in range(2):
        for pp in range(128):
            selk[ft, ft * 4 + pp // 32, pp] = 1.0
    d["selk"] = f(selk.reshape(16, 128))
    bo = np.zeros((2, 128, 8), dtype=np.float32)
    for ft in range(2):
        for pp in range(128):
            bo[ft, pp, ft * 4 + pp // 32] = 1.0
    d["bones"] = f(bo.reshape(256, 8))
    d["ones128"] = np.ones((128, 1), dtype=np.float32)
    d["epsc"] = np.full((128, 1), 1e-5, dtype=np.float32)
    lt = np.tril(np.ones((128, 128), dtype=np.float32), k=-1)
    d["ltri"] = np.ascontiguousarray(lt)
    NT = Tcore // 128
    d["iotf"] = np.ascontiguousarray(
        ((np.arange(Tcore, dtype=np.float32) - Tcore).reshape(NT, 128).T))
    NSLAB = (2 * Tcore + 8 * 255 + 255) // 256
    d["thr"] = np.ascontiguousarray(
        (np.arange(NSLAB, dtype=np.float32) * 256).reshape(1, NSLAB))
    kdp = np.zeros((128, 8), dtype=np.float32)
    for p in range(128):
        for kd in range(8):
            kdp[p, kd] = 128 * kd + p
    d["kdp"] = kdp
    es = np.zeros((8, 8, 128), dtype=np.float32)
    for e in range(8):
        es[e, e, :] = 1.0
    d["esel"] = np.ascontiguousarray(es.transpose(1, 0, 2).reshape(8, 8 * 128))
    d["onesrow"] = np.ones((1, 128), dtype=np.float32)
    d["zeros"] = np.zeros((128, 32), dtype=np.float32)
    return d


class MK:
    def __init__(self, T, stop_after=None):
        self.T = T
        self.NC = T // CT
        self.stop_after = stop_after
        nc = bass.Bass("TRN2", target_bir_lowering=False)
        self.nc = nc
        self.sch = Sched()
        self.dbg = {}
        T_ = T

        def dram(name, shape, dtype=F32):
            return nc.dram_tensor(name, list(shape), dtype, kind="ExternalInput").ap()
        self.i_x = dram("x", (T_, D))
        self.i_WinT = dram("WinT", (D, D))
        self.i_WkT = dram("WkT", (D, M)); self.i_WqT = dram("WqT", (D, M))
        self.i_WvT = dram("WvT", (D, M)); self.i_WbT = dram("WbT", (D, M))
        self.i_bbeta = dram("bbeta", (1, M))
        self.i_WoT = dram("WoT", (M, D))
        self.i_wrout = dram("wrout", (D, E))
        self.i_fc1 = dram("fc1", (E, D, ISH))
        self.i_fc2 = dram("fc2", (E, D, D))
        self.i_gate = dram("gate", (D, ISH)); self.i_up = dram("up", (D, ISH))
        self.i_down = dram("down", (ISH, D))
        self.i_eye = dram("eye", (128, 128))
        self.i_sel = dram("sel", (16, 128)); self.i_selk = dram("selk", (16, 128))
        self.i_bones = dram("bones", (256, 8))
        self.i_ones128 = dram("ones128", (128, 1))
        self.i_onesrow = dram("onesrow", (1, 128))
        self.i_zeros = dram("zeros", (128, 32))
        self.i_esel = dram("esel", (8, 8 * 128))
        self.i_ltri = dram("ltri", (128, 128))
        self.i_iotf = dram("iotf", (128, T_ // 128))
        self.NSLAB = (2 * T_ + 8 * 255 + 255) // 256
        self.NS = 256 * self.NSLAB
        self.i_kdp = dram("kdp", (128, 8))
        self.i_thr = dram("thr", (1, self.NSLAB))
        self.i_epsc = dram("epsc", (128, 1))
        self.o_out = nc.dram_tensor("out", [T_, D], F32, kind="ExternalOutput").ap()
        self.x2tm = nc.dram_tensor("x2tm_i", [T_ + 128, D], F32, kind="Internal").ap()
        self.acctm = nc.dram_tensor("acctm_i", [T_ + 128, D], F32, kind="Internal").ap()
        NPAD = self.NS + 256
        self.NPAD = NPAD
        self.idxg = nc.dram_tensor("idxg_i", [NPAD, 2], F32, kind="Internal").ap()

        a = nc.alloc_sbuf_tensor
        self.s_fm = a("s_fm", [128, NDT, T_], F32)
        self.WinT = a("WinT_s", [128, NDT, D], F32)
        self.WkT = a("WkT_s", [128, NDT, M], F32)
        self.WqT = a("WqT_s", [128, NDT, M], F32)
        self.WvT = a("WvT_s", [128, NDT, M], F32)
        self.WbT = a("WbT_s", [128, NDT, M], F32)
        self.WoT = a("WoT_s", [128, 2, D], F32)
        self.bbeta = a("bbeta_s", [1, M], F32)
        self.wrout = a("wrout_s", [128, NDT, E], F32)
        self.eye = a("eye_s", [128, 128], F32)
        self.sel = [a("sel0_s", [8, 128], F32), a("sel1_s", [8, 128], F32)]
        self.selk = [a("selk0_s", [8, 128], F32), a("selk1_s", [8, 128], F32)]
        self.bones = [a("bones0_s", [128, 8], F32), a("bones1_s", [128, 8], F32)]
        self.ones128 = a("ones128_s", [128, 1], F32)
        self.onesrow = a("onesrow_s", [1, 128], F32)
        self.zeroS = a("zeroS_s", [128, 32], F32)
        self.esel = a("esel_s", [8, 8 * 128], F32)
        self.ltri = a("ltri_s", [128, 128], F32)
        self.iotf = a("iotf_s", [128, T_ // 128], F32)
        self.kdp = a("kdp_s", [128, 8], F32)
        self.thr = a("thr_s", [1, self.NSLAB], F32)
        self.epsc = a("epsc_s", [128, 1], F32)
        self.xin = a("xin", [128, 1, D], F32)
        NW = 3
        self.NW = NW
        self.xw = a("xw", [128, NW, NDT, CT], F32)
        self.xp = a("xp", [128, 1, NDT, CT], F32)
        self.k_sb = a("k_sb", [128, 2, CT], F32)
        self.ksq = a("ksq", [128, 2, CT], F32)
        self.q_sb = a("q_sb", [128, 2, 2, CT], F32)
        self.kn_fm = a("kn_fm", [128, 2, 2, CT], F32)
        self.v_fm = a("v_fm", [128, 2, 2, CT], F32)
        self.b_fm = a("b_fm", [128, 2, 2, CT], F32)
        self.rb_fm = a("rb_fm", [128, 2, 2, CT], F32)
        self.nrm = a("nrm", [8, 3, CT], F32)
        self.knq8 = a("knq8", [8, 2, 2, 32, TS], F32)
        self.bcw = a("bcw", [128, 2, 2, 2, TB, 32], F32)
        self.hist = a("hist", [128, 2, RW, TB, 32], F32)
        self.m_scr = a("m_scr", [128, 2, 32], F32)
        self.inner = a("inner", [128, 2, 32], F32)
        self.dcol = a("dcol", [128, 2, 1], F32)
        self.gcol = a("gcol", [128, 2, 1], F32)
        self.Sq = a("Sq", [128, 1, 2, CT], F32)
        self.gq_scr = a("gq_scr", [128, 1, TB, 32], F32)
        self.cell = a("cell", [128, 2, 2, CT], F32)
        p = nc.alloc_psum_tensor
        self.psT0 = p("psT0", [128, 512], F32)
        self.psT1 = p("psT1", [128, 512], F32)
        self.psA = p("psA", [128, 512], F32)
        self.psB = p("psB", [128, 512], F32)
        self.psW = [[p(f"psW{hf}{j}", [128, 512], F32) for j in range(2)] for hf in range(2)]

        self.sems = {}

    def op(self, eng, fn, reads=(), writes=(), tag=""):
        return self.sch.add(eng, fn, reads, writes, tag)

    def load_weights(self):
        o = self.op
        def ld(dst_ap, src_ap, key):
            o("Q0", lambda e, d=dst_ap, s=src_ap: e.dma_start(out=d, in_=s),
              writes=[key], tag=f"ld:{key}")
        ld(self.WinT.ap(), self.i_WinT.rearrange("(kt p) e -> p kt e", p=128), "WinT")
        ld(self.WkT.ap(), self.i_WkT.rearrange("(kt p) m -> p kt m", p=128), "WkT")
        ld(self.WqT.ap(), self.i_WqT.rearrange("(kt p) m -> p kt m", p=128), "WqT")
        ld(self.WvT.ap(), self.i_WvT.rearrange("(kt p) m -> p kt m", p=128), "WvT")
        ld(self.WbT.ap(), self.i_WbT.rearrange("(kt p) m -> p kt m", p=128), "WbT")
        ld(self.WoT.ap(), self.i_WoT.rearrange("(mt p) d -> p mt d", p=128), "WoT")
        ld(self.bbeta.ap(), self.i_bbeta, "bbeta")
        ld(self.wrout.ap(), self.i_wrout.rearrange("(kt p) e -> p kt e", p=128), "wrout")
        ld(self.eye.ap(), self.i_eye, "eye")
        ld(self.sel[0].ap(), self.i_sel[0:8, :], "sel:0")
        ld(self.sel[1].ap(), self.i_sel[8:16, :], "sel:1")
        ld(self.selk[0].ap(), self.i_selk[0:8, :], "selk:0")
        ld(self.selk[1].ap(), self.i_selk[8:16, :], "selk:1")
        ld(self.bones[0].ap(), self.i_bones[0:128, :], "bones:0")
        ld(self.bones[1].ap(), self.i_bones[128:256, :], "bones:1")
        ld(self.ones128.ap(), self.i_ones128, "ones128")
        ld(self.onesrow.ap(), self.i_onesrow, "onesrow")
        ld(self.zeroS.ap(), self.i_zeros, "zeroS")
        ld(self.esel.ap(), self.i_esel, "esel")
        ld(self.ltri.ap(), self.i_ltri, "ltri")
        ld(self.iotf.ap(), self.i_iotf, "iotf")
        ld(self.thr.ap(), self.i_thr, "thr")
        ld(self.kdp.ap(), self.i_kdp, "kdp")
        ld(self.epsc.ap(), self.i_epsc, "epsc")

    def proj_chunk(self, c):
        o = self.op
        sl = c % 2
        t0 = c * CT
        o("Q0", lambda e: e.dma_start(out=self.xin[:, 0, :], in_=self.i_x[t0:t0 + CT, :]),
          reads=[], writes=["xin"], tag=f"ldx:{c}")
        wsl = c % self.NW
        for dt in range(NDT):
            ps = [self.psT0, self.psT1][dt % 2]
            pskey = f"psT{dt % 2}"
            o("T", lambda e, dt=dt, ps=ps: e.transpose(
                ps[:, 0:128], self.xin[:, 0, dt * 128:(dt + 1) * 128], self.eye[:]),
              reads=["xin", "eye", pskey + ":f"], writes=[pskey], tag=f"tr:{c}:{dt}")
            o("S", lambda e, dt=dt, ps=ps: e.copy(
                out=self.xw[:, wsl, dt, :], in_=ps[:, 0:128]),
              reads=[pskey], writes=[f"xw:{wsl}:{dt}", pskey + ":f"], tag=f"trev:{c}:{dt}")
        for et in range(NDT):
            ps = [self.psA, self.psB][et % 2]
            pk = ["psA", "psB"][et % 2]
            for kd in range(NDT):
                o("T", lambda e, et=et, kd=kd, ps=ps: e.matmul(
                    ps[:, 0:CT], self.WinT[:, kd, et * 128:(et + 1) * 128],
                    self.xw[:, wsl, kd, :], start=(kd == 0), stop=(kd == NDT - 1)),
                  reads=[f"xw:{wsl}:{kd}", "WinT"] + ([pk + ":f"] if kd == 0 else [pk]),
                  writes=[pk], tag=f"xp:{c}:{et}:{kd}")
            o("S", lambda e, et=et, ps=ps: e.activation(
                out=self.xp[:, 0, et, :], in_=ps[:, 0:CT], func=ACT.Silu),
              reads=[pk], writes=[f"xp:{et}", pk + ":f"], tag=f"xpev:{c}:{et}")
        projs = [("k", self.WkT, None), ("q", self.WqT, None),
                 ("v", self.WvT, None), ("b", self.WbT, "bias")]
        for nm, W, bias in projs:
            for ft in range(2):
                ps = [self.psA, self.psB][ft]
                pk = ["psA", "psB"][ft]
                for kd in range(NDT):
                    stp = (kd == NDT - 1 and bias is None)
                    o("T", lambda e, W=W, ft=ft, kd=kd, ps=ps, stp=stp: e.matmul(
                        ps[:, 0:CT], W[:, kd, ft * 128:(ft + 1) * 128],
                        self.xp[:, 0, kd, :], start=(kd == 0), stop=stp),
                      reads=[f"xp:{kd}", "W" + nm + "T"] + ([pk + ":f"] if kd == 0 else [pk]),
                      writes=[pk], tag=f"pj{nm}:{c}:{ft}:{kd}")
                if bias is not None:
                    o("T", lambda e, ft=ft, ps=ps: e.matmul(
                        ps[:, 0:CT], self.bbeta.ap()[0:1, ft * 128:(ft + 1) * 128],
                        self.onesrow.ap()[0:1, 0:CT], start=False, stop=True),
                      reads=["bbeta", "onesrow", pk], writes=[pk], tag=f"pjbias:{c}:{ft}")
                if nm == "k":
                    o("V", lambda e, ft=ft, ps=ps: e.tensor_copy(
                        self.k_sb[:, ft, :], ps[:, 0:CT]),
                      reads=[pk], writes=[f"ksb:{ft}", pk + ":f"], tag=f"kev:{c}:{ft}")
                elif nm == "q":
                    o("V", lambda e, ft=ft, ps=ps: e.tensor_copy(
                        self.q_sb[:, sl, ft, :], ps[:, 0:CT]),
                      reads=[pk], writes=[f"qsb:{sl}:{ft}", pk + ":f"], tag=f"qev:{c}:{ft}")
                elif nm == "v":
                    o("S", lambda e, ft=ft, ps=ps: e.copy(
                        out=self.v_fm[:, sl, ft, :], in_=ps[:, 0:CT]),
                      reads=[pk], writes=[f"vfm:{sl}:{ft}", pk + ":f"], tag=f"vev:{c}:{ft}")
                else:
                    o("S", lambda e, ft=ft, ps=ps: e.activation(
                        out=self.b_fm[:, sl, ft, :], in_=ps[:, 0:CT], func=ACT.Sigmoid),
                      reads=[pk], writes=[f"bfm:{sl}:{ft}", pk + ":f"], tag=f"bev:{c}:{ft}")
                    o("V", lambda e, ft=ft: e.reciprocal(
                        self.rb_fm[:, sl, ft, :], self.b_fm[:, sl, ft, :]),
                      reads=[f"bfm:{sl}:{ft}"], writes=[f"rbfm:{sl}:{ft}"], tag=f"rbev:{c}:{ft}")
        for ft in range(2):
            o("S", lambda e, ft=ft: e.activation(
                out=self.ksq[:, ft, :], in_=self.k_sb[:, ft, :], func=ACT.Square),
              reads=[f"ksb:{ft}"], writes=[f"ksq:{ft}"], tag=f"ksq:{c}:{ft}")
        for ft in range(2):
            o("T", lambda e, ft=ft: e.matmul(
                self.psT1[0:8, 0:CT], self.bones[ft][:],
                self.ksq[:, ft, :], start=(ft == 0), stop=(ft == 1)),
              reads=[f"ksq:{ft}", f"bones:{ft}"] + (["psT1:f"] if ft == 0 else ["psT1"]),
              writes=["psT1"], tag=f"knrm:{c}:{ft}")
        nr = self.nrm
        o("S", lambda e: e.activation(out=nr[:, 1, :], in_=self.psT1[0:8, 0:CT], func=ACT.Sqrt),
          reads=["psT1"], writes=["nrm1"], tag=f"knsqrt:{c}")
        o("V", lambda e: e.tensor_copy(nr[:, 0, :], self.psT1[0:8, 0:CT]),
          reads=["psT1"], writes=["nrm0", "psT1:f"], tag=f"knss:{c}")
        o("V", lambda e: e.reciprocal(nr[:, 2, :], nr[:, 1, :]),
          reads=["nrm1"], writes=["nrm2"], tag=f"knr1:{c}")
        o("V", lambda e: e.tensor_tensor(nr[:, 2, :], nr[:, 0, :], nr[:, 2, :], A.mult),
          reads=["nrm0", "nrm2"], writes=["nrm2"], tag=f"knt:{c}")
        o("V", lambda e: e.tensor_tensor(nr[:, 0, :], nr[:, 1, :], nr[:, 2, :], A.add),
          reads=["nrm1", "nrm2"], writes=["nrm0"], tag=f"kns1:{c}")
        o("V", lambda e: e.tensor_scalar(
            out=nr[:, 0, :], in0=nr[:, 0, :], scalar1=0.5, scalar2=1e-6, op0=A.mult, op1=A.add),
          reads=["nrm0"], writes=["nrm0"], tag=f"kns2:{c}")
        o("V", lambda e: e.reciprocal(nr[:, 2, :], nr[:, 0, :]),
          reads=["nrm0"], writes=["nrm5"], tag=f"knr2:{c}")
        for ft in range(2):
            o("T", lambda e, ft=ft: e.matmul(
                self.psT1[:, 0:CT], self.selk[ft][:],
                nr[:, 2, :], start=True, stop=True),
              reads=["nrm5", f"selk:{ft}", "psT1:f"], writes=["psT1"], tag=f"knbc:{c}:{ft}")
            o("V", lambda e, ft=ft: e.tensor_tensor(
                self.kn_fm[:, sl, ft, :], self.k_sb[:, ft, :], self.psT1[:, 0:CT], A.mult),
              reads=["psT1", f"ksb:{ft}"], writes=[f"knfm:{sl}:{ft}", "psT1:f"],
              tag=f"knmul:{c}:{ft}")

    def scan_chunk(self, c):
        o = self.op
        sl = c % 2
        NWIN = CT // TB
        NSG = CT // TS
        def rsh(sg):
            ss = sg % 2
            for kq, (src_t, skey) in enumerate(
                    [(self.kn_fm, "knfm"), (self.q_sb, "qsb")]):
                for ft in range(2):
                    o("Q0", lambda e, kq=kq, ft=ft, src_t=src_t, ss=ss, sg=sg: e.dma_start(
                        out=self.knq8[ft * 4:(ft + 1) * 4, ss, kq, :, :],
                        in_=src_t[:, sl, ft, sg * TS:(sg + 1) * TS]),
                      reads=[f"{skey}:{sl}:{ft}"], writes=[f"knq8:{ss}"],
                      tag=f"rsh:{c}:{sg}:{kq}:{ft}")
        def prep(w):
            sg = w // (TS // TB)
            wl = w % (TS // TB)
            ss = sg % 2
            wp = w % 2
            for hf in range(2):
                ps = self.psW[hf][wp]
                pk = f"psW{hf}{wp}"
                o("T", lambda e, hf=hf, ps=ps, wl=wl, ss=ss: e.matmul(
                    ps[:, 0:512], self.sel[hf][:],
                    self.knq8[0:8, ss, :, :, wl * TB:(wl + 1) * TB],
                    start=True, stop=True),
                  reads=[f"knq8:{ss}", f"sel:{hf}", pk + ":f"], writes=[pk],
                  tag=f"wbc:{c}:{w}:{hf}")
                o("S", lambda e, hf=hf, ps=ps, wp=wp: e.copy(
                    out=self.bcw[:, hf, wp, :, :, :],
                    in_=ps[:, 0:512].rearrange("p (kq j t) -> p kq j t", kq=2, j=32).transpose([0, 1, 3, 2])),
                  reads=[pk], writes=[f"bcw:{hf}:{wp}", pk + ":f"],
                  tag=f"wbcev:{c}:{w}:{hf}")
        rsh(0)
        prep(0)
        for sg in range(NSG):
            if sg + 1 < NSG:
                rsh(sg + 1)
            for wl in range(TS // TB):
                w = sg * (TS // TB) + wl
                if w + 1 < NWIN:
                    prep(w + 1)
                W = c * NWIN + w
                wp = W % 2
                wr = W % RW
                u0 = w * TB
                for t in range(TB):
                    u = u0 + t
                    for hf in range(2):
                        if W == 0 and t == 0 and c == 0:
                            prev = self.zeroS[:, :]
                            prevkey = "zeroS"
                        elif t == 0:
                            prev = self.hist[:, hf, (W - 1) % RW, TB - 1, :]
                            prevkey = f"hist:{hf}:{(W - 1) % RW}"
                        else:
                            prev = self.hist[:, hf, wr, t - 1, :]
                            prevkey = f"hist:{hf}:{wr}"
                        knsl = self.bcw[:, hf, wp, 0, t, :]
                        vcol = self.v_fm[:, sl, hf, u:u + 1]
                        rbcol = self.rb_fm[:, sl, hf, u:u + 1]
                        bcol = self.b_fm[:, sl, hf, u:u + 1]
                        o("V", lambda e, hf=hf, prev=prev, knsl=knsl: e.scalar_tensor_tensor(
                            out=self.m_scr[:, hf, :], in0=prev, scalar=-1.0, in1=knsl,
                            op0=A.mult, op1=A.mult, accum_out=self.dcol[:, hf, :]),
                          reads=[prevkey, f"bcw:{hf}:{wp}", f"mscr:{hf}:f"],
                          writes=[f"mscr:{hf}", f"dcol:{hf}"], tag=f"ttr:{u}:{hf}")
                        o("V", lambda e, hf=hf, vcol=vcol, rbcol=rbcol: e.tensor_scalar(
                            out=self.gcol[:, hf, :], in0=self.dcol[:, hf, :],
                            scalar1=vcol, scalar2=rbcol, op0=A.add, op1=A.mult),
                          reads=[f"dcol:{hf}", f"vfm:{sl}:{hf}", f"rbfm:{sl}:{hf}"],
                          writes=[f"gcol:{hf}"], tag=f"gc:{u}:{hf}")
                        o("V", lambda e, hf=hf, prev=prev, knsl=knsl: e.scalar_tensor_tensor(
                            out=self.inner[:, hf, :], in0=knsl, scalar=self.gcol[:, hf, :],
                            in1=prev, op0=A.mult, op1=A.add),
                          reads=[prevkey, f"bcw:{hf}:{wp}", f"gcol:{hf}"],
                          writes=[f"inner:{hf}"], tag=f"stt:{u}:{hf}")
                        o("S", lambda e, hf=hf, wr=wr, t=t, bcol=bcol: e.activation(
                            out=self.hist[:, hf, wr, t, :], in_=self.inner[:, hf, :],
                            func=ACT.Tanh, scale=bcol),
                          reads=[f"inner:{hf}", f"bfm:{sl}:{hf}"],
                          writes=[f"hist:{hf}:{wr}", f"mscr:{hf}:f"], tag=f"tanh:{u}:{hf}")
                for hf in range(2):
                    o("G", lambda e, hf=hf, wr=wr, wp=wp: e.tensor_tensor(
                        self.gq_scr[:, 0, :, :], self.hist[:, hf, wr, :, :],
                        self.bcw[:, hf, wp, 1, :, :], A.mult),
                      reads=[f"hist:{hf}:{wr}", f"bcw:{hf}:{wp}", "gqf"],
                      writes=["gq"], tag=f"mq:{c}:{w}:{hf}")
                    o("V", lambda e, hf=hf, u0=u0: e.tensor_reduce(
                        out=self.Sq[:, 0, hf, u0:u0 + TB],
                        in_=self.gq_scr[:, 0, :, :],
                        axis=AX.X, op=A.add),
                      reads=["gq"], writes=[f"Sq:{hf}", "gqf"], tag=f"sqr:{c}:{w}:{hf}")

    def wout_chunk(self, c):
        o = self.op
        sl = c % 2
        wsl = c % self.NW
        for hf in range(2):
            o("S", lambda e, hf=hf: e.activation(
                out=self.cell[:, sl, hf, :], in_=self.Sq[:, 0, hf, :], func=ACT.Sigmoid),
              reads=[f"Sq:{hf}"], writes=[f"cell:{sl}:{hf}"], tag=f"sgm:{c}:{hf}")
            o("V", lambda e, hf=hf: e.tensor_tensor(
                self.cell[:, sl, hf, :], self.cell[:, sl, hf, :], self.Sq[:, 0, hf, :], A.mult),
              reads=[f"Sq:{hf}", f"cell:{sl}:{hf}"], writes=[f"cell:{sl}:{hf}"],
              tag=f"cl1:{c}:{hf}")
            o("V", lambda e, hf=hf: e.tensor_tensor(
                self.cell[:, sl, hf, :], self.cell[:, sl, hf, :], self.Sq[:, 0, hf, :], A.mult),
              reads=[f"Sq:{hf}", f"cell:{sl}:{hf}"], writes=[f"cell:{sl}:{hf}"],
              tag=f"cl2:{c}:{hf}")
        for dt in range(NDT):
            ps = [self.psT0, self.psT1][dt % 2]
            pk = ["psT0", "psT1"][dt % 2]
            for mt in range(2):
                o("T", lambda e, dt=dt, mt=mt, ps=ps: e.matmul(
                    ps[:, 0:CT], self.WoT[:, mt, dt * 128:(dt + 1) * 128],
                    self.cell[:, sl, mt, :], start=(mt == 0), stop=(mt == 1)),
                  reads=[f"cell:{sl}:{mt}", "WoT"] + ([pk + ":f"] if mt == 0 else [pk]),
                  writes=[pk], tag=f"wo:{c}:{dt}:{mt}")
            o("V", lambda e, dt=dt, ps=ps: e.tensor_tensor(
                self.s_fm[:, dt, c * CT:(c + 1) * CT], self.xw[:, wsl, dt, :],
                ps[:, 0:CT], A.add),
              reads=[pk, f"xw:{wsl}:{dt}"], writes=[f"s:{c}:{dt}", pk + ":f"],
              tag=f"woev:{c}:{dt}")

    def dump_sfm(self):
        o = self.op
        for dt in range(NDT):
            o("Q0", lambda e, dt=dt: e.dma_start(
                out=self.o_out[0:self.T, dt * 128:(dt + 1) * 128].transpose([1, 0]),
                in_=self.s_fm[:, dt, :], allow_slow_non_contiguous=True),
              reads=[f"s:{c}:{dt}" for c in range(self.NC)], writes=[f"outdma:{dt % (self.T // 128)}"],
              tag=f"dump:{dt}")

    def terminal(self):
        o = self.op
        keys = [f"outdma:{c}" for c in range(self.T // 128)]
        o("G", lambda e: e.nop(), reads=keys, tag="fin")

    def build_attn(self):
        self.load_weights()
        self.proj_chunk(0)
        for c in range(1, self.NC):
            self.proj_chunk(c)
            self.scan_chunk(c - 1)
            self.wout_chunk(c - 1)
        self.scan_chunk(self.NC - 1)
        self.wout_chunk(self.NC - 1)

    def finish(self):
        self.sch.finalize()
        sems = {}
        for k in _INC:
            if k in ("Q0", "Q1", "QG"):
                for ln in range(NLANES):
                    sems[(k, ln)] = self.nc.alloc_semaphore(f"sem{k}_{ln}")
            else:
                sems[k] = self.nc.alloc_semaphore(f"sem{k}")
        self.sch.emit(self.nc, sems)

    def alloc_ffn(self):
        nc = self.nc
        def addr(t):
            return nc.lookup_mloc(t).addr
        def at(name, shape, base, off, dtype=F32):
            return nc.alloc_sbuf_tensor_at(name, shape, dtype, offset=base + off)
        T_ = self.T
        SLE = 256
        self.SLE = SLE
        baseA = addr(self.WinT)
        offA = 0
        def atA(name, shape, nbytes, dtype=F32):
            nonlocal offA
            t = at(name, shape, baseA, offA, dtype)
            offA += (nbytes + 31) & ~31
            return t
        self.xgfm = atA("xgfm", [128, NDT, SLE], NDT * SLE * 4)
        self.wfp = [atA("wfp0", [128, D], D * 4), atA("wfp1", [128, D], D * 4)]
        self.wfg = [atA("wfg0", [128, D], D * 4), atA("wfg1", [128, D], D * 4)]
        self.wf2s = [atA("wf2s0", [128, D], D * 4), atA("wf2s1", [128, D], D * 4)]
        SLH = min(512, T_)
        self.SLH = SLH
        self.mgrp = atA("mgrp", [128, 8, SLH], 8 * SLH * 4)
        self.f2st = atA("f2st", [128, NDT, SLH], NDT * SLH * 4)
        baseB = addr(self.xin)
        offB = 0
        def atB(name, shape, nbytes, dtype=F32):
            nonlocal offB
            t = at(name, shape, baseB, offB, dtype)
            offB += (nbytes + 31) & ~31
            return t
        self.xgtm = atB("xgtm", [128, D], D * 4)
        self.sc2tm = atB("sc2tm", [128, D], D * 4)
        self.x2c = atB("x2c", [128, 2, D], 2 * D * 4)
        self.accc = atB("accc", [128, 2, D], 2 * D * 4)
        self.otm = atB("otm", [128, 2, D], 2 * D * 4)
        self.s1tm = atB("s1tm", [128, D], D * 4)
        self.o1tm = atB("o1tm", [128, D], D * 4)
        self.scr2 = atB("scr2", [128, D], D * 4)
        self.zrow = self.scr2
        self.hgt = atB("hgt", [128, 512], 512 * 4)
        self.gbc = atB("gbc", [128, SLE], SLE * 4)
        self.gpr = atB("gpr", [1, 2, SLE, 2], 2 * SLE * 2 * 4)
        self.gsx = atB("gsx", [1, 2, SLE], 2 * SLE * 4)
        self.lgt = atB("lgt", [128, 8], 8 * 4)
        self.gtm = atB("gtm", [128, T_ // 128, 8], (T_ // 128) * 8 * 4)
        self.rsc = atB("rsc", [128, 16], 16 * 4)
        self.rnk = atB("rnk", [128, 8], 8 * 4)
        self.rnki = atB("rnki", [128, 2], 2 * 4, mybir.dt.int32)
        self.idxc = atB("idxc", [128, 4], 4 * 4, mybir.dt.int32)
        self.idxcf = atB("idxcf", [128, 4, 2], 4 * 2 * 4)
        self.mAB = atB("mAB", [128, 16], 16 * 4)
        self.prd = atB("prd", [128, 8], 8 * 4)
        self.pr2 = atB("pr2", [128, 4], 4 * 4)
        self.rnkf = atB("rnkf", [128, 2], 2 * 4)
        self.asr = atB("asr", [1, 48], 48 * 4)
        self.ter = atB("ter", [1, self.NSLAB], self.NSLAB * 4)
        self.idxeA = atB("idxeA", [128, self.NSLAB, 8], self.NSLAB * 8 * 4, mybir.dt.int32)
        self.bsrow = atB("bsrow", [1, 16], 16 * 4)
        self.tpre = atB("tpre", [128, 8, 2], 8 * 2 * 4)
        self.lncA = atB("lncA", [128, 12], 64)
        self.lnc = atB("lnc", [128, 12], 64)

    def barrier(self, name, keys):
        self.op("V", lambda e: e.engine_nop(), reads=keys, writes=[name], tag=f"bar:{name}")

    def ln1(self):
        o = self.op
        T_ = self.T
        for c in range(T_ // 128):
            for dt in range(NDT):
                ps = [self.psT0, self.psT1][dt % 2]
                pk = ["psT0", "psT1"][dt % 2]
                o("T", lambda e, c=c, dt=dt, ps=ps: e.transpose(
                    ps[:, 0:128], self.s_fm[:, dt, c * 128:(c + 1) * 128], self.eye[:]),
                  reads=[f"s:{c}:{dt}", "eye", pk + ":f"], writes=[pk], tag=f"l1tr:{c}:{dt}")
                o("S", lambda e, dt=dt, ps=ps: e.copy(
                    out=self.s1tm[:, dt * 128:(dt + 1) * 128], in_=ps[:, 0:128]),
                  reads=[pk, "PHASE2"], writes=["s1tm"], tag=f"l1trev:{c}:{dt}")
            L = self.lncA
            o("S", lambda e: e.activation(
                out=self.o1tm[:, :], in_=self.s1tm[:, :], func=ACT.Copy,
                accum_out=L[:, 0:1]),
              reads=["s1tm", "PHASE2"], writes=["o1tm", "la0"], tag=f"l1sum:{c}")
            o("S", lambda e: e.activation(
                out=self.o1tm[:, :], in_=self.s1tm[:, :], func=ACT.Square,
                accum_out=L[:, 1:2]),
              reads=["s1tm"], writes=["o1tm", "la1"], tag=f"l1sq:{c}")
            sc = 1.0 / D
            o("V", lambda e: e.tensor_scalar(out=L[:, 2:3], in0=L[:, 0:1], scalar1=sc,
                scalar2=None, op0=A.mult),
              reads=["la0"], writes=["la2"], tag=f"l1mu:{c}")
            o("V", lambda e: e.tensor_scalar(out=L[:, 3:4], in0=L[:, 1:2], scalar1=sc,
                scalar2=None, op0=A.mult),
              reads=["la1"], writes=["la3"], tag=f"l1ex2:{c}")
            o("V", lambda e: e.tensor_tensor(L[:, 4:5], L[:, 2:3], L[:, 2:3], A.mult),
              reads=["la2"], writes=["la4"], tag=f"l1m2:{c}")
            o("V", lambda e: e.tensor_tensor(L[:, 4:5], L[:, 3:4], L[:, 4:5], A.subtract),
              reads=["la3", "la4"], writes=["la4"], tag=f"l1var:{c}")
            self.rstd_cols(L, f"l1rs:{c}", 4, 5, eps=1e-5, pre="la")
            o("V", lambda e: e.tensor_scalar(
                out=self.o1tm[:, :], in0=self.s1tm[:, :],
                scalar1=L[:, 2:3], scalar2=L[:, 5:6], op0=A.subtract, op1=A.mult),
              reads=["s1tm", "la2", "la5"], writes=["o1tm"], tag=f"l1nrm:{c}")
            o("Q0", lambda e, c=c: e.dma_start(
                out=self.x2tm[c * 128:(c + 1) * 128, :], in_=self.o1tm[:, :]),
              reads=["o1tm"], writes=["x2tm"], tag=f"l1dma:{c}")
            for dt in range(NDT):
                ps = [self.psT0, self.psT1][dt % 2]
                pk = ["psT0", "psT1"][dt % 2]
                o("T", lambda e, dt=dt, ps=ps: e.transpose(
                    ps[:, 0:128], self.o1tm[:, dt * 128:(dt + 1) * 128], self.eye[:]),
                  reads=["o1tm", "eye", pk + ":f"], writes=[pk], tag=f"l1trb:{c}:{dt}")
                o("V", lambda e, c=c, dt=dt, ps=ps: e.tensor_copy(
                    self.s_fm[:, dt, c * 128:(c + 1) * 128], ps[:, 0:128]),
                  reads=[pk], writes=[f"s:{c}:{dt}", pk + ":f"], tag=f"l1trbe:{c}:{dt}")

    def router(self):
        o = self.op
        T_ = self.T
        nt = T_ // 128
        for tt in range(nt):
            cc = tt
            for kd in range(NDT):
                o("T", lambda e, tt=tt, kd=kd: e.matmul(
                    self.psB[:, 0:8], self.s_fm[:, kd, tt * 128:(tt + 1) * 128],
                    self.wrout[:, kd, :], start=(kd == 0), stop=(kd == NDT - 1)),
                  reads=[f"s:{cc}:{kd}", "wrout"] + (["psB:f"] if kd == 0 else ["psB"]),
                  writes=["psB"], tag=f"rt:{tt}:{kd}")
            o("V", lambda e, tt=tt: e.tensor_copy(self.lgt[:, :], self.psB[:, 0:8]),
              reads=["psB", "PHASE2"], writes=["lgt", "psB:f"], tag=f"rtev:{tt}")
            R = self.rsc
            o("V", lambda e, tt=tt: e.max(out=R[:, 0:8], in_=self.lgt[:, :]),
              reads=["lgt", "PHASE2"], writes=["rsc8"], tag=f"rtmax:{tt}")
            o("V", lambda e: e.tensor_tensor(R[:, 8:9], R[:, 0:1], R[:, 1:2], A.subtract),
              reads=["rsc8"], writes=["rscd"], tag=f"rtd:{tt}")
            o("S", lambda e: e.activation(out=R[:, 9:10], in_=R[:, 8:9], func=ACT.Sigmoid),
              reads=["rscd"], writes=["rscs1"], tag=f"rts1:{tt}")
            o("S", lambda e: e.activation(out=R[:, 10:11], in_=R[:, 8:9], func=ACT.Sigmoid,
                                          scale=-1.0),
              reads=["rscd"], writes=["rscs2"], tag=f"rts2:{tt}")
            o("V", lambda e, tt=tt: e.tensor_tensor(
                self.gtm[:, tt, :], self.lgt[:, :],
                R[:, 0:1].to_broadcast([128, 8]), A.is_equal),
              reads=["lgt", "rsc8"], writes=[f"gtm:{tt}"], tag=f"rte1:{tt}")
            o("V", lambda e, tt=tt: e.tensor_scalar(
                out=self.gtm[:, tt, :], in0=self.gtm[:, tt, :], scalar1=R[:, 9:10],
                scalar2=None, op0=A.mult),
              reads=[f"gtm:{tt}", "rscs1"], writes=[f"gtm:{tt}"], tag=f"rtg1:{tt}")
            o("V", lambda e, tt=tt: e.tensor_tensor(
                R[:, 0:8], self.lgt[:, :],
                R[:, 1:2].to_broadcast([128, 8]), A.is_equal),
              reads=["lgt", "rsc8"], writes=["rsc8b"], tag=f"rte2:{tt}")
            o("V", lambda e: e.tensor_scalar(
                out=R[:, 0:8], in0=R[:, 0:8], scalar1=R[:, 10:11],
                scalar2=None, op0=A.mult),
              reads=["rsc8b", "rscs2"], writes=["rsc8b"], tag=f"rtg2:{tt}")
            o("V", lambda e, tt=tt: e.tensor_tensor(
                self.gtm[:, tt, :], self.gtm[:, tt, :], R[:, 0:8], A.add),
              reads=[f"gtm:{tt}", "rsc8b"], writes=[f"gtm:{tt}"], tag=f"rtg3:{tt}")

    def index_build(self):
        o = self.op
        T_ = self.T
        nt = T_ // 128
        NS = self.NS
        o("G", lambda e: e.memset(self.zrow[:, :], 0.0),
          reads=["PHASE2"], writes=["zrow"], tag="pf_z")
        o("G", lambda e, T_=T_: e.memset(self.tpre[:, :, 0], float(T_)),
          reads=["PHASE2"], writes=["tpre"], tag="pf_t0")
        o("G", lambda e: e.memset(self.tpre[:, :, 1], 0.0),
          reads=["PHASE2"], writes=["tpre"], tag="pf_t1")
        igf = self.idxg.rearrange("n two -> (n two)")
        ntot = self.NPAD * 2
        zoff = 0
        zi = 0
        while zoff < ntot:
            step = min(2048, ntot - zoff)
            o("QG", lambda e, zoff=zoff, step=step: e.dma_start(
                out=igf[zoff:zoff + step], in_=self.tpre[:, 0:step // 256, :]),
              reads=["tpre"], writes=["idxg"], tag=f"pf_ig:{zi}")
            zoff += step
            zi += 1
        for zz in range((T_ + 128) // 128):
            o("QG", lambda e, zz=zz: e.dma_start(
                out=self.acctm[zz * 128:(zz + 1) * 128, :], in_=self.zrow[:, :]),
              reads=["zrow"], writes=["acctm"], tag=f"pf_a:{zz}")
        o("QG", lambda e: e.dma_start(
            out=self.x2tm[T_:T_ + 128, :], in_=self.zrow[:, :]),
          reads=["zrow"], writes=["x2tm:trash"], tag="pf_xt")
        for tt in range(nt):
            o("V", lambda e, tt=tt: e.tensor_scalar(
                out=self.lgt[:, :], in0=self.gtm[:, tt, :], scalar1=0.0,
                scalar2=None, op0=A.is_gt),
              reads=[f"gtm:{tt}"], writes=["lgt"], tag=f"ib1_m:{tt}")
            o("T", lambda e, tt=tt: e.matmul(
                self.psA[0:1, 0:8], self.ones128[:], self.lgt[:, :],
                start=(tt == 0), stop=(tt == nt - 1)),
              reads=["lgt", "ones128"] + (["psA:f"] if tt == 0 else ["psA"]),
              writes=["psA"], tag=f"ib1_c:{tt}")
        AS = self.asr
        o("V", lambda e: e.tensor_copy(AS[0:1, 0:8], self.psA[0:1, 0:8]),
          reads=["psA"], writes=["asr"], tag="ib_cnt")
        o("V", lambda e: e.engine_nop(), reads=["asr", "psA"], writes=["psA:f"], tag="ib_cfree")
        o("V", lambda e: e.memset(AS[0:1, 8:16], 0.0), reads=["asr"],
          writes=["asr"], tag="ib_p0")
        for k in range(8):
            o("V", lambda e, k=k: e.scalar_tensor_tensor(
                out=AS[0:1, 8:16], in0=AS[0:1, 0:8], scalar=float(256 * k),
                in1=AS[0:1, 8:16], op0=A.is_gt, op1=A.add),
              reads=["asr"], writes=["asr"], tag=f"ib_p1:{k}")
        o("V", lambda e: e.tensor_scalar(out=AS[0:1, 8:16], in0=AS[0:1, 8:16],
            scalar1=256.0, scalar2=None, op0=A.mult),
          reads=["asr"], writes=["asr"], tag="ib_p2")
        o("V", lambda e: e.memset(AS[0:1, 16:17], 0.0), reads=["asr"], writes=["asr"], tag="ib_a0")
        for ei in range(1, E):
            o("V", lambda e, ei=ei: e.tensor_tensor(
                AS[0:1, 16 + ei:17 + ei], AS[0:1, 15 + ei:16 + ei],
                AS[0:1, 7 + ei:8 + ei], A.add),
              reads=["asr"], writes=["asr"], tag=f"ib_a:{ei}")
        o("V", lambda e: e.memset(self.ter[0:1, :], -1.0), reads=["PHASE2"],
          writes=["ter"], tag="ib_t0")
        for ei in range(E):
            o("V", lambda e, ei=ei: e.tensor_scalar(
                out=self.bsrow[0:1, 8:9], in0=AS[0:1, 16 + ei:17 + ei],
                scalar1=1.0, scalar2=None, op0=A.mult),
              reads=["asr"], writes=["bsrow"], tag=f"ib_tc:{ei}")
            o("V", lambda e, ei=ei: e.tensor_scalar(
                out=self.gsx[0:1, 0, 0:self.NSLAB], in0=self.thr[0:1, :],
                scalar1=self.bsrow[0:1, 8:9], scalar2=None, op0=A.is_ge),
              reads=["thr", "bsrow"], writes=["gsx"], tag=f"ib_t1:{ei}")
            o("V", lambda e: e.tensor_tensor(
                self.ter[0:1, :], self.ter[0:1, :], self.gsx[0:1, 0, 0:self.NSLAB], A.add),
              reads=["ter", "gsx"], writes=["ter"], tag=f"ib_t2:{ei}")
        o("V", lambda e: e.tensor_copy(self.bsrow[0:1, 0:8], AS[0:1, 16:24]),
          reads=["asr"], writes=["bsrow"], tag="ib_b0")
        R = self.rsc
        for tt in range(nt):
            o("V", lambda e, tt=tt: e.tensor_scalar(
                out=self.lgt[:, :], in0=self.gtm[:, tt, :], scalar1=0.0,
                scalar2=None, op0=A.is_gt),
              reads=[f"gtm:{tt}"], writes=["lgt"], tag=f"ib_m:{tt}")
            o("T", lambda e: e.matmul(self.psB[:, 0:8], self.ltri[:], self.lgt[:, :],
                                      start=True, stop=True),
              reads=["lgt", "ltri", "psB:f"], writes=["psB"], tag=f"ib_cs:{tt}")
            o("T", lambda e: e.matmul(self.psA[0:1, 0:8], self.ones128[:], self.lgt[:, :],
                                      start=True, stop=True),
              reads=["lgt", "ones128", "psA:f"], writes=["psA"], tag=f"ib_cnt:{tt}")
            o("T", lambda e: e.matmul(self.psT0[:, 0:8], self.onesrow[:],
                                      self.bsrow[0:1, 0:8], start=True, stop=True),
              reads=["bsrow", "onesrow", "psT0:f"], writes=["psT0"], tag=f"ib_bb:{tt}")
            o("V", lambda e: e.tensor_copy(self.rnk[:, :], self.psB[:, 0:8]),
              reads=["psB"], writes=["rnk", "psB:f"], tag=f"ib_r0:{tt}")
            o("V", lambda e: e.tensor_tensor(self.rnk[:, :], self.rnk[:, :],
                                             self.psT0[:, 0:8], A.add),
              reads=["rnk", "psT0"], writes=["rnk", "psT0:f"], tag=f"ib_r1:{tt}")
            o("V", lambda e: e.tensor_tensor(self.bsrow[0:1, 0:8], self.bsrow[0:1, 0:8],
                                             self.psA[0:1, 0:8], A.add),
              reads=["bsrow", "psA"], writes=["bsrow", "psA:f"], tag=f"ib_b:{tt}")
            o("V", lambda e: e.tensor_copy(R[:, 0:8], self.lgt[:, :]),
              reads=["lgt"], writes=["rsc"], tag=f"ib_c0:{tt}")
            o("V", lambda e: e.tensor_copy(R[:, 8:9], R[:, 0:1]),
              reads=["rsc"], writes=["rsc"], tag=f"ib_c1a:{tt}")
            o("V", lambda e: e.tensor_tensor(R[:, 9:16], R[:, 1:8], R[:, 0:7], A.add),
              reads=["rsc"], writes=["rsc"], tag=f"ib_c1b:{tt}")
            o("V", lambda e: e.tensor_copy(R[:, 0:2], R[:, 8:10]),
              reads=["rsc"], writes=["rsc"], tag=f"ib_c2a:{tt}")
            o("V", lambda e: e.tensor_tensor(R[:, 2:8], R[:, 10:16], R[:, 8:14], A.add),
              reads=["rsc"], writes=["rsc"], tag=f"ib_c2b:{tt}")
            o("V", lambda e: e.tensor_copy(R[:, 8:12], R[:, 0:4]),
              reads=["rsc"], writes=["rsc"], tag=f"ib_c4a:{tt}")
            o("V", lambda e: e.tensor_tensor(R[:, 12:16], R[:, 4:8], R[:, 0:4], A.add),
              reads=["rsc"], writes=["rsc"], tag=f"ib_c4b:{tt}")
            o("V", lambda e: e.tensor_tensor(R[:, 0:8], R[:, 8:16], self.lgt[:, :],
                                             A.subtract),
              reads=["rsc", "lgt"], writes=["rsc"], tag=f"ib_ex:{tt}")
            o("V", lambda e: e.scalar_tensor_tensor(
                out=self.mAB[:, 0:8], in0=R[:, 0:8], scalar=0.0,
                in1=self.lgt[:, :], op0=A.is_equal, op1=A.mult),
              reads=["rsc", "lgt"], writes=["mAB"], tag=f"ib_mA:{tt}")
            o("V", lambda e: e.tensor_tensor(self.mAB[:, 8:16], self.lgt[:, :],
                                             self.mAB[:, 0:8], A.subtract),
              reads=["lgt", "mAB"], writes=["mAB"], tag=f"ib_mB:{tt}")
            for br in range(2):
                msl = self.mAB[:, br * 8:br * 8 + 8]
                o("V", lambda e, msl=msl, br=br: e.scalar_tensor_tensor(
                    out=self.prd[:, :], in0=self.rnk[:, :], scalar=1.0,
                    in1=msl, op0=A.mult, op1=A.mult,
                    accum_out=self.rnkf[:, br:br + 1]),
                  reads=["rnk", "mAB"], writes=["prd", "rnkf"], tag=f"ib_rs:{tt}:{br}")
                o("V", lambda e, msl=msl, br=br, tt=tt: e.scalar_tensor_tensor(
                    out=self.prd[:, :], in0=self.gtm[:, tt, :], scalar=1.0,
                    in1=msl, op0=A.mult, op1=A.mult,
                    accum_out=self.pr2[:, 2 * br + 1:2 * br + 2]),
                  reads=[f"gtm:{tt}", "mAB"], writes=["prd", "pr2"], tag=f"ib_gs:{tt}:{br}")
                o("V", lambda e, msl=msl: e.scalar_tensor_tensor(
                    out=self.prd[:, :], in0=msl, scalar=1.0,
                    in1=msl, op0=A.mult, op1=A.mult,
                    accum_out=self.rsc[:, 0:1]),
                  reads=["mAB"], writes=["prd", "rsc"], tag=f"ib_cb:{tt}:{br}")
                o("V", lambda e, br=br, tt=tt: e.tensor_tensor(
                    self.pr2[:, 2 * br:2 * br + 1], self.iotf[:, tt:tt + 1],
                    self.rsc[:, 0:1], A.mult),
                  reads=["iotf", "rsc"], writes=["pr2"], tag=f"ib_id:{tt}:{br}")
                o("V", lambda e, br=br: e.tensor_copy(
                    self.rnki[:, br:br + 1], self.rnkf[:, br:br + 1]),
                  reads=["rnkf"], writes=["rnki"], tag=f"ib_cv:{tt}:{br}")
                o("QG", lambda e, br=br: e.indirect_dma_start(
                    out=self.idxg, out_offset=IOA(self.rnki[:, br:br + 1], 0),
                    in_=self.pr2[:, 2 * br:2 * br + 2], in_offset=None,
                    compute_op=A.add),
                  reads=["rnki", "pr2"], writes=["idxg"],
                  tag=f"ib_sc:{tt}:{br}")

    def moe(self):
        o = self.op
        SLE = self.SLE
        NSLAB = self.NSLAB
        f1v = self.i_fc1.rearrange("e d i -> (e d) i")
        f2v = self.i_fc2.rearrange("e d h -> (e d) h")
        banks = [self.psA, self.psB, self.psW[0][0], self.psW[0][1],
                 self.psW[1][0], self.psW[1][1], self.psT0, self.psT1]
        bkey = ["psA", "psB", "psW00", "psW01", "psW10", "psW11", "psT0", "psT1"]
        o("V", lambda e: e.tensor_scalar(
            out=self.asr[0:1, 24:24 + NSLAB], in0=self.ter[0:1, :],
            scalar1=1024.0, scalar2=None, op0=A.mult),
          reads=["ter"], writes=["asr"], tag="ie_tx")
        o("T", lambda e, NSLAB=NSLAB: e.matmul(
            self.psB[:, 0:NSLAB], self.onesrow[:], self.asr[0:1, 24:24 + NSLAB],
            start=True, stop=True),
          reads=["asr", "onesrow", "psB:f"], writes=["psB"], tag="ie_bc")
        o("V", lambda e, NSLAB=NSLAB: e.tensor_tensor(
            self.idxeA[:, :, :],
            self.psB.ap()[:, 0:NSLAB].unsqueeze(2).to_broadcast([128, NSLAB, 8]),
            self.kdp.ap()[:, :].unsqueeze(1).to_broadcast([128, NSLAB, 8]),
            A.add),
          reads=["psB", "kdp"], writes=["idxeA", "psB:f"], tag="ie_ix")
        for s in range(NSLAB):
            gsl = s % 2
            o("Q0", lambda e, s=s, gsl=gsl: e.dma_start(
                out=self.gpr[0:1, gsl, :, :], in_=self.idxg[s * SLE:(s + 1) * SLE, :]),
              reads=["idxg"], writes=[f"gpr:{gsl}"], tag=f"gsl:{s}")
            o("V", lambda e, gsl=gsl: e.tensor_copy(
                self.gsx[0:1, gsl, :], self.gpr[0:1, gsl, :, 1]),
              reads=[f"gpr:{gsl}"], writes=[f"gsx:{gsl}"], tag=f"gsx:{s}")
            o("T", lambda e, gsl=gsl: e.matmul(
                self.psT0[:, 0:SLE], self.onesrow[:],
                self.gsx[0:1, gsl, :], start=True, stop=True),
              reads=[f"gsx:{gsl}", "onesrow", "psT0:f"], writes=["psT0"],
              tag=f"gbcm:{s}")
            o("S", lambda e: e.copy(out=self.gbc[:, :], in_=self.psT0[:, 0:SLE]),
              reads=["psT0"], writes=["gbc", "psT0:f"], tag=f"gbce:{s}")
            for t2 in range(2):
                tl = 2 * s + t2
                isl = tl % 4
                o("Q0", lambda e, tl=tl, isl=isl: e.dma_start(
                    out=self.idxcf[:, isl, :],
                    in_=self.idxg[tl * 128:(tl + 1) * 128, :]),
                  reads=["idxg"], writes=[f"idxcf:{isl}"], tag=f"ixl:{s}:{t2}")
                o("V", lambda e, isl=isl: e.tensor_copy(
                    self.idxc[:, isl:isl + 1], self.idxcf[:, isl, 0:1]),
                  reads=[f"idxcf:{isl}"], writes=[f"idxc:{isl}"], tag=f"ixc:{s}:{t2}")
                o("QG", lambda e, isl=isl: e.indirect_dma_start(
                    out=self.xgtm[:, :], out_offset=None,
                    in_=self.x2tm, in_offset=IOA(self.idxc[:, isl:isl + 1], 0)),
                  reads=[f"idxc:{isl}", "x2tm", "x2tm:trash", "xgtm:f"],
                  writes=["xgtm"], tag=f"gth:{s}:{t2}")
                for dt in range(NDT):
                    ps = [self.psT0, self.psT1][dt % 2]
                    pk = ["psT0", "psT1"][dt % 2]
                    o("T", lambda e, dt=dt, ps=ps: e.transpose(
                        ps[:, 0:128], self.xgtm[:, dt * 128:(dt + 1) * 128], self.eye[:]),
                      reads=["xgtm", "eye", pk + ":f"], writes=[pk],
                      tag=f"gtr:{s}:{t2}:{dt}")
                    wrts = [f"xgfm:{t2}:{dt}", pk + ":f"] + (["xgtm:f"] if dt == NDT - 1 else [])
                    o("S", lambda e, dt=dt, t2=t2, ps=ps: e.copy(
                        out=self.xgfm[:, dt, t2 * 128:(t2 + 1) * 128], in_=ps[:, 0:128]),
                      reads=[pk], writes=wrts, tag=f"gtrev:{s}:{t2}:{dt}")
            xk = [f"xgfm:{t2}:{dt}" for t2 in range(2) for dt in range(NDT)]
            for kd in range(NDT):
                wsl = kd % 2
                o("QG", lambda e, s=s, kd=kd, wsl=wsl: e.indirect_dma_start(
                    out=self.wfp[wsl].ap(), out_offset=None, in_=f1v,
                    in_offset=IOA(self.idxeA[:, s, kd:kd + 1], 0)),
                  reads=["idxeA", f"wfp:{wsl}:f"], writes=[f"wfp:{wsl}"],
                  tag=f"wgp:{s}:{kd}")
                for ki in range(NDT):
                    o("T", lambda e, wsl=wsl, ki=ki, kd=kd, ps=banks[ki]: e.matmul(
                        ps[:, 0:SLE], self.wfp[wsl][:, ki * 128:(ki + 1) * 128],
                        self.xgfm[:, kd, 0:SLE], start=(kd == 0), stop=(kd == NDT - 1)),
                      reads=xk + [f"wfp:{wsl}"] +
                            ([bkey[ki] + ":f"] if kd == 0 else [bkey[ki]]),
                      writes=[bkey[ki]] + ([f"wfp:{wsl}:f"] if ki == NDT - 1 else []),
                      tag=f"f1p:{s}:{kd}:{ki}")
            for ki in range(NDT):
                o("S", lambda e, ki=ki, ps=banks[ki]: e.activation(
                    out=self.mgrp[:, ki, 0:SLE], in_=ps[:, 0:SLE], func=ACT.Silu),
                  reads=[bkey[ki]], writes=[f"m:{ki}", bkey[ki] + ":f"],
                  tag=f"pev:{s}:{ki}")
            for kd in range(NDT):
                wsl = kd % 2
                o("QG", lambda e, s=s, kd=kd, wsl=wsl: e.indirect_dma_start(
                    out=self.wfg[wsl].ap(), out_offset=None, in_=f1v,
                    in_offset=IOA(self.idxeA[:, s, kd:kd + 1], 0), element_offset=D),
                  reads=["idxeA", f"wfg:{wsl}:f"], writes=[f"wfg:{wsl}"],
                  tag=f"wgg:{s}:{kd}")
                for ki in range(NDT):
                    o("T", lambda e, wsl=wsl, ki=ki, kd=kd, ps=banks[ki]: e.matmul(
                        ps[:, 0:SLE], self.wfg[wsl][:, ki * 128:(ki + 1) * 128],
                        self.xgfm[:, kd, 0:SLE], start=(kd == 0), stop=(kd == NDT - 1)),
                      reads=xk + [f"wfg:{wsl}"] +
                            ([bkey[ki] + ":f"] if kd == 0 else [bkey[ki]]),
                      writes=[bkey[ki]] + ([f"wfg:{wsl}:f"] if ki == NDT - 1 else []),
                      tag=f"f1g:{s}:{kd}:{ki}")
            for ki in range(NDT):
                o("V", lambda e, ki=ki, ps=banks[ki]: e.tensor_tensor(
                    self.mgrp[:, ki, 0:SLE], self.mgrp[:, ki, 0:SLE],
                    ps[:, 0:SLE], A.mult),
                  reads=[f"m:{ki}", bkey[ki]], writes=[f"m:{ki}", bkey[ki] + ":f"],
                  tag=f"gev:{s}:{ki}")
            for ki in range(NDT):
                wsl = ki % 2
                o("QG", lambda e, s=s, ki=ki, wsl=wsl: e.indirect_dma_start(
                    out=self.wf2s[wsl].ap(), out_offset=None, in_=f2v,
                    in_offset=IOA(self.idxeA[:, s, ki:ki + 1], 0)),
                  reads=["idxeA", f"wf2s:{wsl}:f"], writes=[f"wf2s:{wsl}"],
                  tag=f"wg2:{s}:{ki}")
                for dt in range(NDT):
                    o("T", lambda e, wsl=wsl, dt=dt, ki=ki, ps=banks[dt]: e.matmul(
                        ps[:, 0:SLE], self.wf2s[wsl][:, dt * 128:(dt + 1) * 128],
                        self.mgrp[:, ki, 0:SLE], start=(ki == 0), stop=(ki == NDT - 1)),
                      reads=[f"m:{ki}", f"wf2s:{wsl}"] +
                            ([bkey[dt] + ":f"] if ki == 0 else [bkey[dt]]),
                      writes=[bkey[dt]] + ([f"wf2s:{wsl}:f"] if dt == NDT - 1 else []),
                      tag=f"f2:{s}:{ki}:{dt}")
            for dt in range(NDT):
                o("V", lambda e, dt=dt, ps=banks[dt]: e.tensor_tensor(
                    self.f2st[:, dt, 0:SLE], ps[:, 0:SLE], self.gbc[:, :], A.mult),
                  reads=[bkey[dt], "gbc"], writes=[f"f2st:{dt}", bkey[dt] + ":f"],
                  tag=f"f2ev:{s}:{dt}")
            self._scatter(f"s{s}", s, SLE, True)
        self._shared()

    def _shared(self):
        o = self.op
        T_ = self.T
        SLH = self.SLH
        banks = [self.psA, self.psB, self.psW[0][0], self.psW[0][1],
                 self.psW[1][0], self.psW[1][1], self.psT0, self.psT1]
        bkey = ["psA", "psB", "psW00", "psW01", "psW10", "psW11", "psT0", "psT1"]
        for sl_i in range(T_ // SLH):
            s0 = sl_i * SLH
            cks = [c for c in range(T_ // CT) if c * CT >= s0 and c * CT < s0 + SLH]
            xk = [f"s:{c}:{kd}" for c in cks for kd in range(NDT)]
            for hi in range(2):
                for nm, wt, src_w, eo in (("p", self.wfp, self.i_gate, 0),
                                          ("g", self.wfg, self.i_up, 0)):
                    for kd in range(NDT):
                        wsl = kd % 2
                        o("Q0" if nm == "p" else "Q1",
                          lambda e, wt=wt, wsl=wsl, src_w=src_w, kd=kd, hi=hi: e.dma_start(
                            out=wt[wsl].ap(),
                            in_=src_w[kd * 128:(kd + 1) * 128, hi * D:(hi + 1) * D]),
                          reads=["PHASE2", f"wf{nm}:{wsl}:f"], writes=[f"wf{nm}:{wsl}"],
                          tag=f"shw{nm}:{sl_i}:{hi}:{kd}")
                        for ki in range(NDT):
                            o("T", lambda e, wt=wt, wsl=wsl, ki=ki, kd=kd, s0=s0, ps=banks[ki]: e.matmul(
                                ps[:, 0:SLH], wt[wsl][:, ki * 128:(ki + 1) * 128],
                                self.s_fm[:, kd, s0:s0 + SLH],
                                start=(kd == 0), stop=(kd == NDT - 1)),
                              reads=xk + [f"wf{nm}:{wsl}"] +
                                    ([bkey[ki] + ":f"] if kd == 0 else [bkey[ki]]),
                              writes=[bkey[ki]] + ([f"wf{nm}:{wsl}:f"] if ki == NDT - 1 else []),
                              tag=f"sh{nm}:{sl_i}:{hi}:{kd}:{ki}")
                    if nm == "p":
                        for ki in range(NDT):
                            o("S", lambda e, ki=ki, ps=banks[ki]: e.activation(
                                out=self.mgrp[:, ki, 0:SLH], in_=ps[:, 0:SLH],
                                func=ACT.Silu),
                              reads=[bkey[ki]], writes=[f"m:{ki}", bkey[ki] + ":f"],
                              tag=f"shpe:{sl_i}:{hi}:{ki}")
                    else:
                        for ki in range(NDT):
                            o("V", lambda e, ki=ki, ps=banks[ki]: e.tensor_tensor(
                                self.mgrp[:, ki, 0:SLH], self.mgrp[:, ki, 0:SLH],
                                ps[:, 0:SLH], A.mult),
                              reads=[f"m:{ki}", bkey[ki]],
                              writes=[f"m:{ki}", bkey[ki] + ":f"],
                              tag=f"shge:{sl_i}:{hi}:{ki}")
                for ki in range(NDT):
                    wsl = ki % 2
                    o("Q1", lambda e, wsl=wsl, ki=ki, hi=hi: e.dma_start(
                        out=self.wf2s[wsl].ap(),
                        in_=self.i_down[hi * D + ki * 128:hi * D + (ki + 1) * 128, :]),
                      reads=["PHASE2", f"wf2s:{wsl}:f"], writes=[f"wf2s:{wsl}"],
                      tag=f"shw2:{sl_i}:{hi}:{ki}")
                    for dt in range(NDT):
                        o("T", lambda e, wsl=wsl, dt=dt, ki=ki, ps=banks[dt]: e.matmul(
                            ps[:, 0:SLH], self.wf2s[wsl][:, dt * 128:(dt + 1) * 128],
                            self.mgrp[:, ki, 0:SLH], start=(ki == 0), stop=(ki == NDT - 1)),
                          reads=[f"m:{ki}", f"wf2s:{wsl}"] +
                                ([bkey[dt] + ":f"] if ki == 0 else [bkey[dt]]),
                          writes=[bkey[dt]] + ([f"wf2s:{wsl}:f"] if dt == NDT - 1 else []),
                          tag=f"shf2:{sl_i}:{hi}:{ki}:{dt}")
                for dt in range(NDT):
                    if hi == 0:
                        o("V", lambda e, dt=dt, ps=banks[dt]: e.tensor_copy(
                            self.f2st[:, dt, 0:SLH], ps[:, 0:SLH]),
                          reads=[bkey[dt]], writes=[f"f2st:{dt}", bkey[dt] + ":f"],
                          tag=f"shf2e:{sl_i}:{hi}:{dt}")
                    else:
                        o("V", lambda e, dt=dt, ps=banks[dt]: e.tensor_tensor(
                            self.f2st[:, dt, 0:SLH], self.f2st[:, dt, 0:SLH],
                            ps[:, 0:SLH], A.add),
                          reads=[f"f2st:{dt}", bkey[dt]],
                          writes=[f"f2st:{dt}", bkey[dt] + ":f"],
                          tag=f"shf2e:{sl_i}:{hi}:{dt}")
            self._scatter(f"sh{sl_i}", sl_i, SLH, False)

    def _scatter(self, etag, s, SLE, indirect):
        o = self.op
        for t2 in range(SLE // 128):
            for dt in range(NDT):
                ps = [self.psT0, self.psT1][dt % 2]
                pk = ["psT0", "psT1"][dt % 2]
                o("T", lambda e, dt=dt, t2=t2, ps=ps: e.transpose(
                    ps[:, 0:128], self.f2st[:, dt, t2 * 128:(t2 + 1) * 128], self.eye[:]),
                  reads=[f"f2st:{dt}", "eye", pk + ":f"], writes=[pk],
                  tag=f"str:{etag}:{t2}:{dt}")
                eng = "V" if dt % 2 == 0 else "S"
                fn = (lambda e, dt=dt, ps=ps: e.tensor_copy(
                    self.sc2tm[:, dt * 128:(dt + 1) * 128], ps[:, 0:128]))                     if eng == "V" else                     (lambda e, dt=dt, ps=ps: e.copy(
                        out=self.sc2tm[:, dt * 128:(dt + 1) * 128], in_=ps[:, 0:128]))
                o(eng, fn, reads=[pk], writes=[f"sc2tm:{dt}", pk + ":f"],
                  tag=f"strev:{etag}:{t2}:{dt}")
            if indirect:
                tl = 2 * s + t2
                o("QG", lambda e, tl=tl: e.indirect_dma_start(
                    out=self.acctm, out_offset=IOA(self.idxc[:, tl % 4:tl % 4 + 1], 0),
                    in_=self.sc2tm[:, :], in_offset=None, compute_op=A.add),
                  reads=[f"sc2tm:{dt2}" for dt2 in range(NDT)] + [f"idxc:{tl % 4}"],
                  writes=["acctm"], tag=f"sca:{etag}:{t2}")
            else:
                t0 = s * SLE + t2 * 128
                o("QG", lambda e, t0=t0: e.dma_start(
                    out=self.acctm[t0:t0 + 128, :], in_=self.sc2tm[:, :],
                    accum_op=A.add),
                  reads=[f"sc2tm:{dt2}" for dt2 in range(NDT)], writes=["acctm"],
                  tag=f"scd:{etag}:{t2}")

    def ln2_out(self):
        o = self.op
        T_ = self.T
        for c in range(T_ // 128):
            sl = c % 2
            o("Q0", lambda e, c=c, sl=sl: e.dma_start(
                out=self.x2c[:, sl, :], in_=self.x2tm[c * 128:(c + 1) * 128, :]),
              reads=["x2tm", "PHASE3"], writes=[f"x2c:{sl}"], tag=f"o_lx:{c}")
            o("Q1", lambda e, c=c, sl=sl: e.dma_start(
                out=self.accc[:, sl, :], in_=self.acctm[c * 128:(c + 1) * 128, :]),
              reads=["acctm", "PHASE3"], writes=[f"accc:{sl}"], tag=f"o_la:{c}")
            o("V", lambda e, sl=sl: e.tensor_tensor(
                self.accc[:, sl, :], self.accc[:, sl, :], self.x2c[:, sl, :], A.add),
              reads=[f"x2c:{sl}", f"accc:{sl}", "PHASE3"], writes=[f"accc:{sl}"],
              tag=f"o_add:{c}")
            L = self.lnc
            o("S", lambda e, sl=sl: e.activation(
                out=self.scr2[:, :], in_=self.accc[:, sl, :], func=ACT.Copy,
                accum_out=L[:, 0:1]),
              reads=[f"accc:{sl}", "PHASE3"], writes=["scr2", "lc0"], tag=f"o_sum:{c}")
            o("S", lambda e, sl=sl: e.activation(
                out=self.scr2[:, :], in_=self.accc[:, sl, :], func=ACT.Square,
                accum_out=L[:, 1:2]),
              reads=[f"accc:{sl}"], writes=["scr2", "lc1"], tag=f"o_sq:{c}")
            sc = 1.0 / D
            o("V", lambda e: e.tensor_scalar(out=L[:, 2:3], in0=L[:, 0:1], scalar1=sc,
                scalar2=None, op0=A.mult),
              reads=["lc0"], writes=["lc2"], tag=f"o_mu:{c}")
            o("V", lambda e: e.tensor_scalar(out=L[:, 3:4], in0=L[:, 1:2], scalar1=sc,
                scalar2=None, op0=A.mult),
              reads=["lc1"], writes=["lc3"], tag=f"o_ex2:{c}")
            o("V", lambda e: e.tensor_tensor(L[:, 4:5], L[:, 2:3], L[:, 2:3], A.mult),
              reads=["lc2"], writes=["lc4"], tag=f"o_m2:{c}")
            o("V", lambda e: e.tensor_tensor(L[:, 4:5], L[:, 3:4], L[:, 4:5], A.subtract),
              reads=["lc3", "lc4"], writes=["lc4"], tag=f"o_var:{c}")
            self.rstd_cols(L, f"o_rs:{c}", 4, 5, eps=1e-5)
            o("V", lambda e, sl=sl: e.tensor_scalar(
                out=self.otm[:, sl, :], in0=self.accc[:, sl, :],
                scalar1=L[:, 2:3], scalar2=L[:, 5:6], op0=A.subtract, op1=A.mult),
              reads=[f"accc:{sl}", "lc2", "lc5", "PHASE3"], writes=[f"otm:{sl}"],
              tag=f"o_nrm:{c}")
            o("Q0", lambda e, c=c, sl=sl: e.dma_start(
                out=self.o_out[c * 128:(c + 1) * 128, :], in_=self.otm[:, sl, :]),
              reads=[f"otm:{sl}"], writes=[f"outdma:{c}"], tag=f"o_dma:{c}")

    def rstd_cols(self, L, tag, vrow, orow, eps, pre="lc"):
        o = self.op
        o("S", lambda e: e.activation(out=L[:, 6:7], in_=L[:, vrow:vrow + 1],
                                      func=ACT.Sqrt, bias=self.epsc.ap()),
          reads=[f"{pre}{vrow}", "epsc"], writes=[pre + "6"], tag=f"{tag}:sqrt")
        o("V", lambda e: e.tensor_scalar(out=L[:, 7:8], in0=L[:, vrow:vrow + 1],
            scalar1=eps, scalar2=None, op0=A.add),
          reads=[f"{pre}{vrow}"], writes=[pre + "7"], tag=f"{tag}:va")
        o("V", lambda e: e.reciprocal(L[:, 8:9], L[:, 6:7]),
          reads=[pre + "6"], writes=[pre + "8"], tag=f"{tag}:r0")
        o("V", lambda e: e.tensor_tensor(L[:, 7:8], L[:, 7:8], L[:, 8:9], A.mult),
          reads=[pre + "7", pre + "8"], writes=[pre + "7"], tag=f"{tag}:t")
        o("V", lambda e: e.tensor_tensor(L[:, 7:8], L[:, 7:8], L[:, 6:7], A.add),
          reads=[pre + "7", pre + "6"], writes=[pre + "7"], tag=f"{tag}:s1")
        o("V", lambda e: e.tensor_scalar(out=L[:, 7:8], in0=L[:, 7:8], scalar1=0.5,
            scalar2=None, op0=A.mult),
          reads=[pre + "7"], writes=[pre + "7"], tag=f"{tag}:s1b")
        o("V", lambda e: e.reciprocal(L[:, orow:orow + 1], L[:, 7:8]),
          reads=[pre + "7"], writes=[f"{pre}{orow}"], tag=f"{tag}:rstd")

    def build_full(self):
        self.build_attn()
        self.alloc_ffn()
        skeys = [f"s:{c}:{dt}" for c in range(self.NC) for dt in range(NDT)]
        self.barrier("PHASE2", skeys)
        self.ln1()
        self.router()
        self.index_build()
        self.moe()
        self.barrier("PHASE3", ["acctm"])
        self.ln2_out()
        self.terminal()
        self.finish()

def build_full_test(T):
    mk = MK(T)
    mk.build_full()
    return mk


def build_attn_test(T):
    mk = MK(T)
    mk.build_attn()
    mk.dump_sfm()
    mk.terminal()
    mk.finish()
    return mk


def kernel(**inputs):
    B, T = 8, 2048
    hp = host_prep(inputs, T)
    x = np.asarray(inputs["L0_attn_x"], dtype=np.float32)
    mk = MK(T)
    mk.build_full()
    in_maps = []
    for b in range(B):
        m = dict(hp)
        m["x"] = np.ascontiguousarray(x[b])
        in_maps.append(m)
    res = run_bass_kernel_spmd(mk.nc, in_maps, core_ids=list(range(B)))
    out = np.stack([np.asarray(r["out"]) for r in res.results], axis=0)
    return out.astype(np.float32)


# version 09
# speedup vs baseline: 1.3848x; 1.0337x over previous
import numpy as np
import concourse.bass as bass
import concourse.mybir as mybir
from concourse.bass_core_rust import IndirectOffsetOnAxis as IOA
from concourse.bass_utils import run_bass_kernel_spmd

F32 = mybir.dt.float32
A = mybir.AluOpType
ACT = mybir.ActivationFunctionType
AX = mybir.AxisListType


NLANES = 8

class Op:
    __slots__ = ("eng", "fn", "reads", "writes", "seq", "need_inc", "waits", "soft", "tag", "lane")
    def __init__(self, eng, fn, reads, writes, tag=""):
        self.eng = eng
        self.fn = fn
        self.reads = reads
        self.writes = writes
        self.seq = None
        self.need_inc = False
        self.waits = []
        self.soft = []
        self.tag = tag
        self.lane = 0

_INC = {"T": 1, "V": 1, "S": 1, "G": 1, "Q0": 16, "Q1": 16, "QG": 16}
_QENG = {"Q0": "sync", "Q1": "scalar", "QG": "gpsimd", "T": "tensor", "V": "vector",
         "S": "scalar", "G": "gpsimd"}

class Sched:
    def __init__(self):
        self.ops = []
        self.last_writer = {}
        self.readers = {}

    def add(self, eng, fn, reads=(), writes=(), tag=""):
        op = Op(eng, fn, list(reads), list(writes), tag)
        deps = []
        for r in op.reads:
            w = self.last_writer.get(r)
            if w is not None:
                deps.append(w)
        for r in op.writes:
            w = self.last_writer.get(r)
            if w is not None:
                deps.append(w)
            deps.extend(self.readers.get(r, ()))
        seen = set()
        for d in deps:
            if d is op or id(d) in seen:
                continue
            seen.add(id(d))
            if d.eng == op.eng and d.eng == "T":
                continue
            d.need_inc = True
            op.waits.append(d)
        for r in op.reads:
            self.readers.setdefault(r, []).append(op)
        for r in op.writes:
            self.last_writer[r] = op
            self.readers[r] = []
        self.ops.append(op)
        return op

    def finalize(self):
        counters = {k: 0 for k in _INC}
        for op in self.ops:
            if op.eng in ("Q0", "Q1", "QG"):
                op.need_inc = True
                n = counters[op.eng]
                op.lane = n % NLANES
                op.seq = n // NLANES + 1
                counters[op.eng] = n + 1
            elif op.need_inc:
                counters[op.eng] += 1
                op.seq = counters[op.eng]
        self.total = dict(counters)

    def emit(self, nc, sems):
        per_eng = {}
        for op in self.ops:
            per_eng.setdefault(_QENG[op.eng], []).append(op)
        def semkey(op):
            return (op.eng, op.lane) if op.eng in ("Q0", "Q1", "QG") else op.eng
        with nc.Block() as block:
            def run(engine_obj, key):
                done = {}
                def wait(sk, tgt):
                    if done.get(sk, 0) < tgt:
                        engine_obj.wait_ge(sems[sk], tgt)
                        done[sk] = tgt
                for op in per_eng.get(key, ()):
                    for d in op.waits:
                        wait(semkey(d), d.seq * _INC[d.eng])
                    for d in op.soft:
                        if d.need_inc:
                            wait(semkey(d), d.seq * _INC[d.eng])
                    if op.eng in ("Q0", "Q1", "QG") and op.seq > 1:
                        wait(semkey(op), (op.seq - 1) * 16)
                    ins = op.fn(engine_obj)
                    if op.need_inc:
                        assert ins is not None, f"op {op.tag} needs inc but fn returned None"
                        ins.then_inc(sems[semkey(op)], _INC[op.eng])
            @block.sync
            def _(e):
                run(e, "sync")
            @block.tensor
            def _(e):
                run(e, "tensor")
            @block.vector
            def _(e):
                run(e, "vector")
            @block.scalar
            def _(e):
                run(e, "scalar")
            @block.gpsimd
            def _(e):
                run(e, "gpsimd")

D = 1024
HN = 8
NH = 32
M = 256
E = 8
ISH = 2048
NDT = 8
CT = 128
TS = 16
TB = 8
RW = 4
SLAB = 256
KG = 4

def perm256():
    p = np.zeros(256, dtype=np.int64)
    for h in range(HN):
        for i in range(NH):
            p[(h // 4) * 128 + i * 4 + (h % 4)] = h * NH + i
    return p

def host_prep(inputs, Tcore):
    P = perm256()
    f = np.ascontiguousarray
    d = {}
    d["WinT"] = f(inputs["L0_attn_W_in"].T)
    d["WkT"] = f(inputs["L0_attn_W_k"].T)
    d["WqT"] = f(inputs["L0_attn_W_q"].T)
    d["WvT"] = f(inputs["L0_attn_W_v"].T[:, P])
    d["WbT"] = f(inputs["L0_attn_W_beta"].T[:, P])
    d["bbeta"] = f(inputs["L0_attn_b_beta"].reshape(1, 256)[:, P])
    d["WoT"] = f(inputs["L0_attn_W_out"].T[P, :])
    d["wrout"] = f(inputs["L0_ffn_w_router"])
    d["fc1"] = f(inputs["L0_ffn_fc1_w"])
    d["fc2"] = f(inputs["L0_ffn_fc2_w"])
    d["gate"] = f(inputs["L0_ffn_gate_w"])
    d["up"] = f(inputs["L0_ffn_up_w"])
    d["down"] = f(inputs["L0_ffn_down_w"])
    d["eye"] = np.eye(128, dtype=np.float32)
    sel = np.zeros((2, 8, 128), dtype=np.float32)
    for hf in range(2):
        for pp in range(128):
            sel[hf, 4 * hf + (pp % 4), pp] = 1.0
    d["sel"] = f(sel.reshape(16, 128))
    selk = np.zeros((2, 8, 128), dtype=np.float32)
    for ft in range(2):
        for pp in range(128):
            selk[ft, ft * 4 + pp // 32, pp] = 1.0
    d["selk"] = f(selk.reshape(16, 128))
    bo = np.zeros((2, 128, 8), dtype=np.float32)
    for ft in range(2):
        for pp in range(128):
            bo[ft, pp, ft * 4 + pp // 32] = 1.0
    d["bones"] = f(bo.reshape(256, 8))
    d["ones128"] = np.ones((128, 1), dtype=np.float32)
    d["epsc"] = np.full((128, 1), 1e-5, dtype=np.float32)
    lt = np.tril(np.ones((128, 128), dtype=np.float32), k=-1)
    d["ltri"] = np.ascontiguousarray(lt)
    NT = Tcore // 128
    d["iotf"] = np.ascontiguousarray(
        (np.arange(Tcore, dtype=np.float32).reshape(NT, 128).T))
    NSLAB = (2 * Tcore + 8 * 255 + 255) // 256
    d["thr"] = np.ascontiguousarray(
        (np.arange(NSLAB, dtype=np.float32) * 256).reshape(1, NSLAB))
    kdp = np.zeros((128, 8), dtype=np.float32)
    for p in range(128):
        for kd in range(8):
            kdp[p, kd] = 128 * kd + p
    d["kdp"] = kdp
    es = np.zeros((8, 8, 128), dtype=np.float32)
    for e in range(8):
        es[e, e, :] = 1.0
    d["esel"] = np.ascontiguousarray(es.transpose(1, 0, 2).reshape(8, 8 * 128))
    d["onesrow"] = np.ones((1, 128), dtype=np.float32)
    d["zeros"] = np.zeros((128, 32), dtype=np.float32)
    return d


class MK:
    def __init__(self, T, stop_after=None):
        self.T = T
        self.NC = T // CT
        self.stop_after = stop_after
        nc = bass.Bass("TRN2", target_bir_lowering=False)
        self.nc = nc
        self.sch = Sched()
        self.dbg = {}
        T_ = T

        def dram(name, shape, dtype=F32):
            return nc.dram_tensor(name, list(shape), dtype, kind="ExternalInput").ap()
        self.i_x = dram("x", (T_, D))
        self.i_WinT = dram("WinT", (D, D))
        self.i_WkT = dram("WkT", (D, M)); self.i_WqT = dram("WqT", (D, M))
        self.i_WvT = dram("WvT", (D, M)); self.i_WbT = dram("WbT", (D, M))
        self.i_bbeta = dram("bbeta", (1, M))
        self.i_WoT = dram("WoT", (M, D))
        self.i_wrout = dram("wrout", (D, E))
        self.i_fc1 = dram("fc1", (E, D, ISH))
        self.i_fc2 = dram("fc2", (E, D, D))
        self.i_gate = dram("gate", (D, ISH)); self.i_up = dram("up", (D, ISH))
        self.i_down = dram("down", (ISH, D))
        self.i_eye = dram("eye", (128, 128))
        self.i_sel = dram("sel", (16, 128)); self.i_selk = dram("selk", (16, 128))
        self.i_bones = dram("bones", (256, 8))
        self.i_ones128 = dram("ones128", (128, 1))
        self.i_onesrow = dram("onesrow", (1, 128))
        self.i_zeros = dram("zeros", (128, 32))
        self.i_esel = dram("esel", (8, 8 * 128))
        self.i_ltri = dram("ltri", (128, 128))
        self.i_iotf = dram("iotf", (128, T_ // 128))
        self.NSLAB = (2 * T_ + 8 * 255 + 255) // 256
        self.NS = 256 * self.NSLAB
        self.i_kdp = dram("kdp", (128, 8))
        self.i_thr = dram("thr", (1, self.NSLAB))
        self.i_epsc = dram("epsc", (128, 1))
        self.o_out = nc.dram_tensor("out", [T_, D], F32, kind="ExternalOutput").ap()
        self.x2tm = nc.dram_tensor("x2tm_i", [T_ + 128, D], F32, kind="Internal").ap()
        self.outtm = nc.dram_tensor("outtm_i", [self.NS + 128, D], F32, kind="Internal").ap()
        self.shtm = nc.dram_tensor("shtm_i", [T_, D], F32, kind="Internal").ap()
        self.t2s = nc.dram_tensor("t2s_i", [T_, 2], F32, kind="Internal").ap()
        NPAD = self.NS + 256
        self.NPAD = NPAD
        self.idxg = nc.dram_tensor("idxg_i", [NPAD, 2], F32, kind="Internal").ap()

        a = nc.alloc_sbuf_tensor
        self.s_fm = a("s_fm", [128, NDT, T_], F32)
        self.WinT = a("WinT_s", [128, NDT, D], F32)
        self.WkT = a("WkT_s", [128, NDT, M], F32)
        self.WqT = a("WqT_s", [128, NDT, M], F32)
        self.WvT = a("WvT_s", [128, NDT, M], F32)
        self.WbT = a("WbT_s", [128, NDT, M], F32)
        self.WoT = a("WoT_s", [128, 2, D], F32)
        self.bbeta = a("bbeta_s", [1, M], F32)
        self.wrout = a("wrout_s", [128, NDT, E], F32)
        self.eye = a("eye_s", [128, 128], F32)
        self.sel = [a("sel0_s", [8, 128], F32), a("sel1_s", [8, 128], F32)]
        self.selk = [a("selk0_s", [8, 128], F32), a("selk1_s", [8, 128], F32)]
        self.bones = [a("bones0_s", [128, 8], F32), a("bones1_s", [128, 8], F32)]
        self.ones128 = a("ones128_s", [128, 1], F32)
        self.onesrow = a("onesrow_s", [1, 128], F32)
        self.zeroS = a("zeroS_s", [128, 32], F32)
        self.esel = a("esel_s", [8, 8 * 128], F32)
        self.ltri = a("ltri_s", [128, 128], F32)
        self.iotf = a("iotf_s", [128, T_ // 128], F32)
        self.kdp = a("kdp_s", [128, 8], F32)
        self.thr = a("thr_s", [1, self.NSLAB], F32)
        self.epsc = a("epsc_s", [128, 1], F32)
        self.xin = a("xin", [128, 1, D], F32)
        NW = 3
        self.NW = NW
        self.xw = a("xw", [128, NW, NDT, CT], F32)
        self.xp = a("xp", [128, 1, NDT, CT], F32)
        self.k_sb = a("k_sb", [128, 2, CT], F32)
        self.ksq = a("ksq", [128, 2, CT], F32)
        self.q_sb = a("q_sb", [128, 2, 2, CT], F32)
        self.kn_fm = a("kn_fm", [128, 2, 2, CT], F32)
        self.v_fm = a("v_fm", [128, 2, 2, CT], F32)
        self.b_fm = a("b_fm", [128, 2, 2, CT], F32)
        self.rb_fm = a("rb_fm", [128, 2, 2, CT], F32)
        self.nrm = a("nrm", [8, 3, CT], F32)
        self.knq8 = a("knq8", [8, 2, 2, 32, TS], F32)
        self.bcw = a("bcw", [128, 2, 2, 2, TB, 32], F32)
        self.hist = a("hist", [128, 2, RW, TB, 32], F32)
        self.m_scr = a("m_scr", [128, 2, 32], F32)
        self.inner = a("inner", [128, 2, 32], F32)
        self.dcol = a("dcol", [128, 2, 1], F32)
        self.gcol = a("gcol", [128, 2, 1], F32)
        self.Sq = a("Sq", [128, 1, 2, CT], F32)
        self.gq_scr = a("gq_scr", [128, 1, TB, 32], F32)
        self.cell = a("cell", [128, 2, 2, CT], F32)
        p = nc.alloc_psum_tensor
        self.psT0 = p("psT0", [128, 512], F32)
        self.psT1 = p("psT1", [128, 512], F32)
        self.psA = p("psA", [128, 512], F32)
        self.psB = p("psB", [128, 512], F32)
        self.psW = [[p(f"psW{hf}{j}", [128, 512], F32) for j in range(2)] for hf in range(2)]

        self.sems = {}

    def op(self, eng, fn, reads=(), writes=(), tag=""):
        return self.sch.add(eng, fn, reads, writes, tag)

    def load_weights(self):
        o = self.op
        def ld(dst_ap, src_ap, key):
            o("Q0", lambda e, d=dst_ap, s=src_ap: e.dma_start(out=d, in_=s),
              writes=[key], tag=f"ld:{key}")
        ld(self.WinT.ap(), self.i_WinT.rearrange("(kt p) e -> p kt e", p=128), "WinT")
        ld(self.WkT.ap(), self.i_WkT.rearrange("(kt p) m -> p kt m", p=128), "WkT")
        ld(self.WqT.ap(), self.i_WqT.rearrange("(kt p) m -> p kt m", p=128), "WqT")
        ld(self.WvT.ap(), self.i_WvT.rearrange("(kt p) m -> p kt m", p=128), "WvT")
        ld(self.WbT.ap(), self.i_WbT.rearrange("(kt p) m -> p kt m", p=128), "WbT")
        ld(self.WoT.ap(), self.i_WoT.rearrange("(mt p) d -> p mt d", p=128), "WoT")
        ld(self.bbeta.ap(), self.i_bbeta, "bbeta")
        ld(self.wrout.ap(), self.i_wrout.rearrange("(kt p) e -> p kt e", p=128), "wrout")
        ld(self.eye.ap(), self.i_eye, "eye")
        ld(self.sel[0].ap(), self.i_sel[0:8, :], "sel:0")
        ld(self.sel[1].ap(), self.i_sel[8:16, :], "sel:1")
        ld(self.selk[0].ap(), self.i_selk[0:8, :], "selk:0")
        ld(self.selk[1].ap(), self.i_selk[8:16, :], "selk:1")
        ld(self.bones[0].ap(), self.i_bones[0:128, :], "bones:0")
        ld(self.bones[1].ap(), self.i_bones[128:256, :], "bones:1")
        ld(self.ones128.ap(), self.i_ones128, "ones128")
        ld(self.onesrow.ap(), self.i_onesrow, "onesrow")
        ld(self.zeroS.ap(), self.i_zeros, "zeroS")
        ld(self.esel.ap(), self.i_esel, "esel")
        ld(self.ltri.ap(), self.i_ltri, "ltri")
        ld(self.iotf.ap(), self.i_iotf, "iotf")
        ld(self.thr.ap(), self.i_thr, "thr")
        ld(self.kdp.ap(), self.i_kdp, "kdp")
        ld(self.epsc.ap(), self.i_epsc, "epsc")

    def proj_chunk(self, c):
        o = self.op
        sl = c % 2
        t0 = c * CT
        o("Q0", lambda e: e.dma_start(out=self.xin[:, 0, :], in_=self.i_x[t0:t0 + CT, :]),
          reads=[], writes=["xin"], tag=f"ldx:{c}")
        wsl = c % self.NW
        for dt in range(NDT):
            ps = [self.psT0, self.psT1][dt % 2]
            pskey = f"psT{dt % 2}"
            o("T", lambda e, dt=dt, ps=ps: e.transpose(
                ps[:, 0:128], self.xin[:, 0, dt * 128:(dt + 1) * 128], self.eye[:]),
              reads=["xin", "eye", pskey + ":f"], writes=[pskey], tag=f"tr:{c}:{dt}")
            o("S", lambda e, dt=dt, ps=ps: e.copy(
                out=self.xw[:, wsl, dt, :], in_=ps[:, 0:128]),
              reads=[pskey], writes=[f"xw:{wsl}:{dt}", pskey + ":f"], tag=f"trev:{c}:{dt}")
        for et in range(NDT):
            ps = [self.psA, self.psB][et % 2]
            pk = ["psA", "psB"][et % 2]
            for kd in range(NDT):
                o("T", lambda e, et=et, kd=kd, ps=ps: e.matmul(
                    ps[:, 0:CT], self.WinT[:, kd, et * 128:(et + 1) * 128],
                    self.xw[:, wsl, kd, :], start=(kd == 0), stop=(kd == NDT - 1)),
                  reads=[f"xw:{wsl}:{kd}", "WinT"] + ([pk + ":f"] if kd == 0 else [pk]),
                  writes=[pk], tag=f"xp:{c}:{et}:{kd}")
            o("S", lambda e, et=et, ps=ps: e.activation(
                out=self.xp[:, 0, et, :], in_=ps[:, 0:CT], func=ACT.Silu),
              reads=[pk], writes=[f"xp:{et}", pk + ":f"], tag=f"xpev:{c}:{et}")
        projs = [("k", self.WkT, None), ("q", self.WqT, None),
                 ("v", self.WvT, None), ("b", self.WbT, "bias")]
        for nm, W, bias in projs:
            for ft in range(2):
                ps = [self.psA, self.psB][ft]
                pk = ["psA", "psB"][ft]
                for kd in range(NDT):
                    stp = (kd == NDT - 1 and bias is None)
                    o("T", lambda e, W=W, ft=ft, kd=kd, ps=ps, stp=stp: e.matmul(
                        ps[:, 0:CT], W[:, kd, ft * 128:(ft + 1) * 128],
                        self.xp[:, 0, kd, :], start=(kd == 0), stop=stp),
                      reads=[f"xp:{kd}", "W" + nm + "T"] + ([pk + ":f"] if kd == 0 else [pk]),
                      writes=[pk], tag=f"pj{nm}:{c}:{ft}:{kd}")
                if bias is not None:
                    o("T", lambda e, ft=ft, ps=ps: e.matmul(
                        ps[:, 0:CT], self.bbeta.ap()[0:1, ft * 128:(ft + 1) * 128],
                        self.onesrow.ap()[0:1, 0:CT], start=False, stop=True),
                      reads=["bbeta", "onesrow", pk], writes=[pk], tag=f"pjbias:{c}:{ft}")
                if nm == "k":
                    o("V", lambda e, ft=ft, ps=ps: e.tensor_copy(
                        self.k_sb[:, ft, :], ps[:, 0:CT]),
                      reads=[pk], writes=[f"ksb:{ft}", pk + ":f"], tag=f"kev:{c}:{ft}")
                elif nm == "q":
                    o("V", lambda e, ft=ft, ps=ps: e.tensor_copy(
                        self.q_sb[:, sl, ft, :], ps[:, 0:CT]),
                      reads=[pk], writes=[f"qsb:{sl}:{ft}", pk + ":f"], tag=f"qev:{c}:{ft}")
                elif nm == "v":
                    o("S", lambda e, ft=ft, ps=ps: e.copy(
                        out=self.v_fm[:, sl, ft, :], in_=ps[:, 0:CT]),
                      reads=[pk], writes=[f"vfm:{sl}:{ft}", pk + ":f"], tag=f"vev:{c}:{ft}")
                else:
                    o("S", lambda e, ft=ft, ps=ps: e.activation(
                        out=self.b_fm[:, sl, ft, :], in_=ps[:, 0:CT], func=ACT.Sigmoid),
                      reads=[pk], writes=[f"bfm:{sl}:{ft}", pk + ":f"], tag=f"bev:{c}:{ft}")
                    o("V", lambda e, ft=ft: e.reciprocal(
                        self.rb_fm[:, sl, ft, :], self.b_fm[:, sl, ft, :]),
                      reads=[f"bfm:{sl}:{ft}"], writes=[f"rbfm:{sl}:{ft}"], tag=f"rbev:{c}:{ft}")
        for ft in range(2):
            o("S", lambda e, ft=ft: e.activation(
                out=self.ksq[:, ft, :], in_=self.k_sb[:, ft, :], func=ACT.Square),
              reads=[f"ksb:{ft}"], writes=[f"ksq:{ft}"], tag=f"ksq:{c}:{ft}")
        for ft in range(2):
            o("T", lambda e, ft=ft: e.matmul(
                self.psT1[0:8, 0:CT], self.bones[ft][:],
                self.ksq[:, ft, :], start=(ft == 0), stop=(ft == 1)),
              reads=[f"ksq:{ft}", f"bones:{ft}"] + (["psT1:f"] if ft == 0 else ["psT1"]),
              writes=["psT1"], tag=f"knrm:{c}:{ft}")
        nr = self.nrm
        o("S", lambda e: e.activation(out=nr[:, 1, :], in_=self.psT1[0:8, 0:CT], func=ACT.Sqrt),
          reads=["psT1"], writes=["nrm1"], tag=f"knsqrt:{c}")
        o("V", lambda e: e.tensor_copy(nr[:, 0, :], self.psT1[0:8, 0:CT]),
          reads=["psT1"], writes=["nrm0", "psT1:f"], tag=f"knss:{c}")
        o("V", lambda e: e.reciprocal(nr[:, 2, :], nr[:, 1, :]),
          reads=["nrm1"], writes=["nrm2"], tag=f"knr1:{c}")
        o("V", lambda e: e.tensor_tensor(nr[:, 2, :], nr[:, 0, :], nr[:, 2, :], A.mult),
          reads=["nrm0", "nrm2"], writes=["nrm2"], tag=f"knt:{c}")
        o("V", lambda e: e.tensor_tensor(nr[:, 0, :], nr[:, 1, :], nr[:, 2, :], A.add),
          reads=["nrm1", "nrm2"], writes=["nrm0"], tag=f"kns1:{c}")
        o("V", lambda e: e.tensor_scalar(
            out=nr[:, 0, :], in0=nr[:, 0, :], scalar1=0.5, scalar2=1e-6, op0=A.mult, op1=A.add),
          reads=["nrm0"], writes=["nrm0"], tag=f"kns2:{c}")
        o("V", lambda e: e.reciprocal(nr[:, 2, :], nr[:, 0, :]),
          reads=["nrm0"], writes=["nrm5"], tag=f"knr2:{c}")
        for ft in range(2):
            o("T", lambda e, ft=ft: e.matmul(
                self.psT1[:, 0:CT], self.selk[ft][:],
                nr[:, 2, :], start=True, stop=True),
              reads=["nrm5", f"selk:{ft}", "psT1:f"], writes=["psT1"], tag=f"knbc:{c}:{ft}")
            o("V", lambda e, ft=ft: e.tensor_tensor(
                self.kn_fm[:, sl, ft, :], self.k_sb[:, ft, :], self.psT1[:, 0:CT], A.mult),
              reads=["psT1", f"ksb:{ft}"], writes=[f"knfm:{sl}:{ft}", "psT1:f"],
              tag=f"knmul:{c}:{ft}")

    def scan_chunk(self, c):
        o = self.op
        sl = c % 2
        NWIN = CT // TB
        NSG = CT // TS
        def rsh(sg):
            ss = sg % 2
            for kq, (src_t, skey) in enumerate(
                    [(self.kn_fm, "knfm"), (self.q_sb, "qsb")]):
                for ft in range(2):
                    o("Q0", lambda e, kq=kq, ft=ft, src_t=src_t, ss=ss, sg=sg: e.dma_start(
                        out=self.knq8[ft * 4:(ft + 1) * 4, ss, kq, :, :],
                        in_=src_t[:, sl, ft, sg * TS:(sg + 1) * TS]),
                      reads=[f"{skey}:{sl}:{ft}"], writes=[f"knq8:{ss}"],
                      tag=f"rsh:{c}:{sg}:{kq}:{ft}")
        def prep(w):
            sg = w // (TS // TB)
            wl = w % (TS // TB)
            ss = sg % 2
            wp = w % 2
            for hf in range(2):
                ps = self.psW[hf][wp]
                pk = f"psW{hf}{wp}"
                o("T", lambda e, hf=hf, ps=ps, wl=wl, ss=ss: e.matmul(
                    ps[:, 0:512], self.sel[hf][:],
                    self.knq8[0:8, ss, :, :, wl * TB:(wl + 1) * TB],
                    start=True, stop=True),
                  reads=[f"knq8:{ss}", f"sel:{hf}", pk + ":f"], writes=[pk],
                  tag=f"wbc:{c}:{w}:{hf}")
                o("S", lambda e, hf=hf, ps=ps, wp=wp: e.copy(
                    out=self.bcw[:, hf, wp, :, :, :],
                    in_=ps[:, 0:512].rearrange("p (kq j t) -> p kq j t", kq=2, j=32).transpose([0, 1, 3, 2])),
                  reads=[pk], writes=[f"bcw:{hf}:{wp}", pk + ":f"],
                  tag=f"wbcev:{c}:{w}:{hf}")
        rsh(0)
        prep(0)
        for sg in range(NSG):
            if sg + 1 < NSG:
                rsh(sg + 1)
            for wl in range(TS // TB):
                w = sg * (TS // TB) + wl
                if w + 1 < NWIN:
                    prep(w + 1)
                W = c * NWIN + w
                wp = W % 2
                wr = W % RW
                u0 = w * TB
                for t in range(TB):
                    u = u0 + t
                    for hf in range(2):
                        if W == 0 and t == 0 and c == 0:
                            prev = self.zeroS[:, :]
                            prevkey = "zeroS"
                        elif t == 0:
                            prev = self.hist[:, hf, (W - 1) % RW, TB - 1, :]
                            prevkey = f"hist:{hf}:{(W - 1) % RW}"
                        else:
                            prev = self.hist[:, hf, wr, t - 1, :]
                            prevkey = f"hist:{hf}:{wr}"
                        knsl = self.bcw[:, hf, wp, 0, t, :]
                        vcol = self.v_fm[:, sl, hf, u:u + 1]
                        rbcol = self.rb_fm[:, sl, hf, u:u + 1]
                        bcol = self.b_fm[:, sl, hf, u:u + 1]
                        o("V", lambda e, hf=hf, prev=prev, knsl=knsl: e.scalar_tensor_tensor(
                            out=self.m_scr[:, hf, :], in0=prev, scalar=-1.0, in1=knsl,
                            op0=A.mult, op1=A.mult, accum_out=self.dcol[:, hf, :]),
                          reads=[prevkey, f"bcw:{hf}:{wp}", f"mscr:{hf}:f"],
                          writes=[f"mscr:{hf}", f"dcol:{hf}"], tag=f"ttr:{u}:{hf}")
                        o("V", lambda e, hf=hf, vcol=vcol, rbcol=rbcol: e.tensor_scalar(
                            out=self.gcol[:, hf, :], in0=self.dcol[:, hf, :],
                            scalar1=vcol, scalar2=rbcol, op0=A.add, op1=A.mult),
                          reads=[f"dcol:{hf}", f"vfm:{sl}:{hf}", f"rbfm:{sl}:{hf}"],
                          writes=[f"gcol:{hf}"], tag=f"gc:{u}:{hf}")
                        o("V", lambda e, hf=hf, prev=prev, knsl=knsl: e.scalar_tensor_tensor(
                            out=self.inner[:, hf, :], in0=knsl, scalar=self.gcol[:, hf, :],
                            in1=prev, op0=A.mult, op1=A.add),
                          reads=[prevkey, f"bcw:{hf}:{wp}", f"gcol:{hf}"],
                          writes=[f"inner:{hf}"], tag=f"stt:{u}:{hf}")
                        o("S", lambda e, hf=hf, wr=wr, t=t, bcol=bcol: e.activation(
                            out=self.hist[:, hf, wr, t, :], in_=self.inner[:, hf, :],
                            func=ACT.Tanh, scale=bcol),
                          reads=[f"inner:{hf}", f"bfm:{sl}:{hf}"],
                          writes=[f"hist:{hf}:{wr}", f"mscr:{hf}:f"], tag=f"tanh:{u}:{hf}")
                for hf in range(2):
                    o("G", lambda e, hf=hf, wr=wr, wp=wp: e.tensor_tensor(
                        self.gq_scr[:, 0, :, :], self.hist[:, hf, wr, :, :],
                        self.bcw[:, hf, wp, 1, :, :], A.mult),
                      reads=[f"hist:{hf}:{wr}", f"bcw:{hf}:{wp}", "gqf"],
                      writes=["gq"], tag=f"mq:{c}:{w}:{hf}")
                    o("V", lambda e, hf=hf, u0=u0: e.tensor_reduce(
                        out=self.Sq[:, 0, hf, u0:u0 + TB],
                        in_=self.gq_scr[:, 0, :, :],
                        axis=AX.X, op=A.add),
                      reads=["gq"], writes=[f"Sq:{hf}", "gqf"], tag=f"sqr:{c}:{w}:{hf}")

    def wout_chunk(self, c):
        o = self.op
        sl = c % 2
        wsl = c % self.NW
        for hf in range(2):
            o("S", lambda e, hf=hf: e.activation(
                out=self.cell[:, sl, hf, :], in_=self.Sq[:, 0, hf, :], func=ACT.Sigmoid),
              reads=[f"Sq:{hf}"], writes=[f"cell:{sl}:{hf}"], tag=f"sgm:{c}:{hf}")
            o("V", lambda e, hf=hf: e.tensor_tensor(
                self.cell[:, sl, hf, :], self.cell[:, sl, hf, :], self.Sq[:, 0, hf, :], A.mult),
              reads=[f"Sq:{hf}", f"cell:{sl}:{hf}"], writes=[f"cell:{sl}:{hf}"],
              tag=f"cl1:{c}:{hf}")
            o("V", lambda e, hf=hf: e.tensor_tensor(
                self.cell[:, sl, hf, :], self.cell[:, sl, hf, :], self.Sq[:, 0, hf, :], A.mult),
              reads=[f"Sq:{hf}", f"cell:{sl}:{hf}"], writes=[f"cell:{sl}:{hf}"],
              tag=f"cl2:{c}:{hf}")
        for dt in range(NDT):
            ps = [self.psT0, self.psT1][dt % 2]
            pk = ["psT0", "psT1"][dt % 2]
            for mt in range(2):
                o("T", lambda e, dt=dt, mt=mt, ps=ps: e.matmul(
                    ps[:, 0:CT], self.WoT[:, mt, dt * 128:(dt + 1) * 128],
                    self.cell[:, sl, mt, :], start=(mt == 0), stop=(mt == 1)),
                  reads=[f"cell:{sl}:{mt}", "WoT"] + ([pk + ":f"] if mt == 0 else [pk]),
                  writes=[pk], tag=f"wo:{c}:{dt}:{mt}")
            o("V", lambda e, dt=dt, ps=ps: e.tensor_tensor(
                self.s_fm[:, dt, c * CT:(c + 1) * CT], self.xw[:, wsl, dt, :],
                ps[:, 0:CT], A.add),
              reads=[pk, f"xw:{wsl}:{dt}"], writes=[f"s:{c}:{dt}", pk + ":f"],
              tag=f"woev:{c}:{dt}")

    def dump_sfm(self):
        o = self.op
        for dt in range(NDT):
            o("Q0", lambda e, dt=dt: e.dma_start(
                out=self.o_out[0:self.T, dt * 128:(dt + 1) * 128].transpose([1, 0]),
                in_=self.s_fm[:, dt, :], allow_slow_non_contiguous=True),
              reads=[f"s:{c}:{dt}" for c in range(self.NC)], writes=[f"outdma:{dt % (self.T // 128)}"],
              tag=f"dump:{dt}")

    def terminal(self):
        o = self.op
        keys = [f"outdma:{c}" for c in range(self.T // 128)]
        o("G", lambda e: e.nop(), reads=keys, tag="fin")

    def build_attn(self):
        self.load_weights()
        self.proj_chunk(0)
        for c in range(1, self.NC):
            self.proj_chunk(c)
            self.scan_chunk(c - 1)
            self.wout_chunk(c - 1)
        self.scan_chunk(self.NC - 1)
        self.wout_chunk(self.NC - 1)

    def finish(self):
        self.sch.finalize()
        sems = {}
        for k in _INC:
            if k in ("Q0", "Q1", "QG"):
                for ln in range(NLANES):
                    sems[(k, ln)] = self.nc.alloc_semaphore(f"sem{k}_{ln}")
            else:
                sems[k] = self.nc.alloc_semaphore(f"sem{k}")
        self.sch.emit(self.nc, sems)

    def alloc_ffn(self):
        nc = self.nc
        def addr(t):
            return nc.lookup_mloc(t).addr
        def at(name, shape, base, off, dtype=F32):
            return nc.alloc_sbuf_tensor_at(name, shape, dtype, offset=base + off)
        T_ = self.T
        SLE = 256
        self.SLE = SLE
        baseA = addr(self.WinT)
        offA = 0
        def atA(name, shape, nbytes, dtype=F32):
            nonlocal offA
            t = at(name, shape, baseA, offA, dtype)
            offA += (nbytes + 31) & ~31
            return t
        self.xgfm = atA("xgfm", [128, NDT, SLE], NDT * SLE * 4)
        self.wfp = [atA("wfp0", [128, D], D * 4), atA("wfp1", [128, D], D * 4)]
        self.wfg = [atA("wfg0", [128, D], D * 4), atA("wfg1", [128, D], D * 4)]
        self.wf2s = [atA("wf2s0", [128, D], D * 4), atA("wf2s1", [128, D], D * 4)]
        SLH = min(512, T_)
        self.SLH = SLH
        self.mgrp = atA("mgrp", [128, 8, SLH], 8 * SLH * 4)
        self.f2st = atA("f2st", [128, NDT, SLH], NDT * SLH * 4)
        baseB = addr(self.xin)
        offB = 0
        def atB(name, shape, nbytes, dtype=F32):
            nonlocal offB
            t = at(name, shape, baseB, offB, dtype)
            offB += (nbytes + 31) & ~31
            return t
        self.xgtm = atB("xgtm", [128, D], D * 4)
        self.sc2tm = atB("sc2tm", [128, D], D * 4)
        self.x2c = atB("x2c", [128, 2, D], 2 * D * 4)
        self.accc = atB("accc", [128, 2, D], 2 * D * 4)
        self.otm = atB("otm", [128, 2, D], 2 * D * 4)
        self.s1tm = atB("s1tm", [128, D], D * 4)
        self.o1tm = atB("o1tm", [128, D], D * 4)
        self.scr2 = atB("scr2", [128, D], D * 4)
        self.zrow = self.scr2
        self.hgt = atB("hgt", [128, 512], 512 * 4)
        self.gbc = atB("gbc", [128, SLE], SLE * 4)
        self.gpr = atB("gpr", [1, 2, SLE, 2], 2 * SLE * 2 * 4)
        self.gsx = atB("gsx", [1, 2, SLE], 2 * SLE * 4)
        self.lgt = atB("lgt", [128, 8], 8 * 4)
        self.gtm = atB("gtm", [128, T_ // 128, 8], (T_ // 128) * 8 * 4)
        self.rsc = atB("rsc", [128, 16], 16 * 4)
        self.rnk = atB("rnk", [128, 8], 8 * 4)
        self.rnki = atB("rnki", [128, 2], 2 * 4, mybir.dt.int32)
        self.idxc = atB("idxc", [128, 4], 4 * 4, mybir.dt.int32)
        self.idxcf = atB("idxcf", [128, 4, 2], 4 * 2 * 4)
        self.mAB = atB("mAB", [128, 16], 16 * 4)
        self.prd = atB("prd", [128, 8], 8 * 4)
        self.pr2 = atB("pr2", [128, 4], 4 * 4)
        self.rnkf = atB("rnkf", [128, 2], 2 * 4)
        self.asr = atB("asr", [1, 48], 48 * 4)
        self.ter = atB("ter", [1, self.NSLAB], self.NSLAB * 4)
        self.idxeA = atB("idxeA", [128, self.NSLAB, 8], self.NSLAB * 8 * 4, mybir.dt.int32)
        self.bsrow = atB("bsrow", [1, 16], 16 * 4)
        self.tpre = atB("tpre", [128, 8, 2], 8 * 2 * 4)
        self.t2sl = atB("t2sl", [128, 2, 2], 2 * 2 * 4)
        self.idxAB = atB("idxAB", [128, 2, 2], 2 * 2 * 4, mybir.dt.int32)
        self.gab = at("gab", [128, 2, 2, D], addr(self.mgrp), 0)
        self.lncA = atB("lncA", [128, 12], 64)
        self.lnc = atB("lnc", [128, 12], 64)

    def barrier(self, name, keys):
        self.op("V", lambda e: e.engine_nop(), reads=keys, writes=[name], tag=f"bar:{name}")

    def ln1(self):
        o = self.op
        T_ = self.T
        for c in range(T_ // 128):
            for dt in range(NDT):
                ps = [self.psT0, self.psT1][dt % 2]
                pk = ["psT0", "psT1"][dt % 2]
                o("T", lambda e, c=c, dt=dt, ps=ps: e.transpose(
                    ps[:, 0:128], self.s_fm[:, dt, c * 128:(c + 1) * 128], self.eye[:]),
                  reads=[f"s:{c}:{dt}", "eye", pk + ":f"], writes=[pk], tag=f"l1tr:{c}:{dt}")
                o("S", lambda e, dt=dt, ps=ps: e.copy(
                    out=self.s1tm[:, dt * 128:(dt + 1) * 128], in_=ps[:, 0:128]),
                  reads=[pk, "PHASE2"], writes=["s1tm"], tag=f"l1trev:{c}:{dt}")
            L = self.lncA
            o("S", lambda e: e.activation(
                out=self.o1tm[:, :], in_=self.s1tm[:, :], func=ACT.Copy,
                accum_out=L[:, 0:1]),
              reads=["s1tm", "PHASE2"], writes=["o1tm", "la0"], tag=f"l1sum:{c}")
            o("S", lambda e: e.activation(
                out=self.o1tm[:, :], in_=self.s1tm[:, :], func=ACT.Square,
                accum_out=L[:, 1:2]),
              reads=["s1tm"], writes=["o1tm", "la1"], tag=f"l1sq:{c}")
            sc = 1.0 / D
            o("V", lambda e: e.tensor_scalar(out=L[:, 2:3], in0=L[:, 0:1], scalar1=sc,
                scalar2=None, op0=A.mult),
              reads=["la0"], writes=["la2"], tag=f"l1mu:{c}")
            o("V", lambda e: e.tensor_scalar(out=L[:, 3:4], in0=L[:, 1:2], scalar1=sc,
                scalar2=None, op0=A.mult),
              reads=["la1"], writes=["la3"], tag=f"l1ex2:{c}")
            o("V", lambda e: e.tensor_tensor(L[:, 4:5], L[:, 2:3], L[:, 2:3], A.mult),
              reads=["la2"], writes=["la4"], tag=f"l1m2:{c}")
            o("V", lambda e: e.tensor_tensor(L[:, 4:5], L[:, 3:4], L[:, 4:5], A.subtract),
              reads=["la3", "la4"], writes=["la4"], tag=f"l1var:{c}")
            self.rstd_cols(L, f"l1rs:{c}", 4, 5, eps=1e-5, pre="la")
            o("V", lambda e: e.tensor_scalar(
                out=self.o1tm[:, :], in0=self.s1tm[:, :],
                scalar1=L[:, 2:3], scalar2=L[:, 5:6], op0=A.subtract, op1=A.mult),
              reads=["s1tm", "la2", "la5"], writes=["o1tm"], tag=f"l1nrm:{c}")
            o("Q0", lambda e, c=c: e.dma_start(
                out=self.x2tm[c * 128:(c + 1) * 128, :], in_=self.o1tm[:, :]),
              reads=["o1tm"], writes=["x2tm"], tag=f"l1dma:{c}")
            for dt in range(NDT):
                ps = [self.psT0, self.psT1][dt % 2]
                pk = ["psT0", "psT1"][dt % 2]
                o("T", lambda e, dt=dt, ps=ps: e.transpose(
                    ps[:, 0:128], self.o1tm[:, dt * 128:(dt + 1) * 128], self.eye[:]),
                  reads=["o1tm", "eye", pk + ":f"], writes=[pk], tag=f"l1trb:{c}:{dt}")
                o("V", lambda e, c=c, dt=dt, ps=ps: e.tensor_copy(
                    self.s_fm[:, dt, c * 128:(c + 1) * 128], ps[:, 0:128]),
                  reads=[pk], writes=[f"s:{c}:{dt}", pk + ":f"], tag=f"l1trbe:{c}:{dt}")

    def router(self):
        o = self.op
        T_ = self.T
        nt = T_ // 128
        for tt in range(nt):
            cc = tt
            for kd in range(NDT):
                o("T", lambda e, tt=tt, kd=kd: e.matmul(
                    self.psB[:, 0:8], self.s_fm[:, kd, tt * 128:(tt + 1) * 128],
                    self.wrout[:, kd, :], start=(kd == 0), stop=(kd == NDT - 1)),
                  reads=[f"s:{cc}:{kd}", "wrout"] + (["psB:f"] if kd == 0 else ["psB"]),
                  writes=["psB"], tag=f"rt:{tt}:{kd}")
            o("V", lambda e, tt=tt: e.tensor_copy(self.lgt[:, :], self.psB[:, 0:8]),
              reads=["psB", "PHASE2"], writes=["lgt", "psB:f"], tag=f"rtev:{tt}")
            R = self.rsc
            o("V", lambda e, tt=tt: e.max(out=R[:, 0:8], in_=self.lgt[:, :]),
              reads=["lgt", "PHASE2"], writes=["rsc8"], tag=f"rtmax:{tt}")
            o("V", lambda e: e.tensor_tensor(R[:, 8:9], R[:, 0:1], R[:, 1:2], A.subtract),
              reads=["rsc8"], writes=["rscd"], tag=f"rtd:{tt}")
            o("S", lambda e: e.activation(out=R[:, 9:10], in_=R[:, 8:9], func=ACT.Sigmoid),
              reads=["rscd"], writes=["rscs1"], tag=f"rts1:{tt}")
            o("S", lambda e: e.activation(out=R[:, 10:11], in_=R[:, 8:9], func=ACT.Sigmoid,
                                          scale=-1.0),
              reads=["rscd"], writes=["rscs2"], tag=f"rts2:{tt}")
            o("V", lambda e, tt=tt: e.tensor_tensor(
                self.gtm[:, tt, :], self.lgt[:, :],
                R[:, 0:1].to_broadcast([128, 8]), A.is_equal),
              reads=["lgt", "rsc8"], writes=[f"gtm:{tt}"], tag=f"rte1:{tt}")
            o("V", lambda e, tt=tt: e.tensor_scalar(
                out=self.gtm[:, tt, :], in0=self.gtm[:, tt, :], scalar1=R[:, 9:10],
                scalar2=None, op0=A.mult),
              reads=[f"gtm:{tt}", "rscs1"], writes=[f"gtm:{tt}"], tag=f"rtg1:{tt}")
            o("V", lambda e, tt=tt: e.tensor_tensor(
                R[:, 0:8], self.lgt[:, :],
                R[:, 1:2].to_broadcast([128, 8]), A.is_equal),
              reads=["lgt", "rsc8"], writes=["rsc8b"], tag=f"rte2:{tt}")
            o("V", lambda e: e.tensor_scalar(
                out=R[:, 0:8], in0=R[:, 0:8], scalar1=R[:, 10:11],
                scalar2=None, op0=A.mult),
              reads=["rsc8b", "rscs2"], writes=["rsc8b"], tag=f"rtg2:{tt}")
            o("V", lambda e, tt=tt: e.tensor_tensor(
                self.gtm[:, tt, :], self.gtm[:, tt, :], R[:, 0:8], A.add),
              reads=[f"gtm:{tt}", "rsc8b"], writes=[f"gtm:{tt}"], tag=f"rtg3:{tt}")

    def index_build(self):
        o = self.op
        T_ = self.T
        nt = T_ // 128
        NS = self.NS
        o("G", lambda e: e.memset(self.zrow[:, :], 0.0),
          reads=["PHASE2"], writes=["zrow"], tag="pf_z")
        o("G", lambda e, T_=T_: e.memset(self.tpre[:, :, 0], float(T_)),
          reads=["PHASE2"], writes=["tpre"], tag="pf_t0")
        o("G", lambda e: e.memset(self.tpre[:, :, 1], 0.0),
          reads=["PHASE2"], writes=["tpre"], tag="pf_t1")
        igf = self.idxg.rearrange("n two -> (n two)")
        ntot = self.NPAD * 2
        zoff = 0
        zi = 0
        while zoff < ntot:
            step = min(2048, ntot - zoff)
            o("QG", lambda e, zoff=zoff, step=step: e.dma_start(
                out=igf[zoff:zoff + step], in_=self.tpre[:, 0:step // 256, :]),
              reads=["tpre"], writes=["idxg"], tag=f"pf_ig:{zi}")
            zoff += step
            zi += 1
        o("QG", lambda e: e.dma_start(
            out=self.outtm[self.NS:self.NS + 128, :], in_=self.zrow[:, :]),
          reads=["zrow"], writes=["outtm:trash"], tag="pf_ot")
        o("QG", lambda e: e.dma_start(
            out=self.x2tm[T_:T_ + 128, :], in_=self.zrow[:, :]),
          reads=["zrow"], writes=["x2tm:trash"], tag="pf_xt")
        for tt in range(nt):
            o("V", lambda e, tt=tt: e.tensor_scalar(
                out=self.lgt[:, :], in0=self.gtm[:, tt, :], scalar1=0.0,
                scalar2=None, op0=A.is_gt),
              reads=[f"gtm:{tt}"], writes=["lgt"], tag=f"ib1_m:{tt}")
            o("T", lambda e, tt=tt: e.matmul(
                self.psA[0:1, 0:8], self.ones128[:], self.lgt[:, :],
                start=(tt == 0), stop=(tt == nt - 1)),
              reads=["lgt", "ones128"] + (["psA:f"] if tt == 0 else ["psA"]),
              writes=["psA"], tag=f"ib1_c:{tt}")
        AS = self.asr
        o("V", lambda e: e.tensor_copy(AS[0:1, 0:8], self.psA[0:1, 0:8]),
          reads=["psA"], writes=["asr"], tag="ib_cnt")
        o("V", lambda e: e.engine_nop(), reads=["asr", "psA"], writes=["psA:f"], tag="ib_cfree")
        o("V", lambda e: e.memset(AS[0:1, 8:16], 0.0), reads=["asr"],
          writes=["asr"], tag="ib_p0")
        for k in range(8):
            o("V", lambda e, k=k: e.scalar_tensor_tensor(
                out=AS[0:1, 8:16], in0=AS[0:1, 0:8], scalar=float(256 * k),
                in1=AS[0:1, 8:16], op0=A.is_gt, op1=A.add),
              reads=["asr"], writes=["asr"], tag=f"ib_p1:{k}")
        o("V", lambda e: e.tensor_scalar(out=AS[0:1, 8:16], in0=AS[0:1, 8:16],
            scalar1=256.0, scalar2=None, op0=A.mult),
          reads=["asr"], writes=["asr"], tag="ib_p2")
        o("V", lambda e: e.memset(AS[0:1, 16:17], 0.0), reads=["asr"], writes=["asr"], tag="ib_a0")
        for ei in range(1, E):
            o("V", lambda e, ei=ei: e.tensor_tensor(
                AS[0:1, 16 + ei:17 + ei], AS[0:1, 15 + ei:16 + ei],
                AS[0:1, 7 + ei:8 + ei], A.add),
              reads=["asr"], writes=["asr"], tag=f"ib_a:{ei}")
        o("V", lambda e: e.memset(self.ter[0:1, :], -1.0), reads=["PHASE2"],
          writes=["ter"], tag="ib_t0")
        for ei in range(E):
            o("V", lambda e, ei=ei: e.tensor_scalar(
                out=self.bsrow[0:1, 8:9], in0=AS[0:1, 16 + ei:17 + ei],
                scalar1=1.0, scalar2=None, op0=A.mult),
              reads=["asr"], writes=["bsrow"], tag=f"ib_tc:{ei}")
            o("V", lambda e, ei=ei: e.tensor_scalar(
                out=self.gsx[0:1, 0, 0:self.NSLAB], in0=self.thr[0:1, :],
                scalar1=self.bsrow[0:1, 8:9], scalar2=None, op0=A.is_ge),
              reads=["thr", "bsrow"], writes=["gsx"], tag=f"ib_t1:{ei}")
            o("V", lambda e: e.tensor_tensor(
                self.ter[0:1, :], self.ter[0:1, :], self.gsx[0:1, 0, 0:self.NSLAB], A.add),
              reads=["ter", "gsx"], writes=["ter"], tag=f"ib_t2:{ei}")
        o("V", lambda e: e.tensor_copy(self.bsrow[0:1, 0:8], AS[0:1, 16:24]),
          reads=["asr"], writes=["bsrow"], tag="ib_b0")
        R = self.rsc
        for tt in range(nt):
            o("V", lambda e, tt=tt: e.tensor_scalar(
                out=self.lgt[:, :], in0=self.gtm[:, tt, :], scalar1=0.0,
                scalar2=None, op0=A.is_gt),
              reads=[f"gtm:{tt}"], writes=["lgt"], tag=f"ib_m:{tt}")
            o("T", lambda e: e.matmul(self.psB[:, 0:8], self.ltri[:], self.lgt[:, :],
                                      start=True, stop=True),
              reads=["lgt", "ltri", "psB:f"], writes=["psB"], tag=f"ib_cs:{tt}")
            o("T", lambda e: e.matmul(self.psA[0:1, 0:8], self.ones128[:], self.lgt[:, :],
                                      start=True, stop=True),
              reads=["lgt", "ones128", "psA:f"], writes=["psA"], tag=f"ib_cnt:{tt}")
            o("T", lambda e: e.matmul(self.psT0[:, 0:8], self.onesrow[:],
                                      self.bsrow[0:1, 0:8], start=True, stop=True),
              reads=["bsrow", "onesrow", "psT0:f"], writes=["psT0"], tag=f"ib_bb:{tt}")
            o("V", lambda e: e.tensor_copy(self.rnk[:, :], self.psB[:, 0:8]),
              reads=["psB"], writes=["rnk", "psB:f"], tag=f"ib_r0:{tt}")
            o("V", lambda e: e.tensor_tensor(self.rnk[:, :], self.rnk[:, :],
                                             self.psT0[:, 0:8], A.add),
              reads=["rnk", "psT0"], writes=["rnk", "psT0:f"], tag=f"ib_r1:{tt}")
            o("V", lambda e: e.tensor_tensor(self.bsrow[0:1, 0:8], self.bsrow[0:1, 0:8],
                                             self.psA[0:1, 0:8], A.add),
              reads=["bsrow", "psA"], writes=["bsrow", "psA:f"], tag=f"ib_b:{tt}")
            o("V", lambda e: e.tensor_copy(R[:, 0:8], self.lgt[:, :]),
              reads=["lgt"], writes=["rsc"], tag=f"ib_c0:{tt}")
            o("V", lambda e: e.tensor_copy(R[:, 8:9], R[:, 0:1]),
              reads=["rsc"], writes=["rsc"], tag=f"ib_c1a:{tt}")
            o("V", lambda e: e.tensor_tensor(R[:, 9:16], R[:, 1:8], R[:, 0:7], A.add),
              reads=["rsc"], writes=["rsc"], tag=f"ib_c1b:{tt}")
            o("V", lambda e: e.tensor_copy(R[:, 0:2], R[:, 8:10]),
              reads=["rsc"], writes=["rsc"], tag=f"ib_c2a:{tt}")
            o("V", lambda e: e.tensor_tensor(R[:, 2:8], R[:, 10:16], R[:, 8:14], A.add),
              reads=["rsc"], writes=["rsc"], tag=f"ib_c2b:{tt}")
            o("V", lambda e: e.tensor_copy(R[:, 8:12], R[:, 0:4]),
              reads=["rsc"], writes=["rsc"], tag=f"ib_c4a:{tt}")
            o("V", lambda e: e.tensor_tensor(R[:, 12:16], R[:, 4:8], R[:, 0:4], A.add),
              reads=["rsc"], writes=["rsc"], tag=f"ib_c4b:{tt}")
            o("V", lambda e: e.tensor_tensor(R[:, 0:8], R[:, 8:16], self.lgt[:, :],
                                             A.subtract),
              reads=["rsc", "lgt"], writes=["rsc"], tag=f"ib_ex:{tt}")
            o("V", lambda e: e.scalar_tensor_tensor(
                out=self.mAB[:, 0:8], in0=R[:, 0:8], scalar=0.0,
                in1=self.lgt[:, :], op0=A.is_equal, op1=A.mult),
              reads=["rsc", "lgt"], writes=["mAB"], tag=f"ib_mA:{tt}")
            o("V", lambda e: e.tensor_tensor(self.mAB[:, 8:16], self.lgt[:, :],
                                             self.mAB[:, 0:8], A.subtract),
              reads=["lgt", "mAB"], writes=["mAB"], tag=f"ib_mB:{tt}")
            for br in range(2):
                msl = self.mAB[:, br * 8:br * 8 + 8]
                o("V", lambda e, msl=msl, br=br: e.scalar_tensor_tensor(
                    out=self.prd[:, :], in0=self.rnk[:, :], scalar=1.0,
                    in1=msl, op0=A.mult, op1=A.mult,
                    accum_out=self.rnkf[:, br:br + 1]),
                  reads=["rnk", "mAB"], writes=["prd", "rnkf"], tag=f"ib_rs:{tt}:{br}")
                o("V", lambda e, msl=msl, br=br, tt=tt: e.scalar_tensor_tensor(
                    out=self.prd[:, :], in0=self.gtm[:, tt, :], scalar=1.0,
                    in1=msl, op0=A.mult, op1=A.mult,
                    accum_out=self.pr2[:, 2 * br + 1:2 * br + 2]),
                  reads=[f"gtm:{tt}", "mAB"], writes=["prd", "pr2"], tag=f"ib_gs:{tt}:{br}")
                o("V", lambda e, msl=msl: e.scalar_tensor_tensor(
                    out=self.prd[:, :], in0=msl, scalar=1.0,
                    in1=msl, op0=A.mult, op1=A.mult,
                    accum_out=self.rsc[:, 0:1]),
                  reads=["mAB"], writes=["prd", "rsc"], tag=f"ib_cb:{tt}:{br}")
                o("V", lambda e, br=br, tt=tt: e.tensor_tensor(
                    self.pr2[:, 2 * br:2 * br + 1], self.iotf[:, tt:tt + 1],
                    self.rsc[:, 0:1], A.mult),
                  reads=["iotf", "rsc"], writes=["pr2"], tag=f"ib_id:{tt}:{br}")
                if br == 1:
                    o("V", lambda e, NS=NS: e.tensor_scalar(
                        out=self.prd[:, 0:1], in0=self.rsc[:, 0:1],
                        scalar1=-float(NS), scalar2=float(NS),
                        op0=A.mult, op1=A.add),
                      reads=["rsc"], writes=["prd"], tag=f"ib_g1:{tt}")
                    o("V", lambda e: e.tensor_tensor(
                        self.rnkf[:, 1:2], self.rnkf[:, 1:2], self.prd[:, 0:1], A.add),
                      reads=["rnkf", "prd"], writes=["rnkf"], tag=f"ib_g2:{tt}")
                o("V", lambda e, br=br: e.tensor_copy(
                    self.rnki[:, br:br + 1], self.rnkf[:, br:br + 1]),
                  reads=["rnkf"], writes=["rnki"], tag=f"ib_cv:{tt}:{br}")
                o("QG", lambda e, br=br: e.indirect_dma_start(
                    out=self.idxg, out_offset=IOA(self.rnki[:, br:br + 1], 0),
                    in_=self.pr2[:, 2 * br:2 * br + 2], in_offset=None),
                  reads=["rnki", "pr2"], writes=["idxg"],
                  tag=f"ib_sc:{tt}:{br}")
            o("QG", lambda e, tt=tt: e.dma_start(
                out=self.t2s[tt * 128:(tt + 1) * 128, :], in_=self.rnkf[:, 0:2]),
              reads=["rnkf"], writes=["t2s"], tag=f"ib_ts:{tt}")

    def moe(self):
        o = self.op
        SLE = self.SLE
        NSLAB = self.NSLAB
        f1v = self.i_fc1.rearrange("e d i -> (e d) i")
        f2v = self.i_fc2.rearrange("e d h -> (e d) h")
        banks = [self.psA, self.psB, self.psW[0][0], self.psW[0][1],
                 self.psW[1][0], self.psW[1][1], self.psT0, self.psT1]
        bkey = ["psA", "psB", "psW00", "psW01", "psW10", "psW11", "psT0", "psT1"]
        o("V", lambda e: e.tensor_scalar(
            out=self.asr[0:1, 24:24 + NSLAB], in0=self.ter[0:1, :],
            scalar1=1024.0, scalar2=None, op0=A.mult),
          reads=["ter"], writes=["asr"], tag="ie_tx")
        o("T", lambda e, NSLAB=NSLAB: e.matmul(
            self.psB[:, 0:NSLAB], self.onesrow[:], self.asr[0:1, 24:24 + NSLAB],
            start=True, stop=True),
          reads=["asr", "onesrow", "psB:f"], writes=["psB"], tag="ie_bc")
        o("V", lambda e, NSLAB=NSLAB: e.tensor_tensor(
            self.idxeA[:, :, :],
            self.psB.ap()[:, 0:NSLAB].unsqueeze(2).to_broadcast([128, NSLAB, 8]),
            self.kdp.ap()[:, :].unsqueeze(1).to_broadcast([128, NSLAB, 8]),
            A.add),
          reads=["psB", "kdp"], writes=["idxeA", "psB:f"], tag="ie_ix")
        for s in range(NSLAB):
            gsl = s % 2
            o("Q0", lambda e, s=s, gsl=gsl: e.dma_start(
                out=self.gpr[0:1, gsl, :, :], in_=self.idxg[s * SLE:(s + 1) * SLE, :]),
              reads=["idxg"], writes=[f"gpr:{gsl}"], tag=f"gsl:{s}")
            o("V", lambda e, gsl=gsl: e.tensor_copy(
                self.gsx[0:1, gsl, :], self.gpr[0:1, gsl, :, 1]),
              reads=[f"gpr:{gsl}"], writes=[f"gsx:{gsl}"], tag=f"gsx:{s}")
            o("T", lambda e, gsl=gsl: e.matmul(
                self.psT0[:, 0:SLE], self.onesrow[:],
                self.gsx[0:1, gsl, :], start=True, stop=True),
              reads=[f"gsx:{gsl}", "onesrow", "psT0:f"], writes=["psT0"],
              tag=f"gbcm:{s}")
            o("S", lambda e: e.copy(out=self.gbc[:, :], in_=self.psT0[:, 0:SLE]),
              reads=["psT0"], writes=["gbc", "psT0:f"], tag=f"gbce:{s}")
            for t2 in range(2):
                tl = 2 * s + t2
                isl = tl % 4
                o("Q0", lambda e, tl=tl, isl=isl: e.dma_start(
                    out=self.idxcf[:, isl, :],
                    in_=self.idxg[tl * 128:(tl + 1) * 128, :]),
                  reads=["idxg"], writes=[f"idxcf:{isl}"], tag=f"ixl:{s}:{t2}")
                o("V", lambda e, isl=isl: e.tensor_copy(
                    self.idxc[:, isl:isl + 1], self.idxcf[:, isl, 0:1]),
                  reads=[f"idxcf:{isl}"], writes=[f"idxc:{isl}"], tag=f"ixc:{s}:{t2}")
                o("QG", lambda e, isl=isl: e.indirect_dma_start(
                    out=self.xgtm[:, :], out_offset=None,
                    in_=self.x2tm, in_offset=IOA(self.idxc[:, isl:isl + 1], 0)),
                  reads=[f"idxc:{isl}", "x2tm", "x2tm:trash", "xgtm:f"],
                  writes=["xgtm"], tag=f"gth:{s}:{t2}")
                for dt in range(NDT):
                    ps = [self.psT0, self.psT1][dt % 2]
                    pk = ["psT0", "psT1"][dt % 2]
                    o("T", lambda e, dt=dt, ps=ps: e.transpose(
                        ps[:, 0:128], self.xgtm[:, dt * 128:(dt + 1) * 128], self.eye[:]),
                      reads=["xgtm", "eye", pk + ":f"], writes=[pk],
                      tag=f"gtr:{s}:{t2}:{dt}")
                    wrts = [f"xgfm:{t2}:{dt}", pk + ":f"] + (["xgtm:f"] if dt == NDT - 1 else [])
                    o("S", lambda e, dt=dt, t2=t2, ps=ps: e.copy(
                        out=self.xgfm[:, dt, t2 * 128:(t2 + 1) * 128], in_=ps[:, 0:128]),
                      reads=[pk], writes=wrts, tag=f"gtrev:{s}:{t2}:{dt}")
            xk = [f"xgfm:{t2}:{dt}" for t2 in range(2) for dt in range(NDT)]
            for kd in range(NDT):
                wsl = kd % 2
                o("QG", lambda e, s=s, kd=kd, wsl=wsl: e.indirect_dma_start(
                    out=self.wfp[wsl].ap(), out_offset=None, in_=f1v,
                    in_offset=IOA(self.idxeA[:, s, kd:kd + 1], 0)),
                  reads=["idxeA", f"wfp:{wsl}:f"], writes=[f"wfp:{wsl}"],
                  tag=f"wgp:{s}:{kd}")
                for ki in range(NDT):
                    o("T", lambda e, wsl=wsl, ki=ki, kd=kd, ps=banks[ki]: e.matmul(
                        ps[:, 0:SLE], self.wfp[wsl][:, ki * 128:(ki + 1) * 128],
                        self.xgfm[:, kd, 0:SLE], start=(kd == 0), stop=(kd == NDT - 1)),
                      reads=xk + [f"wfp:{wsl}"] +
                            ([bkey[ki] + ":f"] if kd == 0 else [bkey[ki]]),
                      writes=[bkey[ki]] + ([f"wfp:{wsl}:f"] if ki == NDT - 1 else []),
                      tag=f"f1p:{s}:{kd}:{ki}")
            for ki in range(NDT):
                o("S", lambda e, ki=ki, ps=banks[ki]: e.activation(
                    out=self.mgrp[:, ki, 0:SLE], in_=ps[:, 0:SLE], func=ACT.Silu),
                  reads=[bkey[ki]], writes=[f"m:{ki}", bkey[ki] + ":f"],
                  tag=f"pev:{s}:{ki}")
            for kd in range(NDT):
                wsl = kd % 2
                o("QG", lambda e, s=s, kd=kd, wsl=wsl: e.indirect_dma_start(
                    out=self.wfg[wsl].ap(), out_offset=None, in_=f1v,
                    in_offset=IOA(self.idxeA[:, s, kd:kd + 1], 0), element_offset=D),
                  reads=["idxeA", f"wfg:{wsl}:f"], writes=[f"wfg:{wsl}"],
                  tag=f"wgg:{s}:{kd}")
                for ki in range(NDT):
                    o("T", lambda e, wsl=wsl, ki=ki, kd=kd, ps=banks[ki]: e.matmul(
                        ps[:, 0:SLE], self.wfg[wsl][:, ki * 128:(ki + 1) * 128],
                        self.xgfm[:, kd, 0:SLE], start=(kd == 0), stop=(kd == NDT - 1)),
                      reads=xk + [f"wfg:{wsl}"] +
                            ([bkey[ki] + ":f"] if kd == 0 else [bkey[ki]]),
                      writes=[bkey[ki]] + ([f"wfg:{wsl}:f"] if ki == NDT - 1 else []),
                      tag=f"f1g:{s}:{kd}:{ki}")
            for ki in range(NDT):
                o("V", lambda e, ki=ki, ps=banks[ki]: e.tensor_tensor(
                    self.mgrp[:, ki, 0:SLE], self.mgrp[:, ki, 0:SLE],
                    ps[:, 0:SLE], A.mult),
                  reads=[f"m:{ki}", bkey[ki]], writes=[f"m:{ki}", bkey[ki] + ":f"],
                  tag=f"gev:{s}:{ki}")
            for ki in range(NDT):
                wsl = ki % 2
                o("QG", lambda e, s=s, ki=ki, wsl=wsl: e.indirect_dma_start(
                    out=self.wf2s[wsl].ap(), out_offset=None, in_=f2v,
                    in_offset=IOA(self.idxeA[:, s, ki:ki + 1], 0)),
                  reads=["idxeA", f"wf2s:{wsl}:f"], writes=[f"wf2s:{wsl}"],
                  tag=f"wg2:{s}:{ki}")
                for dt in range(NDT):
                    o("T", lambda e, wsl=wsl, dt=dt, ki=ki, ps=banks[dt]: e.matmul(
                        ps[:, 0:SLE], self.wf2s[wsl][:, dt * 128:(dt + 1) * 128],
                        self.mgrp[:, ki, 0:SLE], start=(ki == 0), stop=(ki == NDT - 1)),
                      reads=[f"m:{ki}", f"wf2s:{wsl}"] +
                            ([bkey[dt] + ":f"] if ki == 0 else [bkey[dt]]),
                      writes=[bkey[dt]] + ([f"wf2s:{wsl}:f"] if dt == NDT - 1 else []),
                      tag=f"f2:{s}:{ki}:{dt}")
            for dt in range(NDT):
                o("V", lambda e, dt=dt, ps=banks[dt]: e.tensor_tensor(
                    self.f2st[:, dt, 0:SLE], ps[:, 0:SLE], self.gbc[:, :], A.mult),
                  reads=[bkey[dt], "gbc"], writes=[f"f2st:{dt}", bkey[dt] + ":f"],
                  tag=f"f2ev:{s}:{dt}")
            self._scatter(f"s{s}", s, SLE, True)
        self._shared()

    def _shared(self):
        o = self.op
        T_ = self.T
        SLH = self.SLH
        banks = [self.psA, self.psB, self.psW[0][0], self.psW[0][1],
                 self.psW[1][0], self.psW[1][1], self.psT0, self.psT1]
        bkey = ["psA", "psB", "psW00", "psW01", "psW10", "psW11", "psT0", "psT1"]
        for sl_i in range(T_ // SLH):
            s0 = sl_i * SLH
            cks = [c for c in range(T_ // CT) if c * CT >= s0 and c * CT < s0 + SLH]
            xk = [f"s:{c}:{kd}" for c in cks for kd in range(NDT)]
            for hi in range(2):
                for nm, wt, src_w, eo in (("p", self.wfp, self.i_gate, 0),
                                          ("g", self.wfg, self.i_up, 0)):
                    for kd in range(NDT):
                        wsl = kd % 2
                        o("Q0" if nm == "p" else "Q1",
                          lambda e, wt=wt, wsl=wsl, src_w=src_w, kd=kd, hi=hi: e.dma_start(
                            out=wt[wsl].ap(),
                            in_=src_w[kd * 128:(kd + 1) * 128, hi * D:(hi + 1) * D]),
                          reads=["PHASE2", f"wf{nm}:{wsl}:f"], writes=[f"wf{nm}:{wsl}"],
                          tag=f"shw{nm}:{sl_i}:{hi}:{kd}")
                        for ki in range(NDT):
                            o("T", lambda e, wt=wt, wsl=wsl, ki=ki, kd=kd, s0=s0, ps=banks[ki]: e.matmul(
                                ps[:, 0:SLH], wt[wsl][:, ki * 128:(ki + 1) * 128],
                                self.s_fm[:, kd, s0:s0 + SLH],
                                start=(kd == 0), stop=(kd == NDT - 1)),
                              reads=xk + [f"wf{nm}:{wsl}"] +
                                    ([bkey[ki] + ":f"] if kd == 0 else [bkey[ki]]),
                              writes=[bkey[ki]] + ([f"wf{nm}:{wsl}:f"] if ki == NDT - 1 else []),
                              tag=f"sh{nm}:{sl_i}:{hi}:{kd}:{ki}")
                    if nm == "p":
                        for ki in range(NDT):
                            o("S", lambda e, ki=ki, ps=banks[ki]: e.activation(
                                out=self.mgrp[:, ki, 0:SLH], in_=ps[:, 0:SLH],
                                func=ACT.Silu),
                              reads=[bkey[ki]], writes=[f"m:{ki}", bkey[ki] + ":f"],
                              tag=f"shpe:{sl_i}:{hi}:{ki}")
                    else:
                        for ki in range(NDT):
                            o("V", lambda e, ki=ki, ps=banks[ki]: e.tensor_tensor(
                                self.mgrp[:, ki, 0:SLH], self.mgrp[:, ki, 0:SLH],
                                ps[:, 0:SLH], A.mult),
                              reads=[f"m:{ki}", bkey[ki]],
                              writes=[f"m:{ki}", bkey[ki] + ":f"],
                              tag=f"shge:{sl_i}:{hi}:{ki}")
                for ki in range(NDT):
                    wsl = ki % 2
                    o("Q1", lambda e, wsl=wsl, ki=ki, hi=hi: e.dma_start(
                        out=self.wf2s[wsl].ap(),
                        in_=self.i_down[hi * D + ki * 128:hi * D + (ki + 1) * 128, :]),
                      reads=["PHASE2", f"wf2s:{wsl}:f"], writes=[f"wf2s:{wsl}"],
                      tag=f"shw2:{sl_i}:{hi}:{ki}")
                    for dt in range(NDT):
                        o("T", lambda e, wsl=wsl, dt=dt, ki=ki, ps=banks[dt]: e.matmul(
                            ps[:, 0:SLH], self.wf2s[wsl][:, dt * 128:(dt + 1) * 128],
                            self.mgrp[:, ki, 0:SLH], start=(ki == 0), stop=(ki == NDT - 1)),
                          reads=[f"m:{ki}", f"wf2s:{wsl}"] +
                                ([bkey[dt] + ":f"] if ki == 0 else [bkey[dt]]),
                          writes=[bkey[dt]] + ([f"wf2s:{wsl}:f"] if dt == NDT - 1 else []),
                          tag=f"shf2:{sl_i}:{hi}:{ki}:{dt}")
                for dt in range(NDT):
                    if hi == 0:
                        o("V", lambda e, dt=dt, ps=banks[dt]: e.tensor_copy(
                            self.f2st[:, dt, 0:SLH], ps[:, 0:SLH]),
                          reads=[bkey[dt]], writes=[f"f2st:{dt}", bkey[dt] + ":f"],
                          tag=f"shf2e:{sl_i}:{hi}:{dt}")
                    else:
                        o("V", lambda e, dt=dt, ps=banks[dt]: e.tensor_tensor(
                            self.f2st[:, dt, 0:SLH], self.f2st[:, dt, 0:SLH],
                            ps[:, 0:SLH], A.add),
                          reads=[f"f2st:{dt}", bkey[dt]],
                          writes=[f"f2st:{dt}", bkey[dt] + ":f"],
                          tag=f"shf2e:{sl_i}:{hi}:{dt}")
            self._scatter(f"sh{sl_i}", sl_i, SLH, False)

    def _scatter(self, etag, s, SLE, indirect):
        o = self.op
        for t2 in range(SLE // 128):
            for dt in range(NDT):
                ps = [self.psT0, self.psT1][dt % 2]
                pk = ["psT0", "psT1"][dt % 2]
                o("T", lambda e, dt=dt, t2=t2, ps=ps: e.transpose(
                    ps[:, 0:128], self.f2st[:, dt, t2 * 128:(t2 + 1) * 128], self.eye[:]),
                  reads=[f"f2st:{dt}", "eye", pk + ":f"], writes=[pk],
                  tag=f"str:{etag}:{t2}:{dt}")
                eng = "V" if dt % 2 == 0 else "S"
                fn = (lambda e, dt=dt, ps=ps: e.tensor_copy(
                    self.sc2tm[:, dt * 128:(dt + 1) * 128], ps[:, 0:128]))                     if eng == "V" else                     (lambda e, dt=dt, ps=ps: e.copy(
                        out=self.sc2tm[:, dt * 128:(dt + 1) * 128], in_=ps[:, 0:128]))
                o(eng, fn, reads=[pk], writes=[f"sc2tm:{dt}", pk + ":f"],
                  tag=f"strev:{etag}:{t2}:{dt}")
            if indirect:
                tl = 2 * s + t2
                o("QG", lambda e, tl=tl: e.dma_start(
                    out=self.outtm[tl * 128:(tl + 1) * 128, :], in_=self.sc2tm[:, :]),
                  reads=[f"sc2tm:{dt2}" for dt2 in range(NDT)],
                  writes=["outtm"], tag=f"sca:{etag}:{t2}")
            else:
                t0 = s * SLE + t2 * 128
                o("QG", lambda e, t0=t0: e.dma_start(
                    out=self.shtm[t0:t0 + 128, :], in_=self.sc2tm[:, :]),
                  reads=[f"sc2tm:{dt2}" for dt2 in range(NDT)], writes=["shtm"],
                  tag=f"scd:{etag}:{t2}")

    def ln2_out(self):
        o = self.op
        T_ = self.T
        for c in range(T_ // 128):
            sl = c % 2
            o("Q0", lambda e, c=c, sl=sl: e.dma_start(
                out=self.x2c[:, sl, :], in_=self.x2tm[c * 128:(c + 1) * 128, :]),
              reads=["x2tm", "PHASE3"], writes=[f"x2c:{sl}"], tag=f"o_lx:{c}")
            o("Q1", lambda e, c=c, sl=sl: e.dma_start(
                out=self.accc[:, sl, :], in_=self.shtm[c * 128:(c + 1) * 128, :]),
              reads=["shtm", "PHASE3"], writes=[f"accc:{sl}"], tag=f"o_la:{c}")
            o("Q1", lambda e, c=c, sl=sl: e.dma_start(
                out=self.t2sl[:, sl, :], in_=self.t2s[c * 128:(c + 1) * 128, :]),
              reads=["t2s", "PHASE3"], writes=[f"t2sl:{sl}"], tag=f"o_lt:{c}")
            o("V", lambda e, sl=sl: e.tensor_copy(
                self.idxAB[:, sl, :], self.t2sl[:, sl, :]),
              reads=[f"t2sl:{sl}"], writes=[f"idxAB:{sl}"], tag=f"o_cv:{c}")
            for ab in range(2):
                o("QG", lambda e, sl=sl, ab=ab: e.indirect_dma_start(
                    out=self.gab[:, sl, ab, :], out_offset=None,
                    in_=self.outtm,
                    in_offset=IOA(self.idxAB[:, sl, ab:ab + 1], 0)),
                  reads=[f"idxAB:{sl}", "outtm", "outtm:trash", "PHASE3",
                         f"gab:{sl}:{ab}:f"],
                  writes=[f"gab:{sl}:{ab}"], tag=f"o_g{ab}:{c}")
            o("V", lambda e, sl=sl: e.tensor_tensor(
                self.accc[:, sl, :], self.accc[:, sl, :], self.x2c[:, sl, :], A.add),
              reads=[f"x2c:{sl}", f"accc:{sl}", "PHASE3"], writes=[f"accc:{sl}"],
              tag=f"o_add:{c}")
            for ab in range(2):
                o("V", lambda e, sl=sl, ab=ab: e.tensor_tensor(
                    self.accc[:, sl, :], self.accc[:, sl, :], self.gab[:, sl, ab, :],
                    A.add),
                  reads=[f"gab:{sl}:{ab}", f"accc:{sl}"],
                  writes=[f"accc:{sl}", f"gab:{sl}:{ab}:f"], tag=f"o_ga{ab}:{c}")
            L = self.lnc
            o("S", lambda e, sl=sl: e.activation(
                out=self.scr2[:, :], in_=self.accc[:, sl, :], func=ACT.Copy,
                accum_out=L[:, 0:1]),
              reads=[f"accc:{sl}", "PHASE3"], writes=["scr2", "lc0"], tag=f"o_sum:{c}")
            o("S", lambda e, sl=sl: e.activation(
                out=self.scr2[:, :], in_=self.accc[:, sl, :], func=ACT.Square,
                accum_out=L[:, 1:2]),
              reads=[f"accc:{sl}"], writes=["scr2", "lc1"], tag=f"o_sq:{c}")
            sc = 1.0 / D
            o("V", lambda e: e.tensor_scalar(out=L[:, 2:3], in0=L[:, 0:1], scalar1=sc,
                scalar2=None, op0=A.mult),
              reads=["lc0"], writes=["lc2"], tag=f"o_mu:{c}")
            o("V", lambda e: e.tensor_scalar(out=L[:, 3:4], in0=L[:, 1:2], scalar1=sc,
                scalar2=None, op0=A.mult),
              reads=["lc1"], writes=["lc3"], tag=f"o_ex2:{c}")
            o("V", lambda e: e.tensor_tensor(L[:, 4:5], L[:, 2:3], L[:, 2:3], A.mult),
              reads=["lc2"], writes=["lc4"], tag=f"o_m2:{c}")
            o("V", lambda e: e.tensor_tensor(L[:, 4:5], L[:, 3:4], L[:, 4:5], A.subtract),
              reads=["lc3", "lc4"], writes=["lc4"], tag=f"o_var:{c}")
            self.rstd_cols(L, f"o_rs:{c}", 4, 5, eps=1e-5)
            o("V", lambda e, sl=sl: e.tensor_scalar(
                out=self.otm[:, sl, :], in0=self.accc[:, sl, :],
                scalar1=L[:, 2:3], scalar2=L[:, 5:6], op0=A.subtract, op1=A.mult),
              reads=[f"accc:{sl}", "lc2", "lc5", "PHASE3"], writes=[f"otm:{sl}"],
              tag=f"o_nrm:{c}")
            o("Q0", lambda e, c=c, sl=sl: e.dma_start(
                out=self.o_out[c * 128:(c + 1) * 128, :], in_=self.otm[:, sl, :]),
              reads=[f"otm:{sl}"], writes=[f"outdma:{c}"], tag=f"o_dma:{c}")

    def rstd_cols(self, L, tag, vrow, orow, eps, pre="lc"):
        o = self.op
        o("S", lambda e: e.activation(out=L[:, 6:7], in_=L[:, vrow:vrow + 1],
                                      func=ACT.Sqrt, bias=self.epsc.ap()),
          reads=[f"{pre}{vrow}", "epsc"], writes=[pre + "6"], tag=f"{tag}:sqrt")
        o("V", lambda e: e.tensor_scalar(out=L[:, 7:8], in0=L[:, vrow:vrow + 1],
            scalar1=eps, scalar2=None, op0=A.add),
          reads=[f"{pre}{vrow}"], writes=[pre + "7"], tag=f"{tag}:va")
        o("V", lambda e: e.reciprocal(L[:, 8:9], L[:, 6:7]),
          reads=[pre + "6"], writes=[pre + "8"], tag=f"{tag}:r0")
        o("V", lambda e: e.tensor_tensor(L[:, 7:8], L[:, 7:8], L[:, 8:9], A.mult),
          reads=[pre + "7", pre + "8"], writes=[pre + "7"], tag=f"{tag}:t")
        o("V", lambda e: e.tensor_tensor(L[:, 7:8], L[:, 7:8], L[:, 6:7], A.add),
          reads=[pre + "7", pre + "6"], writes=[pre + "7"], tag=f"{tag}:s1")
        o("V", lambda e: e.tensor_scalar(out=L[:, 7:8], in0=L[:, 7:8], scalar1=0.5,
            scalar2=None, op0=A.mult),
          reads=[pre + "7"], writes=[pre + "7"], tag=f"{tag}:s1b")
        o("V", lambda e: e.reciprocal(L[:, orow:orow + 1], L[:, 7:8]),
          reads=[pre + "7"], writes=[f"{pre}{orow}"], tag=f"{tag}:rstd")

    def build_full(self):
        self.build_attn()
        self.alloc_ffn()
        skeys = [f"s:{c}:{dt}" for c in range(self.NC) for dt in range(NDT)]
        self.barrier("PHASE2", skeys)
        self.ln1()
        self.router()
        self.index_build()
        self.moe()
        self.barrier("PHASE3", ["outtm", "outtm:trash", "shtm", "t2s"])
        self.ln2_out()
        self.terminal()
        self.finish()

def build_full_test(T):
    mk = MK(T)
    mk.build_full()
    return mk


def build_attn_test(T):
    mk = MK(T)
    mk.build_attn()
    mk.dump_sfm()
    mk.terminal()
    mk.finish()
    return mk


def kernel(**inputs):
    B, T = 8, 2048
    hp = host_prep(inputs, T)
    x = np.asarray(inputs["L0_attn_x"], dtype=np.float32)
    mk = MK(T)
    mk.build_full()
    in_maps = []
    for b in range(B):
        m = dict(hp)
        m["x"] = np.ascontiguousarray(x[b])
        in_maps.append(m)
    res = run_bass_kernel_spmd(mk.nc, in_maps, core_ids=list(range(B)))
    out = np.stack([np.asarray(r["out"]) for r in res.results], axis=0)
    return out.astype(np.float32)
